# version 09
# baseline (speedup 1.0000x reference)
import math
import numpy as np
import ml_dtypes
import concourse.bass as bass
import concourse.mybir as mybir
from concourse.bass_utils import run_bass_kernel_spmd

F32 = mybir.dt.float32
F32R = mybir.dt.float32r
BF16 = mybir.dt.bfloat16
AF = mybir.ActivationFunctionType
ALU = mybir.AluOpType
AX = mybir.AxisListType

B, T, C, H, DH = 16, 2048, 2048, 16, 128
INTER = 8192
NCORES = 8
SEQN = 2
NCH = C // 128
NACH = INTER // 128
EPS_RMS = float(np.finfo(np.float32).eps)
EPS_LN = 1e-5
FW1_SCALE = 1.0 / (64.0 * math.sqrt(2048.0))
FW2_SCALE = 1.0 / (64.0 * math.sqrt(8192.0))
INVSQ_DH = 1.0 / math.sqrt(DH)
TTOK = 512

ENGINES = ("sync", "vector", "scalar", "tensor", "gpsimd")
COMPUTE = ("vector", "scalar", "tensor", "gpsimd")
ALL_STAGES = ("startup", "seq0", "seq1", "ffn")


def _hadamard(n):
    h = np.array([[1.0]], dtype=np.float32)
    while h.shape[0] < n:
        h = np.block([[h, h], [h, -h]])
    return h


class KB:
    def __init__(self):
        self.nc = bass.Bass("TRN2", target_bir_lowering=False)
        self.ops = {e: [] for e in ENGINES}
        self.sems = {}
        self.cnt = {}
        self.ecnt = {e: 0 for e in COMPUTE}
        self.esem = {e: self.nc.alloc_semaphore(f"e_{e}") for e in COMPUTE}
        self.lastwait = {e: {} for e in ENGINES}

    def sem(self, name):
        if name not in self.sems:
            self.sems[name] = self.nc.alloc_semaphore(name)
            self.cnt[name] = 0
        return self.sems[name]

    def dval(self, name):
        self.sem(name)
        return self.cnt[name]

    def dwait(self, engine, name, value=None):
        s = self.sem(name)
        v = self.cnt[name] if value is None else value
        if v <= 0:
            return
        key = ("d", name)
        if self.lastwait[engine].get(key, -1) >= v:
            return
        self.lastwait[engine][key] = v
        self.ops[engine].append(lambda eng, s=s, v=v: eng.wait_ge(s, v))

    def dma(self, engine, out, in_, sem_name):
        s = self.sem(sem_name)
        self.cnt[sem_name] += 16
        self.ops[engine].append(
            lambda eng, o=out, i=in_, s=s: eng.dma_start(out=o, in_=i).then_inc(s, 16))
        return self.cnt[sem_name]

    def ins(self, engine, build):
        s = self.esem[engine]
        if engine in ("vector", "scalar", "gpsimd") and self.ecnt[engine] > 0:
            v = self.ecnt[engine]
            key = ("e", engine)
            if self.lastwait[engine].get(key, -1) < v:
                self.lastwait[engine][key] = v
                self.ops[engine].append(lambda eng, s=s, v=v: eng.wait_ge(s, v))
        self.ecnt[engine] += 1
        self.ops[engine].append(lambda eng, s=s, b=build: b(eng).then_inc(s, 1))
        return (engine, self.ecnt[engine])

    def mark(self, engine):
        return (engine, self.ecnt[engine])

    def phase_mark(self):
        return {e: self.ecnt[e] for e in COMPUTE}

    def phase_wait(self, engine, pm):
        if pm is None:
            return
        for pe in COMPUTE:
            self.after(engine, (pe, pm[pe]))

    def after(self, engine, *marks):
        for m in marks:
            if m is None:
                continue
            pe, v = m
            if v <= 0:
                continue
            if pe == engine:
                continue
            key = ("e", pe)
            if self.lastwait[engine].get(key, -1) >= v:
                continue
            self.lastwait[engine][key] = v
            self.ops[engine].append(
                lambda eng, s=self.esem[pe], v=v: eng.wait_ge(s, v))

    def fence(self, engine, *marks):
        for m in marks:
            if m is None:
                continue
            pe, v = m
            if v <= 0:
                continue
            key = ("e", pe)
            if self.lastwait[engine].get(key, -1) >= v:
                continue
            self.lastwait[engine][key] = v
            self.ops[engine].append(
                lambda eng, s=self.esem[pe], v=v: eng.wait_ge(s, v))


def build_program(stages=ALL_STAGES, taps=()):
    kb = KB()
    nc = kb.nc
    tapset = set(taps)

    x_in = nc.dram_tensor("x", [SEQN, C, T], F32, kind="ExternalInput").ap()
    wt_in = nc.dram_tensor("Wt", [C, C], F32, kind="ExternalInput").ap()
    w_in = nc.dram_tensor("W", [C, C], F32, kind="ExternalInput").ap()
    wupt_in = nc.dram_tensor("WupT", [C, INTER], F32, kind="ExternalInput").ap()
    wgt_in = nc.dram_tensor("WgT", [C, INTER], F32, kind="ExternalInput").ap()
    wdt_in = nc.dram_tensor("WdT", [16, 4, 128, 2048], F32, kind="ExternalInput").ap()
    g1_in = nc.dram_tensor("g1", [1, 128], F32, kind="ExternalInput").ap()
    g2_in = nc.dram_tensor("g2", [1, 128], F32, kind="ExternalInput").ap()
    g3_in = nc.dram_tensor("g3", [1, 128], F32, kind="ExternalInput").ap()
    suug_in = nc.dram_tensor("SUug", [NCH, 128], F32, kind="ExternalInput").ap()
    sudp_in = nc.dram_tensor("SUdp", [NACH, 128], F32, kind="ExternalInput").ap()
    wsu_in = nc.dram_tensor("WSu", [NACH, 128], F32, kind="ExternalInput").ap()
    wsg_in = nc.dram_tensor("WSg", [NACH, 128], F32, kind="ExternalInput").ap()
    wsd_in = nc.dram_tensor("WSd", [NCH, 128], F32, kind="ExternalInput").ap()
    h128_in = nc.dram_tensor("H128", [128, 128], BF16, kind="ExternalInput").ap()
    idb_in = nc.dram_tensor("IDB", [128, 128], BF16, kind="ExternalInput").ap()
    idf_in = nc.dram_tensor("IDF", [128, 128], F32, kind="ExternalInput").ap()
    onecol_in = nc.dram_tensor("ONECOL", [128, 1], BF16, kind="ExternalInput").ap()
    onerow_in = nc.dram_tensor("ONEROW", [1, 128], F32, kind="ExternalInput").ap()
    onerowb_in = nc.dram_tensor("ONEROWB", [1, 128], BF16, kind="ExternalInput").ap()
    cst_in = nc.dram_tensor("CST", [128, 4], F32, kind="ExternalInput").ap()

    out_t = nc.dram_tensor("out", [SEQN, NCH, 128, T], F32, kind="ExternalOutput").ap()

    wt_s = nc.dram_tensor("wt_s", [4, NCH, 128, 512], BF16, kind="Internal").ap()
    w_s = nc.dram_tensor("w_s", [4, NCH, 128, 512], BF16, kind="Internal").ap()
    up_s = nc.dram_tensor("up_s", [NACH, 128, NCH * 128], BF16, kind="Internal").ap()
    gt_s = nc.dram_tensor("gt_s", [NACH, 128, NCH * 128], BF16, kind="Internal").ap()
    dn_s = nc.dram_tensor("dn_s", [4, 128, NACH * 512], BF16, kind="Internal").ap()
    y_s = nc.dram_tensor("y_s", [SEQN, H, 128, T], BF16, kind="Internal").ap()
    x1_s = nc.dram_tensor("x1_s", [SEQN, NCH, 128, T], F32, kind="Internal").ap()
    bnc_s = nc.dram_tensor("bnc_s", [T], F32, kind="Internal").ap()

    A = nc.alloc_sbuf_tensor
    AR_A = A("AR_A", [128, 16384], F32)[:]
    AR_B = A("AR_B", [128, 16384], F32)[:]
    AR_W = A("AR_W", [128, 6144], F32)[:]

    xbf_sb = [A(f"xbf{i}", [128, T], BF16)[:] for i in range(2)]
    sq_sb = [A(f"sq{i}", [128, T], BF16)[:] for i in range(2)]
    tmp32_sb = A("tmp32", [128, T], F32)[:]
    abc_sb = A("abc", [128, T], F32)[:]
    bbc_sb = A("bbc", [128, T], F32)[:]
    xch_sb = [AR_W[:, 0:2048], AR_W[:, 2048:4096]]
    rbc_sb = bbc_sb
    sgbuf_sb = [sq_sb[0].bitcast(F32)[:, 0:512], sq_sb[1].bitcast(F32)[:, 0:512]]

    mh_sb = A("mh", [128, 16], F32)[:]
    nmh_sb = A("nmh", [128, 16], F32)[:]
    z_sb = A("z", [128, 16], F32)[:]
    lnz_sb = A("lnz", [128, 16], F32)[:]
    cc_sb = A("cc", [128, 16], F32)[:]
    u_sb = A("u", [128, 16], F32)[:]
    iz_sb = A("iz", [128, 16], F32)[:]
    tq_sb = A("tq", [128, 16], F32)[:]
    kc1_sb = A("kc1", [128, 1], F32)[:]
    kb16_sb = A("kb16", [128, 1], BF16)[:]
    kcol_sb = A("kcol", [128, 1], F32)[:]
    nkcol_sb = A("nkcol", [128, 1], F32)[:]
    ktp_sb = A("ktp", [1, 128], F32)[:]
    kscal_sb = A("kscal", [1, 1], F32)[:]
    crow_sb = A("crow", [16, 128], F32)[:]
    cst_sb = A("cst", [128, 4], F32)[:]
    idb_sb = A("idb", [128, 128], BF16)[:]
    idf_sb = A("idf", [128, 128], F32)[:]
    h128_sb = A("h128", [128, 128], BF16)[:]
    onecol_sb = A("onecol", [128, 1], BF16)[:]
    onerow_sb = A("onerow", [1, 128], F32)[:]
    onerowb_sb = A("onerowb", [1, 128], BF16)[:]
    g1_sb = A("g1s", [1, 128], F32)[:]
    g2_sb = A("g2s", [1, 128], F32)[:]
    g3_sb = A("g3s", [1, 128], F32)[:]
    ggcol_sb = A("ggcol", [128, 1], F32)[:]
    g3col_sb = A("g3col", [128, 1], F32)[:]
    suug_sb = A("suug", [128, NCH], F32)[:]
    sudp_sb = A("sudp", [128, NACH], F32)[:]
    wsu_sb = A("wsu", [128, NACH], F32)[:]
    wsg_sb = A("wsg", [128, NACH], F32)[:]
    wsd_sb = A("wsd", [128, NCH], F32)[:]
    sm16_sb = A("sm16", [NCH, 128], F32)[:]
    sm64_sb = A("sm64", [NACH, 128], F32)[:]

    rowm_sb = A("rowm", [1, T], F32)[:]
    trow_sb = abc_sb[0:1, :]
    t2row_sb = bbc_sb[0:1, :]
    m2row_sb = rowm_sb
    arow_sb = tmp32_sb[0:1, :]
    brow_sb = AR_W[0:1, 4096:6144]
    crow1_sb = abc_sb[0:1, :]
    vrow_sb = tmp32_sb[0:1, :]

    ebf = AR_A.bitcast(BF16)
    E_strip = [ebf[:, j * T:(j + 1) * T] for j in range(NCH)]
    xn_ch = E_strip
    g_store = [ebf[:, a * 512:(a + 1) * 512] for a in range(NACH)]
    cvt32 = AR_A[:, 0:8192]
    cvt16 = ebf[:, 16384:16384 + 8192]

    nbf = AR_B.bitcast(BF16)
    n_ch = [nbf[:, h * T:(h + 1) * T] for h in range(H)]
    dnslotB = [nbf[:, i * 512:(i + 1) * 512] for i in range(NACH)]
    x1t = [AR_B[:, ch * 512:(ch + 1) * 512] for ch in range(NCH)]
    xf_ch = [nbf[:, 16384 + ch * 512:16384 + (ch + 1) * 512] for ch in range(NCH)]
    fwpp = [nbf[:, 24576 + ch * 512:24576 + (ch + 1) * 512] for ch in range(NCH)]

    wbf = AR_W.bitcast(BF16)
    wblk = [wbf[:, c * 512:(c + 1) * 512] for c in range(16)]
    ybuf_sb = [wbf[:, 8192:10240], wbf[:, 10240:12288]]
    vtok_sb = wbf[:, 0:2048]
    vtokA_sb = wbf[:, 2048:4096]
    eb_sb = [wbf[:, 4096 + i * 2048:4096 + (i + 1) * 2048] for i in range(2)]
    ng_sb = wbf[:, 8192:10240]
    ng3_sb = wbf[:, 10240:12288]
    vbc_sb = ng3_sb
    ysb_sb = ng_sb
    dncv = [wbf[:, 4096 + k * 512:4096 + (k + 1) * 512] for k in range(4)]
    dncv32 = [AR_W[:, k * 512:(k + 1) * 512] for k in range(4)]
    ugw = [[wbf[:, i * 4096 + c * 128:i * 4096 + (c + 1) * 128] for c in range(32)]
           for i in range(2)]
    dnw = [[tmp32_sb.bitcast(BF16)[:, i * 2048 + a * 512:i * 2048 + (a + 1) * 512]
            for a in range(4)] for i in range(2)]

    psATT = nc.alloc_psum_tensor("psATT", [128, 2048], F32)
    psY = nc.alloc_psum_tensor("psY", [128, 2048], F32)
    pA = psATT[:]
    pY = psY[:]

    assert nc.sbuf_bytes_remaining >= 0

    tap_t = {}
    for name in tapset:
        tap_t[name] = nc.dram_tensor(f"tap_{name}", [128, T], F32, kind="ExternalOutput").ap()

    def tapbf(name, src_bf, width=T, prod=None):
        if name in tapset:
            kb.dwait("vector", "d_tap")
            kb.after("vector", prod)
            m = kb.ins("vector", lambda e, s=src_bf, w=width: e.tensor_copy(bbc_sb[:, 0:w], s))
            kb.after("gpsimd", m)
            kb.after("scalar", m)
            kb.after("tensor", m)
            kb.dma("gpsimd", tap_t[name], bbc_sb, "d_tap")
            for e in ("vector", "scalar", "sync"):
                kb.dwait(e, "d_tap")

    def startup_smalls():
        for ap_in, ap_sb in ((g1_in, g1_sb), (g2_in, g2_sb), (g3_in, g3_sb),
                             (idb_in, idb_sb), (idf_in, idf_sb), (h128_in, h128_sb),
                             (onecol_in, onecol_sb), (onerow_in, onerow_sb),
                             (cst_in, cst_sb), (onerowb_in, onerowb_sb),
                             (suug_in, sm16_sb)):
            kb.dma("sync", ap_sb, ap_in, "d_ld")
        kb.dwait("tensor", "d_ld")
        kb.ins("tensor", lambda e: e.transpose(pA[:, 0:1], g1_sb, idf_sb[0:1, 0:1]))
        kb.ins("tensor", lambda e: e.transpose(pA[:, 1:2], g2_sb, idf_sb[0:1, 0:1]))
        kb.ins("tensor", lambda e: e.transpose(pA[:, 2:3], g3_sb, idf_sb[0:1, 0:1]))
        mt = kb.ins("tensor", lambda e: e.transpose(pA[:, 512:528], sm16_sb,
                                                    idf_sb[0:16, 0:16]))
        kb.after("vector", mt)
        kb.ins("vector", lambda e: e.tensor_copy(suug_sb, pA[:, 512:528]))
        kb.ins("vector", lambda e: e.tensor_copy(ggcol_sb, pA[:, 0:1]))
        kb.ins("vector", lambda e: e.scalar_tensor_tensor(
            ggcol_sb, ggcol_sb, INVSQ_DH, pA[:, 1:2], op0=ALU.mult, op1=ALU.mult))
        mv = kb.ins("vector", lambda e: e.tensor_copy(g3col_sb, pA[:, 2:3]))

        state = {"mv": mv}

        def tp_then(dst_builder, src_dram, width):
            sm = sm64_sb if width == 64 else sm16_sb
            psv = pA[:, 1024:1024 + width]
            kb.after("sync", state["mv"])
            dm = kb.dma("sync", sm, src_dram, "d_ld")
            kb.dwait("tensor", "d_ld", dm)
            kb.after("tensor", state["mv"])
            mt = kb.ins("tensor", lambda e, sm=sm, p=psv: e.transpose(
                p, sm, idf_sb[0:sm.shape[0], 0:sm.shape[0]]))
            kb.after("vector", mt)
            state["mv"] = kb.ins("vector", dst_builder)

        tp_then(lambda e: e.tensor_copy(sudp_sb, pA[:, 1024:1088]), sudp_in, 64)
        tp_then(lambda e: e.scalar_tensor_tensor(
            wsu_sb, pA[:, 1024:1088], 64.0, sudp_sb, op0=ALU.mult, op1=ALU.mult), wsu_in, 64)
        tp_then(lambda e: e.tensor_scalar_mul(wsg_sb, pA[:, 1024:1088], 64.0), wsg_in, 64)
        tp_then(lambda e: e.tensor_scalar_mul(wsd_sb, pA[:, 1024:1040], 64.0 * FW2_SCALE), wsd_in, 16)
        kb.after("tensor", state["mv"])
        kb.after("scalar", state["mv"])

    cvt32p = [AR_A[:, 0:4096], AR_A[:, 4096:8192]]
    cvt16p = [ebf[:, 16384:20480], ebf[:, 20480:24576]]
    cv_state = {"mv": [None, None], "st": [0, 0]}

    def _cv_piece(par, load_src, width, store_fn):
        b32 = cvt32p[par][:, 0:width]
        b16 = cvt16p[par][:, 0:width]
        kb.dwait("sync", f"d_cs{par}", cv_state["st"][par])
        dm = kb.dma("sync", b32, load_src, f"d_cv{par}")
        kb.dwait("vector", f"d_cv{par}", dm)
        mv = kb.ins("vector", lambda e, o=b16, i=b32: e.tensor_copy(o, i))
        kb.after("scalar", mv)
        store_fn(b16, par)
        cv_state["st"][par] = kb.dval(f"d_cs{par}")

    def convert_square(src, dst):
        for ch in range(NCH):
            def store(b16, par, ch=ch, dst=dst):
                for g in range(4):
                    kb.dma("scalar", dst[g, ch], b16[:, g * 512:(g + 1) * 512], f"d_cs{par}")
            _cv_piece(ch % 2, src[ch * 128:(ch + 1) * 128, :], 2048, store)

    def convert_upgt(src, dst):
        for ch in range(NCH):
            for h in range(2):
                def store(b16, par, ch=ch, h=h, dst=dst):
                    for aa in range(32):
                        a = h * 32 + aa
                        kb.dma("scalar", dst[a, :, ch * 128:(ch + 1) * 128],
                               b16[:, aa * 128:(aa + 1) * 128], f"d_cs{par}")
                _cv_piece((ch * 2 + h) % 2,
                          src[ch * 128:(ch + 1) * 128, h * 4096:(h + 1) * 4096], 4096, store)

    def transform_dn():
        slotA = [ebf[:, i * 512:(i + 1) * 512] for i in range(NACH)]
        slotB = dnslotB
        for e in ("scalar", "vector", "tensor", "sync"):
            kb.dwait(e, "cv_st")
            kb.dwait(e, "d_cs0")
            kb.dwait(e, "d_cs1")
        b32g = [AR_W[:, 0:2048], AR_W[:, 2048:4096]]
        b16g = [wbf[:, 8192:10240], wbf[:, 10240:12288]]
        mm_marks = [None, None]
        cp_marks = [None, None]
        v_marks = [None, None]
        for cq in range(4):
            for g in range(NACH // 4):
                par = g % 2
                b32, b16 = b32g[par], b16g[par]
                ps = pA if par == 0 else pY
                kb.after("sync", v_marks[par])
                dm = kb.dma("sync", b32, wdt_in[g, cq], f"d_cv{par}")
                kb.dwait("vector", f"d_cv{par}", dm)
                kb.after("vector", mm_marks[par])
                v_marks[par] = kb.ins("vector", lambda e, o=b16, i=b32: e.tensor_copy(o, i))
                kb.after("tensor", v_marks[par], cp_marks[par])
                mm = None
                for k in range(4):
                    mm = kb.ins("tensor", lambda e, k=k, ps=ps, b16=b16: e.matmul(
                        ps[:, k * 512:(k + 1) * 512], h128_sb,
                        b16[:, k * 512:(k + 1) * 512], start=True, stop=True))
                mm_marks[par] = mm
                kb.after("scalar", mm)
                cp = None
                for k in range(4):
                    cp = kb.ins("scalar", lambda e, k=k, g=g, ps=ps: e.copy(
                        slotA[g * 4 + k], ps[:, k * 512:(k + 1) * 512]))
                cp_marks[par] = cp
            kb.after("vector", *cp_marks)
            src, dst = slotA, slotB
            stage_mark = None
            for bit in (1, 2, 4, 8, 16, 32):
                kb.fence("vector", stage_mark)
                for a in range(NACH):
                    p = a ^ bit
                    if a & bit == 0:
                        stage_mark = kb.ins("vector", lambda e, o=dst[a], u=src[a], v=src[p]:
                                            e.tensor_tensor(o, u, v, op=ALU.add))
                    else:
                        stage_mark = kb.ins("vector", lambda e, o=dst[a], u=src[p], v=src[a]:
                                            e.tensor_tensor(o, u, v, op=ALU.subtract))
                src, dst = dst, src
            kb.after("scalar", stage_mark)
            kb.dma("scalar", dn_s[cq], ebf[:, 0:NACH * 512] if src is slotA
                   else nbf[:, 0:NACH * 512], "cv_st")
            kb.dwait("sync", "cv_st")
            kb.dwait("vector", "cv_st")
            v_marks = [None] * 4

    def ln_pass(x_src, W, resident=None):
        nsl = W // 512
        if resident is not None:
            for ch in range(NCH):
                kb.dma("sync", resident[ch], x_src(ch), "d_xr")
            kb.dwait("vector", "d_xr")
        pe_marks = [None, None]
        v_marks = [None, None]
        mm = None
        for ch in range(NCH):
            if resident is None:
                xb = xch_sb[ch % 2][:, 0:W]
                kb.after("sync", v_marks[ch % 2])
                dm = kb.dma("sync", xb, x_src(ch), f"d_x{ch % 2}")
                kb.dwait("vector", f"d_x{ch % 2}", dm)
            else:
                xb = resident[ch]
            kb.after("vector", pe_marks[ch % 2])
            v_marks[ch % 2] = kb.ins("vector", lambda e, x=xb, ch=ch, W=W: e.tensor_copy(
                xbf_sb[ch % 2][:, 0:W], x))
            kb.fence("vector", v_marks[ch % 2])
            mv2 = kb.ins("vector", lambda e, ch=ch, W=W: e.tensor_tensor(
                sq_sb[ch % 2][:, 0:W], xbf_sb[ch % 2][:, 0:W], xbf_sb[ch % 2][:, 0:W],
                op=ALU.mult))
            kb.after("tensor", mv2)
            st, sp = (ch == 0), (ch == NCH - 1)
            for sl in range(nsl):
                kb.ins("tensor", lambda e, sl=sl, st=st, sp=sp, ch=ch: e.matmul(
                    pY[0:1, sl * 512:(sl + 1) * 512], onecol_sb,
                    xbf_sb[ch % 2][:, sl * 512:(sl + 1) * 512], start=st, stop=sp))
            for sl in range(nsl):
                mm = kb.ins("tensor", lambda e, sl=sl, st=st, sp=sp, ch=ch: e.matmul(
                    pY[32:33, sl * 512:(sl + 1) * 512], onecol_sb,
                    sq_sb[ch % 2][:, sl * 512:(sl + 1) * 512], start=st, stop=sp))
            pe_marks[ch % 2] = mm
        kb.after("vector", mm)
        kb.ins("vector", lambda e, W=W: e.tensor_scalar_mul(trow_sb[:, 0:W], pY[0:1, 0:W],
                                                            1.0 / 2048.0))
        kb.ins("vector", lambda e, W=W: e.tensor_scalar_mul(t2row_sb[:, 0:W],
                                                            pY[32:33, 0:W], 1.0 / 2048.0))
        mva = kb.ins("vector", lambda e, W=W: e.tensor_tensor(
            m2row_sb[:, 0:W], trow_sb[:, 0:W], trow_sb[:, 0:W], op=ALU.mult))
        kb.fence("vector", mva)
        mv = kb.ins("vector", lambda e, W=W: e.tensor_tensor(
            t2row_sb[:, 0:W], t2row_sb[:, 0:W], m2row_sb[:, 0:W], op=ALU.subtract))
        kb.after("scalar", mv)
        ms = kb.ins("scalar", lambda e, W=W: e.activation(
            m2row_sb[:, 0:W], t2row_sb[:, 0:W], AF.Ln, bias=cst_sb[0:1, 0:1], scale=1.0))
        kb.fence("scalar", ms)
        ms = kb.ins("scalar", lambda e, W=W: e.activation(
            arow_sb[:, 0:W], m2row_sb[:, 0:W], AF.Exp, bias=0.0, scale=-0.5))
        kb.after("vector", ms)
        mv = kb.ins("vector", lambda e, W=W: e.tensor_tensor(
            brow_sb[:, 0:W], trow_sb[:, 0:W], arow_sb[:, 0:W],
            op=ALU.mult))
        kb.after("tensor", mv)
        for sl in range(nsl):
            kb.ins("tensor", lambda e, sl=sl: e.matmul(
                pY[:, sl * 512:(sl + 1) * 512], onerow_sb, arow_sb[:, sl * 512:(sl + 1) * 512],
                start=True, stop=True))
        for sl in range(nsl):
            mm = kb.ins("tensor", lambda e, sl=sl: e.matmul(
                pA[:, sl * 512:(sl + 1) * 512], onerow_sb, brow_sb[:, sl * 512:(sl + 1) * 512],
                start=True, stop=True))
        kb.after("scalar", mm)
        kb.ins("scalar", lambda e, W=W: e.copy(abc_sb[:, 0:W], pY[:, 0:W]))
        ms = kb.ins("scalar", lambda e, W=W: e.copy(bbc_sb[:, 0:W], pA[:, 0:W]))
        return ms, v_marks

    def ln_wproj_n(s, pm):
        kb.phase_wait("sync", pm)
        kb.phase_wait("vector", pm)
        kb.phase_wait("scalar", pm)
        kb.phase_wait("tensor", pm)
        mabc, v_marks = ln_pass(lambda ch: x_in[s, ch * 128:(ch + 1) * 128, :], T)
        kb.after("vector", mabc)
        mlast = None
        for ch in range(NCH):
            xb = xch_sb[ch % 2]
            kb.after("sync", v_marks[ch % 2])
            dm = kb.dma("sync", xb, x_in[s, ch * 128:(ch + 1) * 128, :], f"d_x{ch % 2}")
            kb.dwait("vector", f"d_x{ch % 2}", dm)
            v_marks[ch % 2] = kb.ins("vector", lambda e, x=xb: e.tensor_tensor(
                tmp32_sb, x, abc_sb, op=ALU.mult))
            kb.fence("vector", v_marks[ch % 2])
            mlast = kb.ins("vector", lambda e, ch=ch: e.tensor_tensor(
                xn_ch[ch], tmp32_sb, bbc_sb, op=ALU.subtract))
        if s == 0:
            tapbf("xn0", xn_ch[0], prod=mlast)

        kb.after("tensor", mlast)
        m_nmul = None
        m_ncopy = None
        m_rbc_prev = None
        for hch in range(H):
            g = hch // 4
            if hch % 4 == 0:
                kb.after("sync", kb.mark("tensor"), mlast)
                dwb = 0
                for ch in range(NCH):
                    dwb = kb.dma("sync", wblk[ch], wt_s[g, ch], "d_wb")
                kb.dwait("tensor", "d_wb", dwb)
            kb.after("tensor", m_ncopy)
            mm = None
            for sl in range(4):
                for ch in range(NCH):
                    mm = kb.ins("tensor", lambda e, sl=sl, ch=ch, hch=hch: e.matmul(
                        pA[:, sl * 512:(sl + 1) * 512],
                        wblk[ch][:, (hch % 4) * 128:(hch % 4 + 1) * 128],
                        xn_ch[ch][:, sl * 512:(sl + 1) * 512],
                        start=(ch == 0), stop=(ch == NCH - 1)))
            kb.after("scalar", mm)
            m_ncopy = kb.ins("scalar", lambda e, hch=hch: e.copy(n_ch[hch], pA))
            kb.after("vector", m_ncopy)
            mv = kb.ins("vector", lambda e, hch=hch: e.tensor_tensor(
                sq_sb[hch % 2], n_ch[hch], n_ch[hch], op=ALU.mult))
            kb.after("tensor", mv, m_rbc_prev)
            qm = None
            for sl in range(4):
                qm = kb.ins("tensor", lambda e, sl=sl, hch=hch: e.matmul(
                    pY[0:1, sl * 512:(sl + 1) * 512], onecol_sb,
                    sq_sb[hch % 2][:, sl * 512:(sl + 1) * 512], start=True, stop=True))
            kb.after("scalar", qm)
            ms = kb.ins("scalar", lambda e: e.activation(
                m2row_sb, pY[0:1, :], AF.Ln, bias=cst_sb[0:1, 1:2], scale=1.0 / 128.0))
            kb.fence("scalar", ms)
            ms = kb.ins("scalar", lambda e: e.activation(
                arow_sb, m2row_sb, AF.Exp, bias=0.0, scale=-0.5))
            kb.after("tensor", ms)
            rm = None
            for sl in range(4):
                rm = kb.ins("tensor", lambda e, sl=sl: e.matmul(
                    pY[:, sl * 512:(sl + 1) * 512], onerow_sb,
                    arow_sb[:, sl * 512:(sl + 1) * 512], start=True, stop=True))
            kb.after("scalar", rm)
            m_rbc = kb.ins("scalar", lambda e: e.copy(rbc_sb, pY))
            m_rbc_prev = m_rbc
            kb.after("vector", m_rbc)
            m_nmul = kb.ins("vector", lambda e, hch=hch: e.tensor_tensor(
                n_ch[hch], n_ch[hch], rbc_sb, op=ALU.mult))
        if s == 0:
            tapbf("n0", n_ch[0], prod=m_nmul)
        return m_nmul

    def attention_head(s, h, m_prev):
        nh = n_ch[h]
        kb.dwait("scalar", "d_y")
        kb.after("scalar", m_prev)
        ms = kb.ins("scalar", lambda e: e.activation(ng_sb, nh, AF.Copy, bias=0.0,
                                                     scale=ggcol_sb))
        ms3 = kb.ins("scalar", lambda e: e.activation(ng3_sb, nh, AF.Copy, bias=0.0,
                                                      scale=g3col_sb))
        kb.after("vector", ms)
        msqg = kb.ins("vector", lambda e: e.tensor_tensor(sq_sb[0], ng_sb, ng_sb,
                                                          op=ALU.mult))
        kb.after("tensor", ms3, m_prev)
        cp_m = [None, None]
        for j in range(NCH):
            pslot = pA[:, 1024 + (j % 2) * 512:1024 + (j % 2) * 512 + 64].bitcast(BF16)
            kb.after("tensor", cp_m[j % 2])
            tp = kb.ins("tensor", lambda e, j=j, p=pslot: e.transpose(
                p, ng3_sb[:, j * 128:(j + 1) * 128], idb_sb))
            kb.after("scalar", tp)
            cp_m[j % 2] = kb.ins("scalar", lambda e, j=j, p=pslot: e.copy(
                vtok_sb[:, j * 128:(j + 1) * 128], p))
        kb.after("tensor", msqg, *cp_m)
        qm = None
        for j in range(NCH):
            qm = kb.ins("tensor", lambda e, j=j: e.matmul(
                pY[:, j:j + 1], sq_sb[0][:, j * 128:(j + 1) * 128], onecol_sb,
                start=True, stop=True))
        kb.after("scalar", qm)
        ms = kb.ins("scalar", lambda e: e.activation(tq_sb, pY[:, 0:16], AF.Ln,
                                                     bias=cst_sb[:, 2:3], scale=128.0))
        kb.fence("scalar", ms)
        ms = kb.ins("scalar", lambda e: e.activation(mh_sb, tq_sb, AF.Exp, bias=0.0,
                                                     scale=0.5))
        kb.after("vector", ms)
        kb.ins("vector", lambda e: e.tensor_scalar_mul(nmh_sb, mh_sb, -1.0))
        mred = kb.ins("vector", lambda e: e.tensor_reduce(kc1_sb, mh_sb, axis=AX.X,
                                                          op=ALU.max))
        kb.fence("vector", mred)
        mv = kb.ins("vector", lambda e: e.tensor_copy(kb16_sb, kc1_sb))
        kb.after("tensor", mv)
        mt = kb.ins("tensor", lambda e: e.transpose(pY[0:1, 16:80].bitcast(BF16), kb16_sb,
                                                    idb_sb))
        kb.after("scalar", mt)
        ms = kb.ins("scalar", lambda e: e.copy(ktp_sb, pY[0:1, 16:80].bitcast(BF16)))
        kb.after("vector", ms)
        mv = kb.ins("vector", lambda e: e.tensor_reduce(kscal_sb, ktp_sb,
                                                        axis=AX.X, op=ALU.max))
        kb.after("tensor", mv)
        mt = kb.ins("tensor", lambda e: e.matmul(pY[:, 96:97], onerow_sb, kscal_sb,
                                                 start=True, stop=True))
        kb.after("scalar", mt)
        ms = kb.ins("scalar", lambda e: e.copy(kcol_sb, pY[:, 96:97]))
        kb.after("vector", ms)
        mv = kb.ins("vector", lambda e: e.tensor_scalar_mul(nkcol_sb, kcol_sb, -1.0))
        kb.after("scalar", mv)
        mu = kb.ins("scalar", lambda e: e.activation(u_sb, mh_sb, AF.Exp, bias=nkcol_sb,
                                                     scale=1.0))

        kb.after("tensor", mu)
        m_eact = None
        m_eact2 = [None, None]
        for j in range(NCH):
            ps = pA if j % 2 == 0 else pY
            kb.after("tensor", m_eact2[j % 2])
            mm = None
            for sl in range(4):
                mm = kb.ins("tensor", lambda e, j=j, sl=sl, ps=ps: e.matmul(
                    ps[:, sl * 512:(sl + 1) * 512], ng_sb[:, j * 128:(j + 1) * 128],
                    nh[:, sl * 512:(sl + 1) * 512], start=True, stop=True))
            kb.after("scalar", mm)
            m_eact = kb.ins("scalar", lambda e, j=j, ps=ps: e.activation(
                E_strip[j], ps, AF.Exp, bias=nmh_sb[:, j:j + 1], scale=1.0,
                accum_out=z_sb[:, j:j + 1]))
            m_eact2[j % 2] = m_eact

        kb.fence("scalar", m_eact)
        mlnz = kb.ins("scalar", lambda e: e.activation(lnz_sb, z_sb, AF.Ln, bias=0.0,
                                                       scale=1.0))
        kb.after("vector", mlnz)
        mcc = kb.ins("vector", lambda e: e.tensor_tensor(cc_sb, mh_sb, lnz_sb, op=ALU.add))
        miz = kb.ins("vector", lambda e: e.reciprocal(iz_sb, z_sb))
        kb.after("tensor", mcc)
        mt = kb.ins("tensor", lambda e: e.transpose(pA[0:16, 0:128], cc_sb, idf_sb))
        kb.after("scalar", mt)
        ms = kb.ins("scalar", lambda e: e.copy(crow_sb, pA[0:16, 0:128]))
        kb.after("gpsimd", ms)
        db = kb.dma("gpsimd", bnc_s.rearrange("(a b) -> a b", a=16), crow_sb, "d_bn")
        kb.dwait("gpsimd", "d_bn", db)
        db = kb.dma("gpsimd", crow1_sb, bnc_s.rearrange("(a b) -> a b", a=1), "d_bn")
        kb.dwait("scalar", "d_bn", db)
        msv = kb.ins("scalar", lambda e: e.activation(xbf_sb[1][0:1, :], crow1_sb,
                                                      AF.Exp, bias=kcol_sb[0:1, 0:1],
                                                      scale=-1.0))
        kb.after("tensor", msv)
        vm = None
        for sl in range(4):
            vm = kb.ins("tensor", lambda e, sl=sl: e.matmul(
                pA[:, sl * 512:(sl + 1) * 512], onerowb_sb,
                xbf_sb[1][0:1, sl * 512:(sl + 1) * 512], start=True, stop=True))
        kb.after("scalar", vm)
        msvbc = kb.ins("scalar", lambda e: e.copy(vbc_sb, pA))

        kb.after("vector", msvbc, miz, mu)
        m_pv = [None, None]
        pvlast = None
        for j in range(NCH):
            kb.after("vector", m_pv[j % 2])
            m_eb = kb.ins("vector", lambda e, j=j: e.scalar_tensor_tensor(
                eb_sb[j % 2], E_strip[j], u_sb[:, j:j + 1], vbc_sb,
                op0=ALU.mult, op1=ALU.mult))
            m_va = kb.ins("vector", lambda e, j=j: e.tensor_scalar_mul(
                vtokA_sb[:, j * 128:(j + 1) * 128], vtok_sb[:, j * 128:(j + 1) * 128],
                iz_sb[:, j:j + 1]))
            kb.after("tensor", m_eb, m_va)
            for sl in range(4):
                kb.ins("tensor", lambda e, j=j, sl=sl: e.matmul(
                    pY[:, sl * 512:(sl + 1) * 512], vtokA_sb[:, j * 128:(j + 1) * 128],
                    E_strip[j][:, sl * 512:(sl + 1) * 512], start=(j == 0), stop=False))
            for sl in range(4):
                pvlast = kb.ins("tensor", lambda e, j=j, sl=sl: e.matmul(
                    pY[:, sl * 512:(sl + 1) * 512], vtok_sb[:, j * 128:(j + 1) * 128],
                    eb_sb[j % 2][:, sl * 512:(sl + 1) * 512], start=False,
                    stop=(j == NCH - 1)))
            m_pv[j % 2] = pvlast
        kb.after("scalar", pvlast)
        my = kb.ins("scalar", lambda e: e.copy(ysb_sb, pY))
        kb.after("gpsimd", my)
        kb.dma("gpsimd", y_s[s, h], ysb_sb, "d_y")
        if f"y{s}_{h}" in tapset:
            tapbf(f"y{s}_{h}", ysb_sb, prod=my)
        return my

    def proj_residual(s, m_attn):
        kb.dwait("sync", "d_y")
        kb.after("sync", m_attn)
        m_ev = [None, None]
        m_mm = None
        for cp in range(8):
            g = cp // 2
            if cp % 2 == 0:
                if g >= 1:
                    kb.after("sync", m_mm)
                dwb = 0
                for m in range(H):
                    dwb = kb.dma("sync", wblk[m], w_s[g, m], "d_wb")
                kb.dwait("tensor", "d_wb", dwb)
            kb.after("tensor", m_ev[0], m_ev[1])
            pe_y = [None, None]
            for m in range(H):
                yb = ybuf_sb[m % 2]
                q = "sync" if m % 2 == 0 else "scalar"
                if pe_y[m % 2] is not None:
                    kb.after(q, pe_y[m % 2])
                elif cp > 0:
                    kb.after(q, m_mm)
                dyb = kb.dma(q, yb, y_s[s, m], f"d_yb{m % 2}")
                kb.dwait("tensor", f"d_yb{m % 2}", dyb)
                for cc in range(2):
                    for sl in range(4):
                        m_mm = kb.ins("tensor", lambda e, m=m, cc=cc, sl=sl, cp=cp, yb=yb:
                                      e.matmul(
                                          (pA if cc == 0 else pY)[:, sl * 512:(sl + 1) * 512],
                                          wblk[m][:, ((cp % 2) * 2 + cc) * 128:
                                                  ((cp % 2) * 2 + cc + 1) * 128],
                                          yb[:, sl * 512:(sl + 1) * 512],
                                          start=(m == 0), stop=(m == H - 1)))
                pe_y[m % 2] = m_mm
            for cc in range(2):
                chg = cp * 2 + cc
                xdst = tmp32_sb if cc == 0 else abc_sb
                odst = bbc_sb if cc == 0 else tmp32_sb
                kb.dwait("sync", "d_x1s")
                kb.after("sync", m_ev[cc])
                dm = kb.dma("sync", xdst, x_in[s, chg * 128:(chg + 1) * 128, :], f"d_x{cc}")
                kb.dwait("vector", f"d_x{cc}", dm)
                kb.after("vector", m_mm)
                m_ev[cc] = kb.ins("vector", lambda e, cc=cc, xdst=xdst, odst=odst:
                                  e.tensor_tensor(odst, (pA if cc == 0 else pY), xdst,
                                                  op=ALU.add))
                kb.after("gpsimd", m_ev[cc])
                kb.dma("gpsimd", x1_s[s, chg], odst, "d_x1s")
        kb.dwait("sync", "d_x1s")
        kb.dwait("vector", "d_x1s")
        return m_ev[1]

    def ffn_tile(s, q, pm):
        t0 = q * TTOK
        kb.phase_wait("sync", pm)
        kb.phase_wait("vector", pm)
        kb.phase_wait("scalar", pm)
        kb.phase_wait("tensor", pm)
        mabc, _vm = ln_pass(lambda ch: x1_s[s, ch][:, t0:t0 + TTOK], TTOK, resident=x1t)
        kb.after("vector", mabc)
        mfw = None
        for ch in range(NCH):
            m1 = kb.ins("vector", lambda e, ch=ch: e.tensor_tensor(
                tmp32_sb[:, 0:TTOK], x1t[ch], abc_sb[:, 0:TTOK], op=ALU.mult))
            kb.fence("vector", m1)
            m2 = kb.ins("vector", lambda e, ch=ch: e.tensor_tensor(
                tmp32_sb[:, 512:512 + TTOK], tmp32_sb[:, 0:TTOK], bbc_sb[:, 0:TTOK],
                op=ALU.subtract))
            kb.fence("vector", m2)
            mfw = kb.ins("vector", lambda e, ch=ch: e.tensor_scalar_mul(
                fwpp[ch], tmp32_sb[:, 512:512 + TTOK], suug_sb[:, ch:ch + 1]))
        src, dst = fwpp, xf_ch
        stage_mark = mfw
        for bit in (1, 2, 4, 8):
            kb.fence("vector", stage_mark)
            for a in range(NCH):
                p = a ^ bit
                if a & bit == 0:
                    stage_mark = kb.ins("vector", lambda e, o=dst[a], u=src[a], v=src[p]:
                                        e.tensor_tensor(o, u, v, op=ALU.add))
                else:
                    stage_mark = kb.ins("vector", lambda e, o=dst[a], u=src[p], v=src[a]:
                                        e.tensor_tensor(o, u, v, op=ALU.subtract))
            src, dst = dst, src
        kb.after("tensor", stage_mark)
        hm = [None] * 4
        smk = [None] * 4
        for ch in range(NCH):
            k = ch % 4
            kb.after("tensor", smk[k])
            hm[k] = kb.ins("tensor", lambda e, ch=ch, src=src: e.matmul(
                pA[:, (ch % 4) * 512:(ch % 4 + 1) * 512], h128_sb, src[ch],
                start=True, stop=True))
            kb.after("scalar", hm[k])
            smk[k] = kb.ins("scalar", lambda e, ch=ch: e.activation(
                xf_ch[ch], pA[:, (ch % 4) * 512:(ch % 4 + 1) * 512], AF.Copy,
                bias=0.0, scale=FW1_SCALE))
        if q == 0:
            tapbf(f"xf{s}", xf_ch[0], TTOK, prod=smk[0])
        kb.after("tensor", *smk)
        m_gate = [None, None]
        m_sg = [None, None]
        m_ug = [None, None]
        for a in range(NACH):
            par = a % 2
            kb.after("sync", m_ug[par])
            kb.dma("sync", wbf[:, par * 4096:par * 4096 + 2048], up_s[a], f"d_ug{par}")
            dug = kb.dma("sync", wbf[:, par * 4096 + 2048:par * 4096 + 4096], gt_s[a],
                         f"d_ug{par}")
            kb.dwait("tensor", f"d_ug{par}", dug)
            kb.after("tensor", m_gate[par])
            pu = pA[:, par * 1024:par * 1024 + 512]
            pg = pA[:, par * 1024 + 512:par * 1024 + 1024]
            for ch in range(NCH):
                kb.ins("tensor", lambda e, a=a, ch=ch, par=par, pu=pu: e.matmul(
                    pu, ugw[par][ch], xf_ch[ch], start=(ch == 0), stop=(ch == NCH - 1)))
            for ch in range(NCH):
                m_ug[par] = kb.ins("tensor", lambda e, a=a, ch=ch, par=par, pg=pg: e.matmul(
                    pg, ugw[par][16 + ch], xf_ch[ch], start=(ch == 0), stop=(ch == NCH - 1)))
            kb.after("scalar", m_ug[par], m_gate[par])
            m_sg[par] = kb.ins("scalar", lambda e, a=a, pg=pg, par=par: e.activation(
                sgbuf_sb[par], pg, AF.Silu, bias=0.0, scale=wsg_sb[:, a:a + 1]))
            kb.after("vector", m_sg[par])
            m_gate[par] = kb.ins("vector", lambda e, a=a, pu=pu, par=par:
                                 e.scalar_tensor_tensor(
                                     g_store[a], pu, wsu_sb[:, a:a + 1], sgbuf_sb[par],
                                     op0=ALU.mult, op1=ALU.mult))
        kb.after("tensor", m_gate[0], m_gate[1])
        m_add = None
        dn_mm = None
        for cq in range(4):
            dn_prev = [None, None] if dn_mm is None else [dn_mm, dn_mm]
            for grp in range(16):
                par = grp % 2
                kb.after("sync", dn_prev[par])
                ddn = kb.dma("sync", tmp32_sb.bitcast(BF16)[:, par * 2048:(par + 1) * 2048],
                             dn_s[cq][:, grp * 2048:(grp + 1) * 2048], f"d_dn{par}")
                kb.dwait("tensor", f"d_dn{par}", ddn)
                if cq > 0 and grp == 0:
                    kb.after("tensor", m_add)
                for aa in range(4):
                    a = grp * 4 + aa
                    for cc in range(4):
                        dn_mm = kb.ins("tensor", lambda e, a=a, aa=aa, cc=cc, par=par:
                                       e.matmul(
                                           pY[:, cc * 512:(cc + 1) * 512],
                                           dnw[par][aa][:, cc * 128:(cc + 1) * 128],
                                           g_store[a], start=(a == 0), stop=(a == NACH - 1)))
                dn_prev[par] = dn_mm
            kb.after("scalar", dn_mm)
            m_add = None
            for cc in range(4):
                chg = cq * 4 + cc
                kb.dwait("scalar", "d_out")
                kb.after("scalar", m_add)
                msc = kb.ins("scalar", lambda e, cc=cc, chg=chg: e.activation(
                    abc_sb[:, 0:TTOK], pY[:, cc * 512:(cc + 1) * 512], AF.Copy,
                    bias=0.0, scale=wsd_sb[:, chg:chg + 1]))
                kb.after("vector", msc)
                m_add = kb.ins("vector", lambda e, cc=cc, chg=chg: e.tensor_tensor(
                    bbc_sb[:, 0:TTOK], abc_sb[:, 0:TTOK], x1t[chg], op=ALU.add))
                kb.after("gpsimd", m_add)
                kb.dma("gpsimd", out_t[s, chg][:, t0:t0 + TTOK], bbc_sb[:, 0:TTOK], "d_out")
                kb.dwait("vector", "d_out")
        kb.dwait("sync", "d_out")
        return kb.phase_mark()

    if "startup" in stages:
        startup_smalls()
        convert_square(wt_in, wt_s)
        convert_square(w_in, w_s)
        convert_upgt(wupt_in, up_s)
        convert_upgt(wgt_in, gt_s)
        transform_dn()
        for e in ("sync", "gpsimd", "tensor", "scalar", "vector"):
            kb.dwait(e, "cv_st")
            kb.dwait(e, "d_cs0")
            kb.dwait(e, "d_cs1")
    pm = kb.phase_mark()
    for s in range(SEQN):
        if f"seq{s}" in stages:
            m = ln_wproj_n(s, pm)
            for h in range(H):
                m = attention_head(s, h, m)
            proj_residual(s, m)
            pm = kb.phase_mark()
    if "ffn" in stages:
        for s in range(SEQN):
            for q in range(T // TTOK):
                pm = ffn_tile(s, q, pm)
    kb.dwait("gpsimd", "d_out")
    kb.dwait("gpsimd", "d_tap")
    kb.dwait("sync", "d_out")

    with nc.Block() as block:
        @block.sync
        def _(eng):
            for fn in kb.ops["sync"]:
                fn(eng)

        @block.vector
        def _(eng):
            for fn in kb.ops["vector"]:
                fn(eng)

        @block.scalar
        def _(eng):
            for fn in kb.ops["scalar"]:
                fn(eng)

        @block.tensor
        def _(eng):
            for fn in kb.ops["tensor"]:
                fn(eng)

        @block.gpsimd
        def _(eng):
            for fn in kb.ops["gpsimd"]:
                fn(eng)

    return kb


def host_inputs(inputs):
    x = np.asarray(inputs["L0_attn_x"], dtype=np.float32)
    W = np.asarray(inputs["L0_attn_W"], dtype=np.float32)
    Wt = np.ascontiguousarray(W.T)
    WupT = np.ascontiguousarray(np.asarray(inputs["L0_ffn_W_up"], dtype=np.float32).T)
    WgT = np.ascontiguousarray(np.asarray(inputs["L0_ffn_W_gate"], dtype=np.float32).T)
    WdT = np.asarray(inputs["L0_ffn_W_down"], dtype=np.float32).T
    WdTb = np.ascontiguousarray(
        WdT.reshape(16, 4, 128, 4, 512).transpose(0, 3, 2, 1, 4)
    ).reshape(16, 4, 128, 2048)
    g1 = np.asarray(inputs["L0_attn_g1"], dtype=np.float32).reshape(1, 128)
    g2 = np.asarray(inputs["L0_attn_g2"], dtype=np.float32).reshape(1, 128)
    g3 = np.asarray(inputs["L0_attn_g3"], dtype=np.float32).reshape(1, 128)
    suug = np.asarray(inputs["L0_ffn_SU_ug"], dtype=np.float32).reshape(NCH, 128)
    sudp = np.asarray(inputs["L0_ffn_SU_dp"], dtype=np.float32).reshape(NACH, 128)
    wsug = np.asarray(inputs["L0_ffn_Wscale_ug"], dtype=np.float32)
    wsu = np.ascontiguousarray(wsug[:INTER].reshape(NACH, 128))
    wsg = np.ascontiguousarray(wsug[INTER:].reshape(NACH, 128))
    wsd = np.asarray(inputs["L0_ffn_Wscale_dp"], dtype=np.float32).reshape(NCH, 128)

    h128 = _hadamard(128).astype(ml_dtypes.bfloat16)
    idb = np.eye(128, dtype=ml_dtypes.bfloat16)
    idf = np.eye(128, dtype=np.float32)
    onecol = np.ones((128, 1), dtype=ml_dtypes.bfloat16)
    onerow = np.ones((1, 128), dtype=np.float32)
    onerowb = np.ones((1, 128), dtype=ml_dtypes.bfloat16)
    cst = np.zeros((128, 4), dtype=np.float32)
    cst[:, 0] = EPS_LN
    cst[:, 1] = EPS_RMS
    cst[:, 2] = 1e-37

    x_fm = np.ascontiguousarray(x.transpose(0, 2, 1))

    shared = dict(Wt=Wt, W=W, WupT=WupT, WgT=WgT, WdT=WdTb, g1=g1, g2=g2, g3=g3,
                  SUug=suug, SUdp=sudp, WSu=wsu, WSg=wsg, WSd=wsd,
                  H128=h128, IDB=idb, IDF=idf, ONECOL=onecol, ONEROW=onerow,
                  ONEROWB=onerowb, CST=cst)
    in_maps = []
    for k in range(NCORES):
        m = dict(shared)
        m["x"] = x_fm[k * SEQN:(k + 1) * SEQN]
        in_maps.append(m)
    return in_maps


def kernel(**inputs):
    in_maps = host_inputs(inputs)
    kbp = build_program()
    res = run_bass_kernel_spmd(kbp.nc, in_maps, core_ids=list(range(NCORES)))
    parts = [r["out"] for r in res.results]
    out_fm = np.concatenate(parts, axis=0).reshape(B, C, T)
    return np.ascontiguousarray(out_fm.transpose(0, 2, 1))


# version 10
# speedup vs baseline: 1.0109x; 1.0109x over previous
import math
import numpy as np
import ml_dtypes
import concourse.bass as bass
import concourse.mybir as mybir
from concourse.bass_utils import run_bass_kernel_spmd

F32 = mybir.dt.float32
F32R = mybir.dt.float32r
BF16 = mybir.dt.bfloat16
AF = mybir.ActivationFunctionType
ALU = mybir.AluOpType
AX = mybir.AxisListType

B, T, C, H, DH = 16, 2048, 2048, 16, 128
INTER = 8192
NCORES = 8
SEQN = 2
NCH = C // 128
NACH = INTER // 128
EPS_RMS = float(np.finfo(np.float32).eps)
EPS_LN = 1e-5
FW1_SCALE = 1.0 / (64.0 * math.sqrt(2048.0))
FW2_SCALE = 1.0 / (64.0 * math.sqrt(8192.0))
INVSQ_DH = 1.0 / math.sqrt(DH)
TTOK = 512

ENGINES = ("sync", "vector", "scalar", "tensor", "gpsimd")
COMPUTE = ("vector", "scalar", "tensor", "gpsimd")
ALL_STAGES = ("startup", "seq0", "seq1", "ffn")


def _hadamard(n):
    h = np.array([[1.0]], dtype=np.float32)
    while h.shape[0] < n:
        h = np.block([[h, h], [h, -h]])
    return h


class KB:
    def __init__(self):
        self.nc = bass.Bass("TRN2", target_bir_lowering=False)
        self.ops = {e: [] for e in ENGINES}
        self.sems = {}
        self.cnt = {}
        self.ecnt = {e: 0 for e in COMPUTE}
        self.esem = {e: self.nc.alloc_semaphore(f"e_{e}") for e in COMPUTE}
        self.lastwait = {e: {} for e in ENGINES}

    def sem(self, name):
        if name not in self.sems:
            self.sems[name] = self.nc.alloc_semaphore(name)
            self.cnt[name] = 0
        return self.sems[name]

    def dval(self, name):
        self.sem(name)
        return self.cnt[name]

    def dwait(self, engine, name, value=None):
        s = self.sem(name)
        v = self.cnt[name] if value is None else value
        if v <= 0:
            return
        key = ("d", name)
        if self.lastwait[engine].get(key, -1) >= v:
            return
        self.lastwait[engine][key] = v
        self.ops[engine].append(lambda eng, s=s, v=v: eng.wait_ge(s, v))

    def dma(self, engine, out, in_, sem_name):
        s = self.sem(sem_name)
        self.cnt[sem_name] += 16
        self.ops[engine].append(
            lambda eng, o=out, i=in_, s=s: eng.dma_start(out=o, in_=i).then_inc(s, 16))
        return self.cnt[sem_name]

    def ins(self, engine, build):
        s = self.esem[engine]
        if engine in ("vector", "scalar", "gpsimd") and self.ecnt[engine] > 0:
            v = self.ecnt[engine]
            key = ("e", engine)
            if self.lastwait[engine].get(key, -1) < v:
                self.lastwait[engine][key] = v
                self.ops[engine].append(lambda eng, s=s, v=v: eng.wait_ge(s, v))
        self.ecnt[engine] += 1
        self.ops[engine].append(lambda eng, s=s, b=build: b(eng).then_inc(s, 1))
        return (engine, self.ecnt[engine])

    def mark(self, engine):
        return (engine, self.ecnt[engine])

    def phase_mark(self):
        return {e: self.ecnt[e] for e in COMPUTE}

    def phase_wait(self, engine, pm):
        if pm is None:
            return
        for pe in COMPUTE:
            self.after(engine, (pe, pm[pe]))

    def after(self, engine, *marks):
        for m in marks:
            if m is None:
                continue
            pe, v = m
            if v <= 0:
                continue
            if pe == engine:
                continue
            key = ("e", pe)
            if self.lastwait[engine].get(key, -1) >= v:
                continue
            self.lastwait[engine][key] = v
            self.ops[engine].append(
                lambda eng, s=self.esem[pe], v=v: eng.wait_ge(s, v))

    def fence(self, engine, *marks):
        for m in marks:
            if m is None:
                continue
            pe, v = m
            if v <= 0:
                continue
            key = ("e", pe)
            if self.lastwait[engine].get(key, -1) >= v:
                continue
            self.lastwait[engine][key] = v
            self.ops[engine].append(
                lambda eng, s=self.esem[pe], v=v: eng.wait_ge(s, v))


def build_program(stages=ALL_STAGES, taps=()):
    kb = KB()
    nc = kb.nc
    tapset = set(taps)

    x_in = nc.dram_tensor("x", [SEQN, C, T], F32, kind="ExternalInput").ap()
    wt_in = nc.dram_tensor("Wt", [C, C], F32, kind="ExternalInput").ap()
    w_in = nc.dram_tensor("W", [C, C], F32, kind="ExternalInput").ap()
    wupt_in = nc.dram_tensor("WupT", [C, INTER], F32, kind="ExternalInput").ap()
    wgt_in = nc.dram_tensor("WgT", [C, INTER], F32, kind="ExternalInput").ap()
    wdt_in = nc.dram_tensor("WdT", [16, 4, 128, 2048], F32, kind="ExternalInput").ap()
    g1_in = nc.dram_tensor("g1", [1, 128], F32, kind="ExternalInput").ap()
    g2_in = nc.dram_tensor("g2", [1, 128], F32, kind="ExternalInput").ap()
    g3_in = nc.dram_tensor("g3", [1, 128], F32, kind="ExternalInput").ap()
    suug_in = nc.dram_tensor("SUug", [NCH, 128], F32, kind="ExternalInput").ap()
    sudp_in = nc.dram_tensor("SUdp", [NACH, 128], F32, kind="ExternalInput").ap()
    wsu_in = nc.dram_tensor("WSu", [NACH, 128], F32, kind="ExternalInput").ap()
    wsg_in = nc.dram_tensor("WSg", [NACH, 128], F32, kind="ExternalInput").ap()
    wsd_in = nc.dram_tensor("WSd", [NCH, 128], F32, kind="ExternalInput").ap()
    h128_in = nc.dram_tensor("H128", [128, 128], BF16, kind="ExternalInput").ap()
    idb_in = nc.dram_tensor("IDB", [128, 128], BF16, kind="ExternalInput").ap()
    idf_in = nc.dram_tensor("IDF", [128, 128], F32, kind="ExternalInput").ap()
    onecol_in = nc.dram_tensor("ONECOL", [128, 1], BF16, kind="ExternalInput").ap()
    onerow_in = nc.dram_tensor("ONEROW", [1, 128], F32, kind="ExternalInput").ap()
    onerowb_in = nc.dram_tensor("ONEROWB", [1, 128], BF16, kind="ExternalInput").ap()
    cst_in = nc.dram_tensor("CST", [128, 4], F32, kind="ExternalInput").ap()

    out_t = nc.dram_tensor("out", [SEQN, NCH, 128, T], F32, kind="ExternalOutput").ap()

    wt_s = nc.dram_tensor("wt_s", [4, NCH, 128, 512], BF16, kind="Internal").ap()
    w_s = nc.dram_tensor("w_s", [4, NCH, 128, 512], BF16, kind="Internal").ap()
    up_s = nc.dram_tensor("up_s", [NACH, 128, NCH * 128], BF16, kind="Internal").ap()
    gt_s = nc.dram_tensor("gt_s", [NACH, 128, NCH * 128], BF16, kind="Internal").ap()
    dn_s = nc.dram_tensor("dn_s", [4, 128, NACH * 512], BF16, kind="Internal").ap()
    y_s = nc.dram_tensor("y_s", [SEQN, H, 128, T], BF16, kind="Internal").ap()
    x1_s = nc.dram_tensor("x1_s", [SEQN, NCH, 128, T], F32, kind="Internal").ap()
    bnc_s = nc.dram_tensor("bnc_s", [T], F32, kind="Internal").ap()

    A = nc.alloc_sbuf_tensor
    AR_A = A("AR_A", [128, 16384], F32)[:]
    AR_B = A("AR_B", [128, 16384], F32)[:]
    AR_W = A("AR_W", [128, 6144], F32)[:]

    xbf_sb = [A(f"xbf{i}", [128, T], BF16)[:] for i in range(2)]
    sq_sb = [A(f"sq{i}", [128, T], BF16)[:] for i in range(2)]
    tmp32_sb = A("tmp32", [128, T], F32)[:]
    abc_sb = A("abc", [128, T], F32)[:]
    bbc_sb = A("bbc", [128, T], F32)[:]
    xch_sb = [AR_W[:, 0:2048], AR_W[:, 2048:4096]]
    rbc_sb = bbc_sb
    sgbuf_sb = [sq_sb[0].bitcast(F32)[:, 0:512], sq_sb[1].bitcast(F32)[:, 0:512]]

    mh_sb = A("mh", [128, 16], F32)[:]
    nmh_sb = A("nmh", [128, 16], F32)[:]
    z_sb = A("z", [128, 16], F32)[:]
    lnz_sb = A("lnz", [128, 16], F32)[:]
    cc_sb = A("cc", [128, 16], F32)[:]
    u_sb = A("u", [128, 16], F32)[:]
    iz_sb = A("iz", [128, 16], F32)[:]
    tq_sb = A("tq", [128, 16], F32)[:]
    kc1_sb = A("kc1", [128, 1], F32)[:]
    kb16_sb = A("kb16", [128, 1], BF16)[:]
    kcol_sb = A("kcol", [128, 1], F32)[:]
    nkcol_sb = A("nkcol", [128, 1], F32)[:]
    ktp_sb = A("ktp", [1, 128], F32)[:]
    kscal_sb = A("kscal", [1, 1], F32)[:]
    crow_sb = A("crow", [16, 128], F32)[:]
    cst_sb = A("cst", [128, 4], F32)[:]
    idb_sb = A("idb", [128, 128], BF16)[:]
    idf_sb = A("idf", [128, 128], F32)[:]
    h128_sb = A("h128", [128, 128], BF16)[:]
    onecol_sb = A("onecol", [128, 1], BF16)[:]
    onerow_sb = A("onerow", [1, 128], F32)[:]
    onerowb_sb = A("onerowb", [1, 128], BF16)[:]
    g1_sb = A("g1s", [1, 128], F32)[:]
    g2_sb = A("g2s", [1, 128], F32)[:]
    g3_sb = A("g3s", [1, 128], F32)[:]
    ggcol_sb = A("ggcol", [128, 1], F32)[:]
    g3col_sb = A("g3col", [128, 1], F32)[:]
    suug_sb = A("suug", [128, NCH], F32)[:]
    sudp_sb = A("sudp", [128, NACH], F32)[:]
    wsu_sb = A("wsu", [128, NACH], F32)[:]
    wsg_sb = A("wsg", [128, NACH], F32)[:]
    wsd_sb = A("wsd", [128, NCH], F32)[:]
    sm16_sb = A("sm16", [NCH, 128], F32)[:]
    sm64_sb = A("sm64", [NACH, 128], F32)[:]

    rowm_sb = A("rowm", [1, T], F32)[:]
    trow_sb = abc_sb[0:1, :]
    t2row_sb = bbc_sb[0:1, :]
    m2row_sb = rowm_sb
    arow_sb = tmp32_sb[0:1, :]
    brow_sb = AR_W[0:1, 4096:6144]
    crow1_sb = abc_sb[0:1, :]
    vrow_sb = tmp32_sb[0:1, :]

    ebf = AR_A.bitcast(BF16)
    E_strip = [ebf[:, j * T:(j + 1) * T] for j in range(NCH)]
    xn_ch = E_strip
    g_store = [ebf[:, a * 512:(a + 1) * 512] for a in range(NACH)]
    cvt32 = AR_A[:, 0:8192]
    cvt16 = ebf[:, 16384:16384 + 8192]

    nbf = AR_B.bitcast(BF16)
    n_ch = [nbf[:, h * T:(h + 1) * T] for h in range(H)]
    dnslotB = [nbf[:, i * 512:(i + 1) * 512] for i in range(NACH)]
    x1t = [AR_B[:, ch * 512:(ch + 1) * 512] for ch in range(NCH)]
    xf_ch = [nbf[:, 16384 + ch * 512:16384 + (ch + 1) * 512] for ch in range(NCH)]
    fwpp = [nbf[:, 24576 + ch * 512:24576 + (ch + 1) * 512] for ch in range(NCH)]

    wbf = AR_W.bitcast(BF16)
    wblk = [wbf[:, c * 512:(c + 1) * 512] for c in range(16)]
    ybuf_sb = [wbf[:, 8192:10240], wbf[:, 10240:12288]]
    vtok_sb = wbf[:, 0:2048]
    vtokA_sb = wbf[:, 2048:4096]
    eb_sb = [wbf[:, 4096 + i * 2048:4096 + (i + 1) * 2048] for i in range(2)]
    ng_sb = wbf[:, 8192:10240]
    ng3_sb = wbf[:, 10240:12288]
    vbc_sb = ng3_sb
    ysb_sb = ng_sb
    dncv = [wbf[:, 4096 + k * 512:4096 + (k + 1) * 512] for k in range(4)]
    dncv32 = [AR_W[:, k * 512:(k + 1) * 512] for k in range(4)]
    ugw = [[wbf[:, i * 4096 + c * 128:i * 4096 + (c + 1) * 128] for c in range(32)]
           for i in range(2)]
    dnw = [[tmp32_sb.bitcast(BF16)[:, i * 2048 + a * 512:i * 2048 + (a + 1) * 512]
            for a in range(4)] for i in range(2)]

    psATT = nc.alloc_psum_tensor("psATT", [128, 2048], F32)
    psY = nc.alloc_psum_tensor("psY", [128, 2048], F32)
    pA = psATT[:]
    pY = psY[:]

    assert nc.sbuf_bytes_remaining >= 0

    tap_t = {}
    for name in tapset:
        tap_t[name] = nc.dram_tensor(f"tap_{name}", [128, T], F32, kind="ExternalOutput").ap()

    def tapbf(name, src_bf, width=T, prod=None):
        if name in tapset:
            kb.dwait("vector", "d_tap")
            kb.after("vector", prod)
            m = kb.ins("vector", lambda e, s=src_bf, w=width: e.tensor_copy(bbc_sb[:, 0:w], s))
            kb.after("gpsimd", m)
            kb.after("scalar", m)
            kb.after("tensor", m)
            kb.dma("gpsimd", tap_t[name], bbc_sb, "d_tap")
            for e in ("vector", "scalar", "sync"):
                kb.dwait(e, "d_tap")

    def startup_smalls():
        for ap_in, ap_sb in ((g1_in, g1_sb), (g2_in, g2_sb), (g3_in, g3_sb),
                             (idb_in, idb_sb), (idf_in, idf_sb), (h128_in, h128_sb),
                             (onecol_in, onecol_sb), (onerow_in, onerow_sb),
                             (cst_in, cst_sb), (onerowb_in, onerowb_sb),
                             (suug_in, sm16_sb)):
            kb.dma("sync", ap_sb, ap_in, "d_ld")
        kb.dwait("tensor", "d_ld")
        kb.ins("tensor", lambda e: e.transpose(pA[:, 0:1], g1_sb, idf_sb[0:1, 0:1]))
        kb.ins("tensor", lambda e: e.transpose(pA[:, 1:2], g2_sb, idf_sb[0:1, 0:1]))
        kb.ins("tensor", lambda e: e.transpose(pA[:, 2:3], g3_sb, idf_sb[0:1, 0:1]))
        mt = kb.ins("tensor", lambda e: e.transpose(pA[:, 512:528], sm16_sb,
                                                    idf_sb[0:16, 0:16]))
        kb.after("vector", mt)
        kb.ins("vector", lambda e: e.tensor_copy(suug_sb, pA[:, 512:528]))
        kb.ins("vector", lambda e: e.tensor_copy(ggcol_sb, pA[:, 0:1]))
        kb.ins("vector", lambda e: e.scalar_tensor_tensor(
            ggcol_sb, ggcol_sb, INVSQ_DH, pA[:, 1:2], op0=ALU.mult, op1=ALU.mult))
        mv = kb.ins("vector", lambda e: e.tensor_copy(g3col_sb, pA[:, 2:3]))

        state = {"mv": mv}

        def tp_then(dst_builder, src_dram, width):
            sm = sm64_sb if width == 64 else sm16_sb
            psv = pA[:, 1024:1024 + width]
            kb.after("sync", state["mv"])
            dm = kb.dma("sync", sm, src_dram, "d_ld")
            kb.dwait("tensor", "d_ld", dm)
            kb.after("tensor", state["mv"])
            mt = kb.ins("tensor", lambda e, sm=sm, p=psv: e.transpose(
                p, sm, idf_sb[0:sm.shape[0], 0:sm.shape[0]]))
            kb.after("vector", mt)
            state["mv"] = kb.ins("vector", dst_builder)

        tp_then(lambda e: e.tensor_copy(sudp_sb, pA[:, 1024:1088]), sudp_in, 64)
        tp_then(lambda e: e.scalar_tensor_tensor(
            wsu_sb, pA[:, 1024:1088], 64.0, sudp_sb, op0=ALU.mult, op1=ALU.mult), wsu_in, 64)
        tp_then(lambda e: e.tensor_scalar_mul(wsg_sb, pA[:, 1024:1088], 64.0), wsg_in, 64)
        tp_then(lambda e: e.tensor_scalar_mul(wsd_sb, pA[:, 1024:1040], 64.0 * FW2_SCALE), wsd_in, 16)
        kb.after("tensor", state["mv"])
        kb.after("scalar", state["mv"])

    cvt32p = [AR_A[:, 0:4096], AR_A[:, 4096:8192]]
    cvt16p = [ebf[:, 16384:20480], ebf[:, 20480:24576]]
    cv_state = {"mv": [None, None], "st": [0, 0]}

    def _cv_piece(par, load_src, width, store_fn):
        b32 = cvt32p[par][:, 0:width]
        b16 = cvt16p[par][:, 0:width]
        kb.dwait("sync", f"d_cs{par}", cv_state["st"][par])
        dm = kb.dma("sync", b32, load_src, f"d_cv{par}")
        kb.dwait("vector", f"d_cv{par}", dm)
        mv = kb.ins("vector", lambda e, o=b16, i=b32: e.tensor_copy(o, i))
        kb.after("scalar", mv)
        store_fn(b16, par)
        cv_state["st"][par] = kb.dval(f"d_cs{par}")

    def convert_square(src, dst):
        for ch in range(NCH):
            def store(b16, par, ch=ch, dst=dst):
                for g in range(4):
                    kb.dma("scalar", dst[g, ch], b16[:, g * 512:(g + 1) * 512], f"d_cs{par}")
            _cv_piece(ch % 2, src[ch * 128:(ch + 1) * 128, :], 2048, store)

    def convert_upgt(src, dst):
        for ch in range(NCH):
            for h in range(2):
                def store(b16, par, ch=ch, h=h, dst=dst):
                    for aa in range(32):
                        a = h * 32 + aa
                        kb.dma("scalar", dst[a, :, ch * 128:(ch + 1) * 128],
                               b16[:, aa * 128:(aa + 1) * 128], f"d_cs{par}")
                _cv_piece((ch * 2 + h) % 2,
                          src[ch * 128:(ch + 1) * 128, h * 4096:(h + 1) * 4096], 4096, store)

    def transform_dn():
        slotA = [ebf[:, i * 512:(i + 1) * 512] for i in range(NACH)]
        slotB = dnslotB
        for e in ("scalar", "vector", "tensor", "sync"):
            kb.dwait(e, "cv_st")
            kb.dwait(e, "d_cs0")
            kb.dwait(e, "d_cs1")
        b32g = [AR_W[:, 0:2048], AR_W[:, 2048:4096]]
        b16g = [wbf[:, 8192:10240], wbf[:, 10240:12288]]
        mm_marks = [None, None]
        cp_marks = [None, None]
        v_marks = [None, None]
        for cq in range(4):
            for g in range(NACH // 4):
                par = g % 2
                b32, b16 = b32g[par], b16g[par]
                ps = pA if par == 0 else pY
                kb.after("sync", v_marks[par])
                dm = kb.dma("sync", b32, wdt_in[g, cq], f"d_cv{par}")
                kb.dwait("vector", f"d_cv{par}", dm)
                kb.after("vector", mm_marks[par])
                v_marks[par] = kb.ins("vector", lambda e, o=b16, i=b32: e.tensor_copy(o, i))
                kb.after("tensor", v_marks[par], cp_marks[par])
                mm = None
                for k in range(4):
                    mm = kb.ins("tensor", lambda e, k=k, ps=ps, b16=b16: e.matmul(
                        ps[:, k * 512:(k + 1) * 512], h128_sb,
                        b16[:, k * 512:(k + 1) * 512], start=True, stop=True))
                mm_marks[par] = mm
                kb.after("scalar", mm)
                cp = None
                for k in range(4):
                    cp = kb.ins("scalar", lambda e, k=k, g=g, ps=ps: e.copy(
                        slotA[g * 4 + k], ps[:, k * 512:(k + 1) * 512]))
                cp_marks[par] = cp
            kb.after("vector", *cp_marks)
            src, dst = slotA, slotB
            stage_mark = None
            for bit in (1, 2, 4, 8, 16, 32):
                kb.fence("vector", stage_mark)
                for a in range(NACH):
                    p = a ^ bit
                    if a & bit == 0:
                        stage_mark = kb.ins("vector", lambda e, o=dst[a], u=src[a], v=src[p]:
                                            e.tensor_tensor(o, u, v, op=ALU.add))
                    else:
                        stage_mark = kb.ins("vector", lambda e, o=dst[a], u=src[p], v=src[a]:
                                            e.tensor_tensor(o, u, v, op=ALU.subtract))
                src, dst = dst, src
            kb.after("scalar", stage_mark)
            kb.dma("scalar", dn_s[cq], ebf[:, 0:NACH * 512] if src is slotA
                   else nbf[:, 0:NACH * 512], "cv_st")
            kb.dwait("sync", "cv_st")
            kb.dwait("vector", "cv_st")
            v_marks = [None] * 4

    def ln_pass(x_src, W, resident=None):
        nsl = W // 512
        if resident is not None:
            for ch in range(NCH):
                kb.dma("sync", resident[ch], x_src(ch), "d_xr")
            kb.dwait("vector", "d_xr")
        pe_marks = [None, None]
        v_marks = [None, None]
        mm = None
        for ch in range(NCH):
            if resident is None:
                xb = xch_sb[ch % 2][:, 0:W]
                kb.after("sync", v_marks[ch % 2])
                dm = kb.dma("sync", xb, x_src(ch), f"d_x{ch % 2}")
                kb.dwait("vector", f"d_x{ch % 2}", dm)
            else:
                xb = resident[ch]
            kb.after("vector", pe_marks[ch % 2])
            v_marks[ch % 2] = kb.ins("vector", lambda e, x=xb, ch=ch, W=W: e.tensor_copy(
                xbf_sb[ch % 2][:, 0:W], x))
            kb.fence("vector", v_marks[ch % 2])
            mv2 = kb.ins("vector", lambda e, ch=ch, W=W: e.tensor_tensor(
                sq_sb[ch % 2][:, 0:W], xbf_sb[ch % 2][:, 0:W], xbf_sb[ch % 2][:, 0:W],
                op=ALU.mult))
            kb.after("tensor", mv2)
            st, sp = (ch == 0), (ch == NCH - 1)
            for sl in range(nsl):
                kb.ins("tensor", lambda e, sl=sl, st=st, sp=sp, ch=ch: e.matmul(
                    pY[0:1, sl * 512:(sl + 1) * 512], onecol_sb,
                    xbf_sb[ch % 2][:, sl * 512:(sl + 1) * 512], start=st, stop=sp))
            for sl in range(nsl):
                mm = kb.ins("tensor", lambda e, sl=sl, st=st, sp=sp, ch=ch: e.matmul(
                    pY[32:33, sl * 512:(sl + 1) * 512], onecol_sb,
                    sq_sb[ch % 2][:, sl * 512:(sl + 1) * 512], start=st, stop=sp))
            pe_marks[ch % 2] = mm
        kb.after("vector", mm)
        kb.ins("vector", lambda e, W=W: e.tensor_scalar_mul(trow_sb[:, 0:W], pY[0:1, 0:W],
                                                            1.0 / 2048.0))
        kb.ins("vector", lambda e, W=W: e.tensor_scalar_mul(t2row_sb[:, 0:W],
                                                            pY[32:33, 0:W], 1.0 / 2048.0))
        mva = kb.ins("vector", lambda e, W=W: e.tensor_tensor(
            m2row_sb[:, 0:W], trow_sb[:, 0:W], trow_sb[:, 0:W], op=ALU.mult))
        kb.fence("vector", mva)
        mv = kb.ins("vector", lambda e, W=W: e.tensor_tensor(
            t2row_sb[:, 0:W], t2row_sb[:, 0:W], m2row_sb[:, 0:W], op=ALU.subtract))
        kb.after("scalar", mv)
        ms = kb.ins("scalar", lambda e, W=W: e.activation(
            m2row_sb[:, 0:W], t2row_sb[:, 0:W], AF.Ln, bias=cst_sb[0:1, 0:1], scale=1.0))
        kb.fence("scalar", ms)
        ms = kb.ins("scalar", lambda e, W=W: e.activation(
            arow_sb[:, 0:W], m2row_sb[:, 0:W], AF.Exp, bias=0.0, scale=-0.5))
        kb.after("vector", ms)
        mv = kb.ins("vector", lambda e, W=W: e.tensor_tensor(
            brow_sb[:, 0:W], trow_sb[:, 0:W], arow_sb[:, 0:W],
            op=ALU.mult))
        kb.after("tensor", mv)
        for sl in range(nsl):
            kb.ins("tensor", lambda e, sl=sl: e.matmul(
                pY[:, sl * 512:(sl + 1) * 512], onerow_sb, arow_sb[:, sl * 512:(sl + 1) * 512],
                start=True, stop=True))
        for sl in range(nsl):
            mm = kb.ins("tensor", lambda e, sl=sl: e.matmul(
                pA[:, sl * 512:(sl + 1) * 512], onerow_sb, brow_sb[:, sl * 512:(sl + 1) * 512],
                start=True, stop=True))
        kb.after("scalar", mm)
        kb.ins("scalar", lambda e, W=W: e.copy(abc_sb[:, 0:W], pY[:, 0:W]))
        ms = kb.ins("scalar", lambda e, W=W: e.copy(bbc_sb[:, 0:W], pA[:, 0:W]))
        return ms, v_marks

    def ln_wproj_n(s, pm):
        kb.phase_wait("sync", pm)
        kb.phase_wait("vector", pm)
        kb.phase_wait("scalar", pm)
        kb.phase_wait("tensor", pm)
        mabc, v_marks = ln_pass(lambda ch: x_in[s, ch * 128:(ch + 1) * 128, :], T)
        kb.after("vector", mabc)
        mlast = None
        for ch in range(NCH):
            xb = xch_sb[ch % 2]
            kb.after("sync", v_marks[ch % 2])
            dm = kb.dma("sync", xb, x_in[s, ch * 128:(ch + 1) * 128, :], f"d_x{ch % 2}")
            kb.dwait("vector", f"d_x{ch % 2}", dm)
            v_marks[ch % 2] = kb.ins("vector", lambda e, x=xb: e.tensor_tensor(
                tmp32_sb, x, abc_sb, op=ALU.mult))
            kb.fence("vector", v_marks[ch % 2])
            mlast = kb.ins("vector", lambda e, ch=ch: e.tensor_tensor(
                xn_ch[ch], tmp32_sb, bbc_sb, op=ALU.subtract))
        if s == 0:
            tapbf("xn0", xn_ch[0], prod=mlast)

        kb.after("tensor", mlast)
        m_nmul = None
        m_ncopy = None
        m_rbc_prev = None
        for hch in range(H):
            g = hch // 4
            if hch % 4 == 0:
                kb.after("sync", kb.mark("tensor"), mlast)
                dwb = 0
                for ch in range(NCH):
                    dwb = kb.dma("sync", wblk[ch], wt_s[g, ch], "d_wb")
                kb.dwait("tensor", "d_wb", dwb)
            kb.after("tensor", m_ncopy)
            mm = None
            for sl in range(4):
                for ch in range(NCH):
                    mm = kb.ins("tensor", lambda e, sl=sl, ch=ch, hch=hch: e.matmul(
                        pA[:, sl * 512:(sl + 1) * 512],
                        wblk[ch][:, (hch % 4) * 128:(hch % 4 + 1) * 128],
                        xn_ch[ch][:, sl * 512:(sl + 1) * 512],
                        start=(ch == 0), stop=(ch == NCH - 1)))
            kb.after("scalar", mm)
            m_ncopy = kb.ins("scalar", lambda e, hch=hch: e.copy(n_ch[hch], pA))
            kb.after("vector", m_ncopy)
            mv = kb.ins("vector", lambda e, hch=hch: e.tensor_tensor(
                sq_sb[hch % 2], n_ch[hch], n_ch[hch], op=ALU.mult))
            kb.after("tensor", mv, m_rbc_prev)
            qm = None
            for sl in range(4):
                qm = kb.ins("tensor", lambda e, sl=sl, hch=hch: e.matmul(
                    pY[0:1, sl * 512:(sl + 1) * 512], onecol_sb,
                    sq_sb[hch % 2][:, sl * 512:(sl + 1) * 512], start=True, stop=True))
            kb.after("scalar", qm)
            ms = kb.ins("scalar", lambda e: e.activation(
                m2row_sb, pY[0:1, :], AF.Ln, bias=cst_sb[0:1, 1:2], scale=1.0 / 128.0))
            kb.fence("scalar", ms)
            ms = kb.ins("scalar", lambda e: e.activation(
                arow_sb, m2row_sb, AF.Exp, bias=0.0, scale=-0.5))
            kb.after("tensor", ms)
            rm = None
            for sl in range(4):
                rm = kb.ins("tensor", lambda e, sl=sl: e.matmul(
                    pY[:, sl * 512:(sl + 1) * 512], onerow_sb,
                    arow_sb[:, sl * 512:(sl + 1) * 512], start=True, stop=True))
            kb.after("scalar", rm)
            m_rbc = kb.ins("scalar", lambda e: e.copy(rbc_sb, pY))
            m_rbc_prev = m_rbc
            kb.after("vector", m_rbc)
            m_nmul = kb.ins("vector", lambda e, hch=hch: e.tensor_tensor(
                n_ch[hch], n_ch[hch], rbc_sb, op=ALU.mult))
        if s == 0:
            tapbf("n0", n_ch[0], prod=m_nmul)
        return m_nmul

    def attention_head(s, h, m_prev):
        nh = n_ch[h]
        kb.dwait("scalar", "d_y")
        kb.after("scalar", m_prev)
        ms = kb.ins("scalar", lambda e: e.activation(ng_sb, nh, AF.Copy, bias=0.0,
                                                     scale=ggcol_sb))
        ms3 = kb.ins("scalar", lambda e: e.activation(ng3_sb, nh, AF.Copy, bias=0.0,
                                                      scale=g3col_sb))
        kb.after("vector", ms)
        msqg = kb.ins("vector", lambda e: e.tensor_tensor(sq_sb[0], ng_sb, ng_sb,
                                                          op=ALU.mult))
        kb.after("tensor", ms3, m_prev)
        cp_m = [None, None]
        for j in range(NCH):
            pslot = pA[:, 1024 + (j % 2) * 512:1024 + (j % 2) * 512 + 64].bitcast(BF16)
            kb.after("tensor", cp_m[j % 2])
            tp = kb.ins("tensor", lambda e, j=j, p=pslot: e.transpose(
                p, ng3_sb[:, j * 128:(j + 1) * 128], idb_sb))
            kb.after("scalar", tp)
            cp_m[j % 2] = kb.ins("scalar", lambda e, j=j, p=pslot: e.copy(
                vtok_sb[:, j * 128:(j + 1) * 128], p))
        kb.after("tensor", msqg, *cp_m)
        qm = None
        for j in range(NCH):
            qm = kb.ins("tensor", lambda e, j=j: e.matmul(
                pY[:, j:j + 1], sq_sb[0][:, j * 128:(j + 1) * 128], onecol_sb,
                start=True, stop=True))
        kb.after("scalar", qm)
        ms = kb.ins("scalar", lambda e: e.activation(tq_sb, pY[:, 0:16], AF.Ln,
                                                     bias=cst_sb[:, 2:3], scale=128.0))
        kb.fence("scalar", ms)
        ms = kb.ins("scalar", lambda e: e.activation(mh_sb, tq_sb, AF.Exp, bias=0.0,
                                                     scale=0.5))
        kb.after("vector", ms)
        kb.ins("vector", lambda e: e.tensor_scalar_mul(nmh_sb, mh_sb, -1.0))
        mred = kb.ins("vector", lambda e: e.tensor_reduce(kc1_sb, mh_sb, axis=AX.X,
                                                          op=ALU.max))
        kb.fence("vector", mred)
        mv = kb.ins("vector", lambda e: e.tensor_copy(kb16_sb, kc1_sb))
        kb.after("tensor", mv)
        mt = kb.ins("tensor", lambda e: e.transpose(pY[0:1, 16:80].bitcast(BF16), kb16_sb,
                                                    idb_sb))
        kb.after("scalar", mt)
        ms = kb.ins("scalar", lambda e: e.copy(ktp_sb, pY[0:1, 16:80].bitcast(BF16)))
        kb.after("vector", ms)
        mv = kb.ins("vector", lambda e: e.tensor_reduce(kscal_sb, ktp_sb,
                                                        axis=AX.X, op=ALU.max))
        kb.after("tensor", mv)
        mt = kb.ins("tensor", lambda e: e.matmul(pY[:, 96:97], onerow_sb, kscal_sb,
                                                 start=True, stop=True))
        kb.after("scalar", mt)
        ms = kb.ins("scalar", lambda e: e.copy(kcol_sb, pY[:, 96:97]))
        kb.after("vector", ms)
        mv = kb.ins("vector", lambda e: e.tensor_scalar_mul(nkcol_sb, kcol_sb, -1.0))
        kb.after("scalar", mv)
        mu = kb.ins("scalar", lambda e: e.activation(u_sb, mh_sb, AF.Exp, bias=nkcol_sb,
                                                     scale=1.0))

        kb.after("tensor", mu)
        m_eact = None
        m_eact2 = [None, None]
        for j in range(NCH):
            ps = pA if j % 2 == 0 else pY
            kb.after("tensor", m_eact2[j % 2])
            mm = None
            for sl in range(4):
                mm = kb.ins("tensor", lambda e, j=j, sl=sl, ps=ps: e.matmul(
                    ps[:, sl * 512:(sl + 1) * 512], ng_sb[:, j * 128:(j + 1) * 128],
                    nh[:, sl * 512:(sl + 1) * 512], start=True, stop=True))
            kb.after("scalar", mm)
            m_eact = kb.ins("scalar", lambda e, j=j, ps=ps: e.activation(
                E_strip[j], ps, AF.Exp, bias=nmh_sb[:, j:j + 1], scale=1.0,
                accum_out=z_sb[:, j:j + 1]))
            m_eact2[j % 2] = m_eact

        kb.fence("scalar", m_eact)
        mlnz = kb.ins("scalar", lambda e: e.activation(lnz_sb, z_sb, AF.Ln, bias=0.0,
                                                       scale=1.0))
        kb.after("vector", mlnz)
        mcc = kb.ins("vector", lambda e: e.tensor_tensor(cc_sb, mh_sb, lnz_sb, op=ALU.add))
        miz = kb.ins("vector", lambda e: e.reciprocal(iz_sb, z_sb))
        kb.after("tensor", mcc)
        mt = kb.ins("tensor", lambda e: e.transpose(pA[0:16, 0:128], cc_sb, idf_sb))
        kb.after("scalar", mt)
        ms = kb.ins("scalar", lambda e: e.copy(crow_sb, pA[0:16, 0:128]))
        kb.after("gpsimd", ms)
        db = kb.dma("gpsimd", bnc_s.rearrange("(a b) -> a b", a=16), crow_sb, "d_bn")
        kb.dwait("gpsimd", "d_bn", db)
        db = kb.dma("gpsimd", crow1_sb, bnc_s.rearrange("(a b) -> a b", a=1), "d_bn")
        kb.dwait("scalar", "d_bn", db)
        msv = kb.ins("scalar", lambda e: e.activation(xbf_sb[1][0:1, :], crow1_sb,
                                                      AF.Exp, bias=kcol_sb[0:1, 0:1],
                                                      scale=-1.0))
        kb.after("tensor", msv)
        vm = None
        for sl in range(4):
            vm = kb.ins("tensor", lambda e, sl=sl: e.matmul(
                pA[:, sl * 512:(sl + 1) * 512], onerowb_sb,
                xbf_sb[1][0:1, sl * 512:(sl + 1) * 512], start=True, stop=True))
        kb.after("scalar", vm)
        msvbc = kb.ins("scalar", lambda e: e.copy(vbc_sb, pA))

        kb.after("vector", msvbc, miz, mu)
        m_pv = [None, None]
        pvlast = None
        for j in range(NCH):
            kb.after("vector", m_pv[j % 2])
            m_eb = kb.ins("vector", lambda e, j=j: e.scalar_tensor_tensor(
                eb_sb[j % 2], E_strip[j], u_sb[:, j:j + 1], vbc_sb,
                op0=ALU.mult, op1=ALU.mult))
            m_va = kb.ins("vector", lambda e, j=j: e.tensor_scalar_mul(
                vtokA_sb[:, j * 128:(j + 1) * 128], vtok_sb[:, j * 128:(j + 1) * 128],
                iz_sb[:, j:j + 1]))
            kb.after("tensor", m_eb, m_va)
            for sl in range(4):
                kb.ins("tensor", lambda e, j=j, sl=sl: e.matmul(
                    pY[:, sl * 512:(sl + 1) * 512], vtokA_sb[:, j * 128:(j + 1) * 128],
                    E_strip[j][:, sl * 512:(sl + 1) * 512], start=(j == 0), stop=False))
            for sl in range(4):
                pvlast = kb.ins("tensor", lambda e, j=j, sl=sl: e.matmul(
                    pY[:, sl * 512:(sl + 1) * 512], vtok_sb[:, j * 128:(j + 1) * 128],
                    eb_sb[j % 2][:, sl * 512:(sl + 1) * 512], start=False,
                    stop=(j == NCH - 1)))
            m_pv[j % 2] = pvlast
        kb.after("scalar", pvlast)
        my = kb.ins("scalar", lambda e: e.copy(ysb_sb, pY))
        kb.after("gpsimd", my)
        kb.dma("gpsimd", y_s[s, h], ysb_sb, "d_y")
        if f"y{s}_{h}" in tapset:
            tapbf(f"y{s}_{h}", ysb_sb, prod=my)
        return my

    def proj_residual(s, m_attn):
        kb.dwait("sync", "d_y")
        kb.after("sync", m_attn)
        m_ev = [None, None]
        m_mm = None
        for cp in range(8):
            g = cp // 2
            if cp % 2 == 0:
                if g >= 1:
                    kb.after("sync", m_mm)
                dwb = 0
                for m in range(H):
                    dwb = kb.dma("sync", wblk[m], w_s[g, m], "d_wb")
                kb.dwait("tensor", "d_wb", dwb)
            kb.after("tensor", m_ev[0], m_ev[1])
            pe_y = [None, None]
            for m in range(H):
                yb = ybuf_sb[m % 2]
                q = "sync" if m % 2 == 0 else "scalar"
                if pe_y[m % 2] is not None:
                    kb.after(q, pe_y[m % 2])
                elif cp > 0:
                    kb.after(q, m_mm)
                dyb = kb.dma(q, yb, y_s[s, m], f"d_yb{m % 2}")
                kb.dwait("tensor", f"d_yb{m % 2}", dyb)
                for cc in range(2):
                    for sl in range(4):
                        m_mm = kb.ins("tensor", lambda e, m=m, cc=cc, sl=sl, cp=cp, yb=yb:
                                      e.matmul(
                                          (pA if cc == 0 else pY)[:, sl * 512:(sl + 1) * 512],
                                          wblk[m][:, ((cp % 2) * 2 + cc) * 128:
                                                  ((cp % 2) * 2 + cc + 1) * 128],
                                          yb[:, sl * 512:(sl + 1) * 512],
                                          start=(m == 0), stop=(m == H - 1)))
                pe_y[m % 2] = m_mm
            for cc in range(2):
                chg = cp * 2 + cc
                xdst = tmp32_sb if cc == 0 else abc_sb
                odst = bbc_sb if cc == 0 else tmp32_sb
                kb.dwait("sync", "d_x1s")
                kb.after("sync", m_ev[cc])
                dm = kb.dma("sync", xdst, x_in[s, chg * 128:(chg + 1) * 128, :], f"d_x{cc}")
                kb.dwait("vector", f"d_x{cc}", dm)
                kb.after("vector", m_mm)
                m_ev[cc] = kb.ins("vector", lambda e, cc=cc, xdst=xdst, odst=odst:
                                  e.tensor_tensor(odst, (pA if cc == 0 else pY), xdst,
                                                  op=ALU.add))
                kb.after("gpsimd", m_ev[cc])
                kb.dma("gpsimd", x1_s[s, chg], odst, "d_x1s")
        kb.dwait("sync", "d_x1s")
        kb.dwait("vector", "d_x1s")
        return m_ev[1]

    def ffn_tile(s, q, pm):
        t0 = q * TTOK
        kb.phase_wait("sync", pm)
        kb.phase_wait("vector", pm)
        kb.phase_wait("scalar", pm)
        kb.phase_wait("tensor", pm)
        mabc, _vm = ln_pass(lambda ch: x1_s[s, ch][:, t0:t0 + TTOK], TTOK, resident=x1t)
        kb.after("vector", mabc)
        mfw = None
        for ch in range(NCH):
            m1 = kb.ins("vector", lambda e, ch=ch: e.tensor_tensor(
                tmp32_sb[:, 0:TTOK], x1t[ch], abc_sb[:, 0:TTOK], op=ALU.mult))
            kb.fence("vector", m1)
            m2 = kb.ins("vector", lambda e, ch=ch: e.tensor_tensor(
                tmp32_sb[:, 512:512 + TTOK], tmp32_sb[:, 0:TTOK], bbc_sb[:, 0:TTOK],
                op=ALU.subtract))
            kb.fence("vector", m2)
            mfw = kb.ins("vector", lambda e, ch=ch: e.tensor_scalar_mul(
                fwpp[ch], tmp32_sb[:, 512:512 + TTOK], suug_sb[:, ch:ch + 1]))
        src, dst = fwpp, xf_ch
        stage_mark = mfw
        for bit in (1, 2, 4, 8):
            kb.fence("vector", stage_mark)
            for a in range(NCH):
                p = a ^ bit
                if a & bit == 0:
                    stage_mark = kb.ins("vector", lambda e, o=dst[a], u=src[a], v=src[p]:
                                        e.tensor_tensor(o, u, v, op=ALU.add))
                else:
                    stage_mark = kb.ins("vector", lambda e, o=dst[a], u=src[p], v=src[a]:
                                        e.tensor_tensor(o, u, v, op=ALU.subtract))
            src, dst = dst, src
        kb.after("tensor", stage_mark)
        hm = [None] * 4
        smk = [None] * 4
        for ch in range(NCH):
            k = ch % 4
            kb.after("tensor", smk[k])
            hm[k] = kb.ins("tensor", lambda e, ch=ch, src=src: e.matmul(
                pA[:, (ch % 4) * 512:(ch % 4 + 1) * 512], h128_sb, src[ch],
                start=True, stop=True))
            kb.after("scalar", hm[k])
            smk[k] = kb.ins("scalar", lambda e, ch=ch: e.activation(
                xf_ch[ch], pA[:, (ch % 4) * 512:(ch % 4 + 1) * 512], AF.Copy,
                bias=0.0, scale=FW1_SCALE))
        if q == 0:
            tapbf(f"xf{s}", xf_ch[0], TTOK, prod=smk[0])
        kb.after("tensor", *smk)
        m_gate = [None, None]
        m_sg = [None, None]
        m_ug = [None, None]
        for a in range(NACH):
            par = a % 2
            kb.after("sync", m_ug[par])
            kb.dma("sync", wbf[:, par * 4096:par * 4096 + 2048], up_s[a], f"d_ug{par}")
            dug = kb.dma("sync", wbf[:, par * 4096 + 2048:par * 4096 + 4096], gt_s[a],
                         f"d_ug{par}")
            kb.dwait("tensor", f"d_ug{par}", dug)
            kb.after("tensor", m_gate[par])
            pu = pA[:, par * 1024:par * 1024 + 512]
            pg = pA[:, par * 1024 + 512:par * 1024 + 1024]
            for ch in range(NCH):
                kb.ins("tensor", lambda e, a=a, ch=ch, par=par, pu=pu: e.matmul(
                    pu, ugw[par][ch], xf_ch[ch], start=(ch == 0), stop=(ch == NCH - 1)))
            for ch in range(NCH):
                m_ug[par] = kb.ins("tensor", lambda e, a=a, ch=ch, par=par, pg=pg: e.matmul(
                    pg, ugw[par][16 + ch], xf_ch[ch], start=(ch == 0), stop=(ch == NCH - 1)))
            kb.after("scalar", m_ug[par], m_gate[par])
            m_sg[par] = kb.ins("scalar", lambda e, a=a, pg=pg, par=par: e.activation(
                sgbuf_sb[par], pg, AF.Silu, bias=0.0, scale=wsg_sb[:, a:a + 1]))
            kb.after("vector", m_sg[par])
            m_gate[par] = kb.ins("vector", lambda e, a=a, pu=pu, par=par:
                                 e.scalar_tensor_tensor(
                                     g_store[a], pu, wsu_sb[:, a:a + 1], sgbuf_sb[par],
                                     op0=ALU.mult, op1=ALU.mult))
        kb.after("tensor", m_gate[0], m_gate[1])
        m_add = None
        dn_mm = None
        for cq in range(4):
            dn_prev = [None, None] if dn_mm is None else [dn_mm, dn_mm]
            for grp in range(16):
                par = grp % 2
                kb.after("sync", dn_prev[par])
                ddn = kb.dma("sync", tmp32_sb.bitcast(BF16)[:, par * 2048:(par + 1) * 2048],
                             dn_s[cq][:, grp * 2048:(grp + 1) * 2048], f"d_dn{par}")
                kb.dwait("tensor", f"d_dn{par}", ddn)
                if cq > 0 and grp == 0:
                    kb.after("tensor", m_add)
                for aa in range(4):
                    a = grp * 4 + aa
                    for cc in range(4):
                        dn_mm = kb.ins("tensor", lambda e, a=a, aa=aa, cc=cc, par=par:
                                       e.matmul(
                                           pY[:, cc * 512:(cc + 1) * 512],
                                           dnw[par][aa][:, cc * 128:(cc + 1) * 128],
                                           g_store[a], start=(a == 0), stop=(a == NACH - 1)))
                dn_prev[par] = dn_mm
            kb.after("scalar", dn_mm)
            m_add = None
            for cc in range(4):
                chg = cq * 4 + cc
                kb.dwait("scalar", "d_out")
                kb.after("scalar", m_add)
                msc = kb.ins("scalar", lambda e, cc=cc, chg=chg: e.activation(
                    abc_sb[:, 0:TTOK], pY[:, cc * 512:(cc + 1) * 512], AF.Copy,
                    bias=0.0, scale=wsd_sb[:, chg:chg + 1]))
                kb.after("vector", msc)
                m_add = kb.ins("vector", lambda e, cc=cc, chg=chg: e.tensor_tensor(
                    bbc_sb[:, 0:TTOK], abc_sb[:, 0:TTOK], x1t[chg], op=ALU.add))
                kb.after("gpsimd", m_add)
                kb.dma("gpsimd", out_t[s, chg][:, t0:t0 + TTOK], bbc_sb[:, 0:TTOK], "d_out")
                kb.dwait("vector", "d_out")
        kb.dwait("sync", "d_out")
        return kb.phase_mark()

    if "startup" in stages:
        startup_smalls()
        convert_square(wt_in, wt_s)
        convert_square(w_in, w_s)
        convert_upgt(wupt_in, up_s)
        convert_upgt(wgt_in, gt_s)
        transform_dn()
        for e in ("sync", "gpsimd", "tensor", "scalar", "vector"):
            kb.dwait(e, "cv_st")
            kb.dwait(e, "d_cs0")
            kb.dwait(e, "d_cs1")
    pm = kb.phase_mark()
    for s in range(SEQN):
        if f"seq{s}" in stages:
            m = ln_wproj_n(s, pm)
            for h in range(H):
                m = attention_head(s, h, m)
            proj_residual(s, m)
            pm = kb.phase_mark()
    if "ffn" in stages:
        for s in range(SEQN):
            for q in range(T // TTOK):
                pm = ffn_tile(s, q, pm)
    kb.dwait("gpsimd", "d_out")
    kb.dwait("gpsimd", "d_tap")
    kb.dwait("sync", "d_out")

    with nc.Block() as block:
        @block.sync
        def _(eng):
            for fn in kb.ops["sync"]:
                fn(eng)

        @block.vector
        def _(eng):
            for fn in kb.ops["vector"]:
                fn(eng)

        @block.scalar
        def _(eng):
            for fn in kb.ops["scalar"]:
                fn(eng)

        @block.tensor
        def _(eng):
            for fn in kb.ops["tensor"]:
                fn(eng)

        @block.gpsimd
        def _(eng):
            for fn in kb.ops["gpsimd"]:
                fn(eng)

    return kb


def host_inputs(inputs):
    x = np.asarray(inputs["L0_attn_x"], dtype=np.float32)
    W = np.asarray(inputs["L0_attn_W"], dtype=np.float32)
    Wt = np.ascontiguousarray(W.T)
    WupT = np.ascontiguousarray(np.asarray(inputs["L0_ffn_W_up"], dtype=np.float32).T)
    WgT = np.ascontiguousarray(np.asarray(inputs["L0_ffn_W_gate"], dtype=np.float32).T)
    WdT = np.asarray(inputs["L0_ffn_W_down"], dtype=np.float32).T
    WdTb = np.ascontiguousarray(
        WdT.reshape(16, 4, 128, 4, 512).transpose(0, 3, 2, 1, 4)
    ).reshape(16, 4, 128, 2048)
    g1 = np.asarray(inputs["L0_attn_g1"], dtype=np.float32).reshape(1, 128)
    g2 = np.asarray(inputs["L0_attn_g2"], dtype=np.float32).reshape(1, 128)
    g3 = np.asarray(inputs["L0_attn_g3"], dtype=np.float32).reshape(1, 128)
    suug = np.asarray(inputs["L0_ffn_SU_ug"], dtype=np.float32).reshape(NCH, 128)
    sudp = np.asarray(inputs["L0_ffn_SU_dp"], dtype=np.float32).reshape(NACH, 128)
    wsug = np.asarray(inputs["L0_ffn_Wscale_ug"], dtype=np.float32)
    wsu = np.ascontiguousarray(wsug[:INTER].reshape(NACH, 128))
    wsg = np.ascontiguousarray(wsug[INTER:].reshape(NACH, 128))
    wsd = np.asarray(inputs["L0_ffn_Wscale_dp"], dtype=np.float32).reshape(NCH, 128)

    h128 = _hadamard(128).astype(ml_dtypes.bfloat16)
    idb = np.eye(128, dtype=ml_dtypes.bfloat16)
    idf = np.eye(128, dtype=np.float32)
    onecol = np.ones((128, 1), dtype=ml_dtypes.bfloat16)
    onerow = np.ones((1, 128), dtype=np.float32)
    onerowb = np.ones((1, 128), dtype=ml_dtypes.bfloat16)
    cst = np.zeros((128, 4), dtype=np.float32)
    cst[:, 0] = EPS_LN
    cst[:, 1] = EPS_RMS
    cst[:, 2] = 1e-37

    x_fm = np.ascontiguousarray(x.transpose(0, 2, 1))

    shared = dict(g1=g1, g2=g2, g3=g3,
                  SUug=suug, SUdp=sudp, WSu=wsu, WSg=wsg, WSd=wsd,
                  H128=h128, IDB=idb, IDF=idf, ONECOL=onecol, ONEROW=onerow,
                  ONEROWB=onerowb, CST=cst, Wt=Wt, WdT=WdTb, W=W,
                  WupT=WupT, WgT=WgT)
    in_maps = []
    for k in range(NCORES):
        m = {"x": x_fm[k * SEQN:(k + 1) * SEQN]}
        m.update(shared)
        in_maps.append(m)
    return in_maps


def kernel(**inputs):
    in_maps = host_inputs(inputs)
    kbp = build_program()
    res = run_bass_kernel_spmd(kbp.nc, in_maps, core_ids=list(range(NCORES)))
    parts = [r["out"] for r in res.results]
    out_fm = np.concatenate(parts, axis=0).reshape(B, C, T)
    return np.ascontiguousarray(out_fm.transpose(0, 2, 1))


# version 11
# speedup vs baseline: 1.1615x; 1.1490x over previous
import math
import numpy as np
import ml_dtypes
import concourse.bass as bass
import concourse.mybir as mybir
from concourse.bass_utils import run_bass_kernel_spmd

F32 = mybir.dt.float32
F32R = mybir.dt.float32r
BF16 = mybir.dt.bfloat16
AF = mybir.ActivationFunctionType
ALU = mybir.AluOpType
AX = mybir.AxisListType

B, T, C, H, DH = 16, 2048, 2048, 16, 128
INTER = 8192
NCORES = 8
SEQN = 2
NCH = C // 128
NACH = INTER // 128
EPS_RMS = float(np.finfo(np.float32).eps)
EPS_LN = 1e-5
FW1_SCALE = 1.0 / (64.0 * math.sqrt(2048.0))
FW2_SCALE = 1.0 / (64.0 * math.sqrt(8192.0))
INVSQ_DH = 1.0 / math.sqrt(DH)
TTOK = 512

ENGINES = ("sync", "vector", "scalar", "tensor", "gpsimd")
COMPUTE = ("vector", "scalar", "tensor", "gpsimd")
ALL_STAGES = ("startup", "seq0", "seq1", "ffn")


def _hadamard(n):
    h = np.array([[1.0]], dtype=np.float32)
    while h.shape[0] < n:
        h = np.block([[h, h], [h, -h]])
    return h


class KB:
    def __init__(self):
        self.nc = bass.Bass("TRN2", target_bir_lowering=False)
        self.ops = {e: [] for e in ENGINES}
        self.sems = {}
        self.cnt = {}
        self.ecnt = {e: 0 for e in COMPUTE}
        self.esem = {e: self.nc.alloc_semaphore(f"e_{e}") for e in COMPUTE}
        self.lastwait = {e: {} for e in ENGINES}

    def sem(self, name):
        if name not in self.sems:
            self.sems[name] = self.nc.alloc_semaphore(name)
            self.cnt[name] = 0
        return self.sems[name]

    def dval(self, name):
        self.sem(name)
        return self.cnt[name]

    def dwait(self, engine, name, value=None):
        s = self.sem(name)
        v = self.cnt[name] if value is None else value
        if v <= 0:
            return
        key = ("d", name)
        if self.lastwait[engine].get(key, -1) >= v:
            return
        self.lastwait[engine][key] = v
        self.ops[engine].append(lambda eng, s=s, v=v: eng.wait_ge(s, v))

    def dma(self, engine, out, in_, sem_name):
        s = self.sem(sem_name)
        self.cnt[sem_name] += 16
        self.ops[engine].append(
            lambda eng, o=out, i=in_, s=s: eng.dma_start(out=o, in_=i).then_inc(s, 16))
        return self.cnt[sem_name]

    def ins(self, engine, build):
        s = self.esem[engine]
        if engine in ("vector", "scalar", "gpsimd") and self.ecnt[engine] > 0:
            v = self.ecnt[engine]
            key = ("e", engine)
            if self.lastwait[engine].get(key, -1) < v:
                self.lastwait[engine][key] = v
                self.ops[engine].append(lambda eng, s=s, v=v: eng.wait_ge(s, v))
        self.ecnt[engine] += 1
        self.ops[engine].append(lambda eng, s=s, b=build: b(eng).then_inc(s, 1))
        return (engine, self.ecnt[engine])

    def mark(self, engine):
        return (engine, self.ecnt[engine])

    def phase_mark(self):
        return {e: self.ecnt[e] for e in COMPUTE}

    def phase_wait(self, engine, pm):
        if pm is None:
            return
        for pe in COMPUTE:
            self.after(engine, (pe, pm[pe]))

    def after(self, engine, *marks):
        for m in marks:
            if m is None:
                continue
            pe, v = m
            if v <= 0:
                continue
            if pe == engine:
                continue
            key = ("e", pe)
            if self.lastwait[engine].get(key, -1) >= v:
                continue
            self.lastwait[engine][key] = v
            self.ops[engine].append(
                lambda eng, s=self.esem[pe], v=v: eng.wait_ge(s, v))

    def fence(self, engine, *marks):
        for m in marks:
            if m is None:
                continue
            pe, v = m
            if v <= 0:
                continue
            key = ("e", pe)
            if self.lastwait[engine].get(key, -1) >= v:
                continue
            self.lastwait[engine][key] = v
            self.ops[engine].append(
                lambda eng, s=self.esem[pe], v=v: eng.wait_ge(s, v))


def build_program(stages=ALL_STAGES, taps=()):
    kb = KB()
    nc = kb.nc
    tapset = set(taps)

    x_in = nc.dram_tensor("x", [SEQN, C, T], F32, kind="ExternalInput").ap()
    wt_in = nc.dram_tensor("Wt", [C, C], F32, kind="ExternalInput").ap()
    w_in = nc.dram_tensor("W", [C, C], F32, kind="ExternalInput").ap()
    wupt_in = nc.dram_tensor("WupT", [C, INTER], F32, kind="ExternalInput").ap()
    wgt_in = nc.dram_tensor("WgT", [C, INTER], F32, kind="ExternalInput").ap()
    wdt_in = nc.dram_tensor("WdT", [16, 4, 128, 2048], F32, kind="ExternalInput").ap()
    g1_in = nc.dram_tensor("g1", [1, 128], F32, kind="ExternalInput").ap()
    g2_in = nc.dram_tensor("g2", [1, 128], F32, kind="ExternalInput").ap()
    g3_in = nc.dram_tensor("g3", [1, 128], F32, kind="ExternalInput").ap()
    suug_in = nc.dram_tensor("SUug", [NCH, 128], F32, kind="ExternalInput").ap()
    sudp_in = nc.dram_tensor("SUdp", [NACH, 128], F32, kind="ExternalInput").ap()
    wsu_in = nc.dram_tensor("WSu", [NACH, 128], F32, kind="ExternalInput").ap()
    wsg_in = nc.dram_tensor("WSg", [NACH, 128], F32, kind="ExternalInput").ap()
    wsd_in = nc.dram_tensor("WSd", [NCH, 128], F32, kind="ExternalInput").ap()
    h128_in = nc.dram_tensor("H128", [128, 128], BF16, kind="ExternalInput").ap()
    idb_in = nc.dram_tensor("IDB", [128, 128], BF16, kind="ExternalInput").ap()
    idf_in = nc.dram_tensor("IDF", [128, 128], F32, kind="ExternalInput").ap()
    onecol_in = nc.dram_tensor("ONECOL", [128, 1], BF16, kind="ExternalInput").ap()
    onerow_in = nc.dram_tensor("ONEROW", [1, 128], F32, kind="ExternalInput").ap()
    onerowb_in = nc.dram_tensor("ONEROWB", [1, 128], BF16, kind="ExternalInput").ap()
    cst_in = nc.dram_tensor("CST", [128, 4], F32, kind="ExternalInput").ap()

    out_t = nc.dram_tensor("out", [SEQN, NCH, 128, T], F32, kind="ExternalOutput").ap()

    wt_s = nc.dram_tensor("wt_s", [4, NCH, 128, 512], BF16, kind="Internal").ap()
    w_s = nc.dram_tensor("w_s", [4, NCH, 128, 512], BF16, kind="Internal").ap()
    up_s = nc.dram_tensor("up_s", [NACH, 128, NCH * 128], BF16, kind="Internal").ap()
    gt_s = nc.dram_tensor("gt_s", [NACH, 128, NCH * 128], BF16, kind="Internal").ap()
    dn_s = nc.dram_tensor("dn_s", [4, 128, NACH * 512], BF16, kind="Internal").ap()
    y_s = nc.dram_tensor("y_s", [SEQN, H, 128, T], BF16, kind="Internal").ap()
    x1_s = nc.dram_tensor("x1_s", [SEQN, NCH, 128, T], F32, kind="Internal").ap()
    bnc_s = nc.dram_tensor("bnc_s", [T], F32, kind="Internal").ap()

    A = nc.alloc_sbuf_tensor
    AR_A = A("AR_A", [128, 16384], F32)[:]
    AR_B = A("AR_B", [128, 16384], F32)[:]
    AR_W = A("AR_W", [128, 6144], F32)[:]

    xbf_sb = [A(f"xbf{i}", [128, T], BF16)[:] for i in range(2)]
    sq_sb = [A(f"sq{i}", [128, T], BF16)[:] for i in range(2)]
    tmp32_sb = A("tmp32", [128, T], F32)[:]
    abc_sb = A("abc", [128, T], F32)[:]
    bbc_sb = A("bbc", [128, T], F32)[:]
    xch_sb = [AR_W[:, 0:2048], AR_W[:, 2048:4096]]
    rbc_sb = bbc_sb
    sgbuf_sb = [sq_sb[0].bitcast(F32)[:, 0:512], sq_sb[1].bitcast(F32)[:, 0:512]]

    z2_sb = A("z2", [128, 32], F32)[:]
    mh_sb = A("mh", [128, 16], F32)[:]
    nmh_sb = A("nmh", [128, 16], F32)[:]
    z_sb = A("z", [128, 16], F32)[:]
    lnz_sb = A("lnz", [128, 16], F32)[:]
    cc_sb = A("cc", [128, 16], F32)[:]
    u_sb = A("u", [128, 16], F32)[:]
    iz_sb = A("iz", [128, 16], F32)[:]
    tq_sb = A("tq", [128, 16], F32)[:]
    kc1_sb = A("kc1", [128, 1], F32)[:]
    kb16_sb = A("kb16", [128, 1], BF16)[:]
    kcol_sb = A("kcol", [128, 1], F32)[:]
    nkcol_sb = A("nkcol", [128, 1], F32)[:]
    ktp_sb = A("ktp", [1, 128], F32)[:]
    kscal_sb = A("kscal", [1, 1], F32)[:]
    crow_sb = A("crow", [16, 128], F32)[:]
    cst_sb = A("cst", [128, 4], F32)[:]
    idb_sb = A("idb", [128, 128], BF16)[:]
    idf_sb = A("idf", [128, 128], F32)[:]
    h128_sb = A("h128", [128, 128], BF16)[:]
    onecol_sb = A("onecol", [128, 1], BF16)[:]
    onerow_sb = A("onerow", [1, 128], F32)[:]
    onerowb_sb = A("onerowb", [1, 128], BF16)[:]
    g1_sb = A("g1s", [1, 128], F32)[:]
    g2_sb = A("g2s", [1, 128], F32)[:]
    g3_sb = A("g3s", [1, 128], F32)[:]
    ggcol_sb = A("ggcol", [128, 1], F32)[:]
    g3col_sb = A("g3col", [128, 1], F32)[:]
    suug_sb = A("suug", [128, NCH], F32)[:]
    sudp_sb = A("sudp", [128, NACH], F32)[:]
    wsu_sb = A("wsu", [128, NACH], F32)[:]
    wsg_sb = A("wsg", [128, NACH], F32)[:]
    wsd_sb = A("wsd", [128, NCH], F32)[:]
    sm16_sb = A("sm16", [NCH, 128], F32)[:]
    sm64_sb = A("sm64", [NACH, 128], F32)[:]

    rowm_sb = A("rowm", [1, T], F32)[:]
    trow_sb = abc_sb[0:1, :]
    t2row_sb = bbc_sb[0:1, :]
    m2row_sb = rowm_sb
    arow_sb = tmp32_sb[0:1, :]
    brow_sb = AR_W[0:1, 4096:6144]
    crow1_sb = abc_sb[0:1, :]
    vrow_sb = tmp32_sb[0:1, :]

    ebf = AR_A.bitcast(BF16)
    E_strip = [ebf[:, j * T:(j + 1) * T] for j in range(NCH)]
    xn_ch = E_strip
    g_store = [ebf[:, a * 512:(a + 1) * 512] for a in range(NACH)]
    cvt32 = AR_A[:, 0:8192]
    cvt16 = ebf[:, 16384:16384 + 8192]

    nbf = AR_B.bitcast(BF16)
    n_ch = [nbf[:, h * T:(h + 1) * T] for h in range(H)]
    dnslotB = [nbf[:, i * 512:(i + 1) * 512] for i in range(NACH)]
    x1t = [AR_B[:, ch * 512:(ch + 1) * 512] for ch in range(NCH)]
    xf_ch = [nbf[:, 16384 + ch * 512:16384 + (ch + 1) * 512] for ch in range(NCH)]
    fwpp = [nbf[:, 24576 + ch * 512:24576 + (ch + 1) * 512] for ch in range(NCH)]

    wbf = AR_W.bitcast(BF16)
    wblk = [wbf[:, c * 512:(c + 1) * 512] for c in range(16)]
    ybuf_sb = [wbf[:, 8192:10240], wbf[:, 10240:12288]]
    vtok_sb = wbf[:, 0:2048]
    vtokA_sb = wbf[:, 2048:4096]
    vtokB_sb = wbf[:, 4096:6144]
    ysb_sb = wbf[:, 6144:8192]
    ng_sb = wbf[:, 8192:10240]
    ng3_sb = wbf[:, 10240:12288]
    dncv = [wbf[:, 4096 + k * 512:4096 + (k + 1) * 512] for k in range(4)]
    dncv32 = [AR_W[:, k * 512:(k + 1) * 512] for k in range(4)]
    ugw = [[wbf[:, i * 4096 + c * 128:i * 4096 + (c + 1) * 128] for c in range(32)]
           for i in range(2)]
    dnw = [[tmp32_sb.bitcast(BF16)[:, i * 2048 + a * 512:i * 2048 + (a + 1) * 512]
            for a in range(4)] for i in range(2)]

    psATT = nc.alloc_psum_tensor("psATT", [128, 2048], F32)
    psY = nc.alloc_psum_tensor("psY", [128, 2048], F32)
    pA = psATT[:]
    pY = psY[:]

    assert nc.sbuf_bytes_remaining >= 0

    tap_t = {}
    for name in tapset:
        tap_t[name] = nc.dram_tensor(f"tap_{name}", [128, T], F32, kind="ExternalOutput").ap()

    def tapbf(name, src_bf, width=T, prod=None):
        if name in tapset:
            kb.dwait("vector", "d_tap")
            kb.after("vector", prod)
            m = kb.ins("vector", lambda e, s=src_bf, w=width: e.tensor_copy(bbc_sb[:, 0:w], s))
            kb.after("gpsimd", m)
            kb.after("scalar", m)
            kb.after("tensor", m)
            kb.dma("gpsimd", tap_t[name], bbc_sb, "d_tap")
            for e in ("vector", "scalar", "sync"):
                kb.dwait(e, "d_tap")

    def startup_smalls():
        for ap_in, ap_sb in ((g1_in, g1_sb), (g2_in, g2_sb), (g3_in, g3_sb),
                             (idb_in, idb_sb), (idf_in, idf_sb), (h128_in, h128_sb),
                             (onecol_in, onecol_sb), (onerow_in, onerow_sb),
                             (cst_in, cst_sb), (onerowb_in, onerowb_sb),
                             (suug_in, sm16_sb)):
            kb.dma("sync", ap_sb, ap_in, "d_ld")
        kb.dwait("tensor", "d_ld")
        kb.ins("tensor", lambda e: e.transpose(pA[:, 0:1], g1_sb, idf_sb[0:1, 0:1]))
        kb.ins("tensor", lambda e: e.transpose(pA[:, 1:2], g2_sb, idf_sb[0:1, 0:1]))
        kb.ins("tensor", lambda e: e.transpose(pA[:, 2:3], g3_sb, idf_sb[0:1, 0:1]))
        mt = kb.ins("tensor", lambda e: e.transpose(pA[:, 512:528], sm16_sb,
                                                    idf_sb[0:16, 0:16]))
        kb.after("vector", mt)
        kb.ins("vector", lambda e: e.tensor_copy(suug_sb, pA[:, 512:528]))
        kb.ins("vector", lambda e: e.tensor_copy(ggcol_sb, pA[:, 0:1]))
        kb.ins("vector", lambda e: e.scalar_tensor_tensor(
            ggcol_sb, ggcol_sb, INVSQ_DH, pA[:, 1:2], op0=ALU.mult, op1=ALU.mult))
        mv = kb.ins("vector", lambda e: e.tensor_copy(g3col_sb, pA[:, 2:3]))

        state = {"mv": mv}

        def tp_then(dst_builder, src_dram, width):
            sm = sm64_sb if width == 64 else sm16_sb
            psv = pA[:, 1024:1024 + width]
            kb.after("sync", state["mv"])
            dm = kb.dma("sync", sm, src_dram, "d_ld")
            kb.dwait("tensor", "d_ld", dm)
            kb.after("tensor", state["mv"])
            mt = kb.ins("tensor", lambda e, sm=sm, p=psv: e.transpose(
                p, sm, idf_sb[0:sm.shape[0], 0:sm.shape[0]]))
            kb.after("vector", mt)
            state["mv"] = kb.ins("vector", dst_builder)

        tp_then(lambda e: e.tensor_copy(sudp_sb, pA[:, 1024:1088]), sudp_in, 64)
        tp_then(lambda e: e.scalar_tensor_tensor(
            wsu_sb, pA[:, 1024:1088], 64.0, sudp_sb, op0=ALU.mult, op1=ALU.mult), wsu_in, 64)
        tp_then(lambda e: e.tensor_scalar_mul(wsg_sb, pA[:, 1024:1088], 64.0), wsg_in, 64)
        tp_then(lambda e: e.tensor_scalar_mul(wsd_sb, pA[:, 1024:1040], 64.0 * FW2_SCALE), wsd_in, 16)
        kb.after("tensor", state["mv"])
        kb.after("scalar", state["mv"])

    cvt32p = [AR_A[:, 0:4096], AR_A[:, 4096:8192]]
    cvt16p = [ebf[:, 16384:20480], ebf[:, 20480:24576]]
    cv_state = {"mv": [None, None], "st": [0, 0]}

    def _cv_piece(par, load_src, width, store_fn):
        b32 = cvt32p[par][:, 0:width]
        b16 = cvt16p[par][:, 0:width]
        kb.dwait("sync", f"d_cs{par}", cv_state["st"][par])
        dm = kb.dma("sync", b32, load_src, f"d_cv{par}")
        kb.dwait("vector", f"d_cv{par}", dm)
        mv = kb.ins("vector", lambda e, o=b16, i=b32: e.tensor_copy(o, i))
        kb.after("scalar", mv)
        store_fn(b16, par)
        cv_state["st"][par] = kb.dval(f"d_cs{par}")

    def convert_square(src, dst):
        for ch in range(NCH):
            def store(b16, par, ch=ch, dst=dst):
                kb.dma("scalar", dst[:, ch].rearrange("g p f -> p g f"), b16, f"d_cs{par}")
            _cv_piece(ch % 2, src[ch * 128:(ch + 1) * 128, :], 2048, store)

    def convert_upgt(src, dst):
        for ch in range(NCH):
            for h in range(2):
                def store(b16, par, ch=ch, h=h, dst=dst):
                    kb.dma("scalar",
                           dst[h * 32:(h + 1) * 32, :, ch * 128:(ch + 1) * 128]
                           .rearrange("a p j -> p a j"),
                           b16.rearrange("p (a j) -> p a j", a=32), f"d_cs{par}")
                _cv_piece((ch * 2 + h) % 2,
                          src[ch * 128:(ch + 1) * 128, h * 4096:(h + 1) * 4096], 4096, store)

    def transform_dn():
        slotA = [ebf[:, i * 512:(i + 1) * 512] for i in range(NACH)]
        slotB = dnslotB
        for e in ("scalar", "vector", "tensor", "sync"):
            kb.dwait(e, "cv_st")
            kb.dwait(e, "d_cs0")
            kb.dwait(e, "d_cs1")
        b32g = [AR_W[:, 0:2048], AR_W[:, 2048:4096]]
        b16g = [wbf[:, 8192:10240], wbf[:, 10240:12288]]
        mm_marks = [None, None]
        cp_marks = [None, None]
        v_marks = [None, None]
        for cq in range(4):
            for g in range(NACH // 4):
                par = g % 2
                b32, b16 = b32g[par], b16g[par]
                ps = pA if par == 0 else pY
                kb.after("sync", v_marks[par])
                dm = kb.dma("sync", b32, wdt_in[g, cq], f"d_cv{par}")
                kb.dwait("vector", f"d_cv{par}", dm)
                kb.after("vector", mm_marks[par])
                v_marks[par] = kb.ins("vector", lambda e, o=b16, i=b32: e.tensor_copy(o, i))
                kb.after("tensor", v_marks[par], cp_marks[par])
                mm = None
                for k in range(4):
                    mm = kb.ins("tensor", lambda e, k=k, ps=ps, b16=b16: e.matmul(
                        ps[:, k * 512:(k + 1) * 512], h128_sb,
                        b16[:, k * 512:(k + 1) * 512], start=True, stop=True))
                mm_marks[par] = mm
                kb.after("scalar", mm)
                cp = None
                for k in range(4):
                    cp = kb.ins("scalar", lambda e, k=k, g=g, ps=ps: e.copy(
                        slotA[g * 4 + k], ps[:, k * 512:(k + 1) * 512]))
                cp_marks[par] = cp
            kb.after("vector", *cp_marks)
            src, dst = slotA, slotB
            stage_mark = None
            for bit in (1, 2, 4, 8, 16, 32):
                kb.fence("vector", stage_mark)
                for a in range(NACH):
                    p = a ^ bit
                    if a & bit == 0:
                        stage_mark = kb.ins("vector", lambda e, o=dst[a], u=src[a], v=src[p]:
                                            e.tensor_tensor(o, u, v, op=ALU.add))
                    else:
                        stage_mark = kb.ins("vector", lambda e, o=dst[a], u=src[p], v=src[a]:
                                            e.tensor_tensor(o, u, v, op=ALU.subtract))
                src, dst = dst, src
            kb.after("scalar", stage_mark)
            kb.dma("scalar", dn_s[cq], ebf[:, 0:NACH * 512] if src is slotA
                   else nbf[:, 0:NACH * 512], "cv_st")
            kb.dwait("sync", "cv_st")
            kb.dwait("vector", "cv_st")
            v_marks = [None] * 4

    def ln_pass(x_src, W, resident=None):
        nsl = W // 512
        if resident is not None:
            for ch in range(NCH):
                kb.dma("sync", resident[ch], x_src(ch), "d_xr")
            kb.dwait("vector", "d_xr")
        pe_marks = [None, None]
        v_marks = [None, None]
        mm = None
        for ch in range(NCH):
            if resident is None:
                xb = xch_sb[ch % 2][:, 0:W]
                kb.after("sync", v_marks[ch % 2])
                dm = kb.dma("sync", xb, x_src(ch), f"d_x{ch % 2}")
                kb.dwait("vector", f"d_x{ch % 2}", dm)
            else:
                xb = resident[ch]
            kb.after("vector", pe_marks[ch % 2])
            v_marks[ch % 2] = kb.ins("vector", lambda e, x=xb, ch=ch, W=W: e.tensor_copy(
                xbf_sb[ch % 2][:, 0:W], x))
            kb.fence("vector", v_marks[ch % 2])
            mv2 = kb.ins("vector", lambda e, ch=ch, W=W: e.tensor_tensor(
                sq_sb[ch % 2][:, 0:W], xbf_sb[ch % 2][:, 0:W], xbf_sb[ch % 2][:, 0:W],
                op=ALU.mult))
            kb.after("tensor", mv2)
            st, sp = (ch == 0), (ch == NCH - 1)
            for sl in range(nsl):
                kb.ins("tensor", lambda e, sl=sl, st=st, sp=sp, ch=ch: e.matmul(
                    pY[0:1, sl * 512:(sl + 1) * 512], onecol_sb,
                    xbf_sb[ch % 2][:, sl * 512:(sl + 1) * 512], start=st, stop=sp))
            for sl in range(nsl):
                mm = kb.ins("tensor", lambda e, sl=sl, st=st, sp=sp, ch=ch: e.matmul(
                    pY[32:33, sl * 512:(sl + 1) * 512], onecol_sb,
                    sq_sb[ch % 2][:, sl * 512:(sl + 1) * 512], start=st, stop=sp))
            pe_marks[ch % 2] = mm
        kb.after("vector", mm)
        kb.ins("vector", lambda e, W=W: e.tensor_scalar_mul(trow_sb[:, 0:W], pY[0:1, 0:W],
                                                            1.0 / 2048.0))
        kb.ins("vector", lambda e, W=W: e.tensor_scalar_mul(t2row_sb[:, 0:W],
                                                            pY[32:33, 0:W], 1.0 / 2048.0))
        mva = kb.ins("vector", lambda e, W=W: e.tensor_tensor(
            m2row_sb[:, 0:W], trow_sb[:, 0:W], trow_sb[:, 0:W], op=ALU.mult))
        kb.fence("vector", mva)
        mv = kb.ins("vector", lambda e, W=W: e.tensor_tensor(
            t2row_sb[:, 0:W], t2row_sb[:, 0:W], m2row_sb[:, 0:W], op=ALU.subtract))
        kb.after("scalar", mv)
        ms = kb.ins("scalar", lambda e, W=W: e.activation(
            m2row_sb[:, 0:W], t2row_sb[:, 0:W], AF.Ln, bias=cst_sb[0:1, 0:1], scale=1.0))
        kb.fence("scalar", ms)
        ms = kb.ins("scalar", lambda e, W=W: e.activation(
            arow_sb[:, 0:W], m2row_sb[:, 0:W], AF.Exp, bias=0.0, scale=-0.5))
        kb.after("vector", ms)
        mv = kb.ins("vector", lambda e, W=W: e.tensor_tensor(
            brow_sb[:, 0:W], trow_sb[:, 0:W], arow_sb[:, 0:W],
            op=ALU.mult))
        kb.after("tensor", mv)
        for sl in range(nsl):
            kb.ins("tensor", lambda e, sl=sl: e.matmul(
                pY[:, sl * 512:(sl + 1) * 512], onerow_sb, arow_sb[:, sl * 512:(sl + 1) * 512],
                start=True, stop=True))
        for sl in range(nsl):
            mm = kb.ins("tensor", lambda e, sl=sl: e.matmul(
                pA[:, sl * 512:(sl + 1) * 512], onerow_sb, brow_sb[:, sl * 512:(sl + 1) * 512],
                start=True, stop=True))
        kb.after("scalar", mm)
        kb.ins("scalar", lambda e, W=W: e.copy(abc_sb[:, 0:W], pY[:, 0:W]))
        ms = kb.ins("scalar", lambda e, W=W: e.copy(bbc_sb[:, 0:W], pA[:, 0:W]))
        return ms, v_marks

    def ln_wproj_n(s, pm):
        kb.phase_wait("sync", pm)
        kb.phase_wait("vector", pm)
        kb.phase_wait("scalar", pm)
        kb.phase_wait("tensor", pm)
        mabc, v_marks = ln_pass(lambda ch: x_in[s, ch * 128:(ch + 1) * 128, :], T)
        kb.after("vector", mabc)
        mlast = None
        for ch in range(NCH):
            xb = xch_sb[ch % 2]
            kb.after("sync", v_marks[ch % 2])
            dm = kb.dma("sync", xb, x_in[s, ch * 128:(ch + 1) * 128, :], f"d_x{ch % 2}")
            kb.dwait("vector", f"d_x{ch % 2}", dm)
            v_marks[ch % 2] = kb.ins("vector", lambda e, x=xb: e.tensor_tensor(
                tmp32_sb, x, abc_sb, op=ALU.mult))
            kb.fence("vector", v_marks[ch % 2])
            mlast = kb.ins("vector", lambda e, ch=ch: e.tensor_tensor(
                xn_ch[ch], tmp32_sb, bbc_sb, op=ALU.subtract))
        if s == 0:
            tapbf("xn0", xn_ch[0], prod=mlast)

        kb.after("tensor", mlast)
        m_nmul = None
        m_ncopy = None
        m_rbc_prev = None
        for hch in range(H):
            g = hch // 4
            if hch % 4 == 0:
                kb.after("sync", kb.mark("tensor"), mlast)
                dwb = 0
                for ch in range(NCH):
                    dwb = kb.dma("sync", wblk[ch], wt_s[g, ch], "d_wb")
                kb.dwait("tensor", "d_wb", dwb)
            kb.after("tensor", m_ncopy)
            mm = None
            for sl in range(4):
                for ch in range(NCH):
                    mm = kb.ins("tensor", lambda e, sl=sl, ch=ch, hch=hch: e.matmul(
                        pA[:, sl * 512:(sl + 1) * 512],
                        wblk[ch][:, (hch % 4) * 128:(hch % 4 + 1) * 128],
                        xn_ch[ch][:, sl * 512:(sl + 1) * 512],
                        start=(ch == 0), stop=(ch == NCH - 1)))
            kb.after("scalar", mm)
            m_ncopy = kb.ins("scalar", lambda e, hch=hch: e.copy(n_ch[hch], pA))
            kb.after("vector", m_ncopy)
            mv = kb.ins("vector", lambda e, hch=hch: e.tensor_tensor(
                sq_sb[hch % 2], n_ch[hch], n_ch[hch], op=ALU.mult))
            kb.after("tensor", mv, m_rbc_prev)
            qm = None
            for sl in range(4):
                qm = kb.ins("tensor", lambda e, sl=sl, hch=hch: e.matmul(
                    pY[0:1, sl * 512:(sl + 1) * 512], onecol_sb,
                    sq_sb[hch % 2][:, sl * 512:(sl + 1) * 512], start=True, stop=True))
            kb.after("scalar", qm)
            ms = kb.ins("scalar", lambda e: e.activation(
                m2row_sb, pY[0:1, :], AF.Ln, bias=cst_sb[0:1, 1:2], scale=1.0 / 128.0))
            kb.fence("scalar", ms)
            ms = kb.ins("scalar", lambda e: e.activation(
                arow_sb, m2row_sb, AF.Exp, bias=0.0, scale=-0.5))
            kb.after("tensor", ms)
            rm = None
            for sl in range(4):
                rm = kb.ins("tensor", lambda e, sl=sl: e.matmul(
                    pY[:, sl * 512:(sl + 1) * 512], onerow_sb,
                    arow_sb[:, sl * 512:(sl + 1) * 512], start=True, stop=True))
            kb.after("scalar", rm)
            m_rbc = kb.ins("scalar", lambda e: e.copy(rbc_sb, pY))
            m_rbc_prev = m_rbc
            kb.after("vector", m_rbc)
            m_nmul = kb.ins("vector", lambda e, hch=hch: e.tensor_tensor(
                n_ch[hch], n_ch[hch], rbc_sb, op=ALU.mult))
        if s == 0:
            tapbf("n0", n_ch[0], prod=m_nmul)
        return m_nmul

    def attn_prechain(s, h, P):
        nh = n_ch[h]
        kb.after("scalar", P.get("att_last"))
        ms_ng = kb.ins("scalar", lambda e: e.activation(
            ng_sb, nh, AF.Copy, bias=0.0, scale=ggcol_sb))
        ms_ng3 = kb.ins("scalar", lambda e: e.activation(
            ng3_sb, nh, AF.Copy, bias=0.0, scale=g3col_sb))
        kb.after("vector", ms_ng, P.get("qm_last"))
        msqg = kb.ins("vector", lambda e: e.tensor_tensor(
            sq_sb[0], ng_sb, ng_sb, op=ALU.mult))
        kb.after("scalar", P.get("zadd15"))
        cp_m = [None, None]
        for j in range(NCH):
            pslot = pY[:, 1024 + (j % 2) * 512:1024 + (j % 2) * 512 + 64].bitcast(BF16)
            kb.after("tensor", cp_m[j % 2],
                     ms_ng3 if j < 2 else None,
                     P.get("ms_abc") if j < 2 else None)
            tp = kb.ins("tensor", lambda e, j=j, p=pslot: e.transpose(
                p, ng3_sb[:, j * 128:(j + 1) * 128], idb_sb))
            kb.after("scalar", tp)
            cp_m[j % 2] = kb.ins("scalar", lambda e, j=j, p=pslot: e.copy(
                vtok_sb[:, j * 128:(j + 1) * 128], p))
        kb.after("tensor", msqg, P.get("ms_abc"))
        qm = None
        for j in range(NCH):
            qm = kb.ins("tensor", lambda e, j=j: e.matmul(
                pY[:, j:j + 1], sq_sb[0][:, j * 128:(j + 1) * 128], onecol_sb,
                start=True, stop=True))
        kb.after("scalar", qm, P.get("mcc"))
        ms = kb.ins("scalar", lambda e: e.activation(tq_sb, pY[:, 0:16], AF.Ln,
                                                     bias=cst_sb[:, 2:3], scale=128.0))
        kb.fence("scalar", ms)
        ms_mh = kb.ins("scalar", lambda e: e.activation(mh_sb, tq_sb, AF.Exp, bias=0.0,
                                                        scale=0.5))
        kb.after("vector", ms_mh)
        mv_nmh = kb.ins("vector", lambda e: e.tensor_scalar_mul(nmh_sb, mh_sb, -1.0))
        mred = kb.ins("vector", lambda e: e.tensor_reduce(kc1_sb, mh_sb, axis=AX.X,
                                                          op=ALU.max))
        kb.fence("vector", mred)
        mv = kb.ins("vector", lambda e: e.tensor_copy(kb16_sb, kc1_sb))
        kb.after("tensor", mv)
        mt = kb.ins("tensor", lambda e: e.transpose(pY[0:1, 16:80].bitcast(BF16), kb16_sb,
                                                    idb_sb))
        kb.after("scalar", mt)
        ms = kb.ins("scalar", lambda e: e.copy(ktp_sb, pY[0:1, 16:80].bitcast(BF16)))
        kb.after("vector", ms)
        mv = kb.ins("vector", lambda e: e.tensor_reduce(kscal_sb, ktp_sb,
                                                        axis=AX.X, op=ALU.max))
        kb.after("tensor", mv)
        mt = kb.ins("tensor", lambda e: e.matmul(pY[:, 96:97], onerow_sb, kscal_sb,
                                                 start=True, stop=True))
        kb.after("scalar", mt)
        ms_kcol = kb.ins("scalar", lambda e: e.copy(kcol_sb, pY[:, 96:97]))
        kb.after("vector", ms_kcol)
        mv_nk = kb.ins("vector", lambda e: e.tensor_scalar_mul(nkcol_sb, kcol_sb, -1.0))
        kb.after("scalar", mv_nk)
        mu = kb.ins("scalar", lambda e: e.activation(u_sb, mh_sb, AF.Exp, bias=nkcol_sb,
                                                     scale=1.0))
        return {"mu": mu, "nmh": mv_nmh, "ms_kcol": ms_kcol, "qm_last": qm,
                "cp_last": cp_m}

    def attn_head(s, h, P, Q):
        nh = n_ch[h]
        kb.after("vector", P["mu"], Q.get("pv2_last"), *[m for m in P["cp_last"] if m])
        mvB = None
        for j in range(NCH):
            mvB = kb.ins("vector", lambda e, j=j: e.tensor_scalar_mul(
                vtokB_sb[:, j * 128:(j + 1) * 128], vtok_sb[:, j * 128:(j + 1) * 128],
                u_sb[:, j:j + 1]))
        kb.after("tensor", Q.get("y2"), P["ms_kcol"], P["mu"])
        kb.after("scalar", Q.get("pv2_last"), P["nmh"], Q.get("zadd15"))
        expm = {}
        vAm = {}
        att_last = None
        pv1_last = None

        def emit_att(j):
            nonlocal att_last
            for half in range(2):
                kb.after("tensor", expm.get((j - 1, half)))
                for sl in (0, 1):
                    c = half * 2 + sl
                    att_last = kb.ins("tensor", lambda e, j=j, c=c: e.matmul(
                        pA[:, c * 512:(c + 1) * 512], ng_sb[:, j * 128:(j + 1) * 128],
                        nh[:, c * 512:(c + 1) * 512], start=True, stop=True))
                kb.after("scalar", att_last)
                expm[(j, half)] = kb.ins("scalar", lambda e, j=j, half=half: e.activation(
                    E_strip[j][:, half * 1024:(half + 1) * 1024],
                    pA[:, half * 1024:(half + 1) * 1024], AF.Exp,
                    bias=nmh_sb[:, j:j + 1], scale=1.0,
                    accum_out=z2_sb[:, 2 * j + half:2 * j + half + 1]))

        def emit_zA(j):
            kb.after("vector", expm[(j, 1)],
                     Q.get("mlnz_prev") if j == 0 else None)
            kb.ins("vector", lambda e, j=j: e.tensor_tensor(
                z_sb[:, j:j + 1], z2_sb[:, 2 * j:2 * j + 1],
                z2_sb[:, 2 * j + 1:2 * j + 2], op=ALU.add))
            kb.ins("vector", lambda e, j=j: e.reciprocal(iz_sb[:, j:j + 1],
                                                         z_sb[:, j:j + 1]))
            vAm[j] = kb.ins("vector", lambda e, j=j: e.tensor_scalar_mul(
                vtokA_sb[:, j * 128:(j + 1) * 128], vtok_sb[:, j * 128:(j + 1) * 128],
                iz_sb[:, j:j + 1]))

        def emit_pv(j, ps, off):
            nonlocal pv1_last
            st, sp = (j == 0), (j == NCH - 1)
            for c, (vt, half) in enumerate(((vtokA_sb, 0 + off), (vtokA_sb, 1 + off),
                                            (vtokB_sb, 0 + off), (vtokB_sb, 1 + off))):
                pv1_last = kb.ins("tensor", lambda e, j=j, c=c, vt=vt, half=half,
                                  st=st, sp=sp, ps=ps: e.matmul(
                    ps[:, c * 512:(c + 1) * 512], vt[:, j * 128:(j + 1) * 128],
                    E_strip[j][:, half * 512:(half + 1) * 512], start=st, stop=sp))

        emit_att(0)
        for j in range(NCH):
            if j + 1 < NCH:
                emit_att(j + 1)
            emit_zA(j)
            kb.after("tensor", vAm[j], mvB if j == 0 else None)
            emit_pv(j, pY, 0)
        pv1_last_m = pv1_last

        kb.after("scalar", vAm[NCH - 1])
        mlnz = kb.ins("scalar", lambda e: e.activation(lnz_sb, z_sb, AF.Ln, bias=0.0,
                                                       scale=1.0))
        kb.after("vector", mlnz)
        mcc = kb.ins("vector", lambda e: e.tensor_tensor(cc_sb, mh_sb, lnz_sb,
                                                         op=ALU.add))
        kb.after("scalar", pv1_last_m, Q.get("y1_prev"))
        cpA = kb.ins("scalar", lambda e: e.copy(tmp32_sb[:, 0:1024], pY[:, 0:1024]))
        cpB = kb.ins("scalar", lambda e: e.copy(tmp32_sb[:, 1024:2048], pY[:, 1024:2048]))

        kb.after("tensor", expm[(NCH - 1, 1)])
        for j in range(4):
            emit_pv(j, pA, 2)
        pv2_j3 = pv1_last
        kb.after("tensor", mcc, cpB)
        mt = kb.ins("tensor", lambda e: e.transpose(pY[0:16, 0:128], cc_sb, idf_sb))
        kb.after("scalar", mt)
        kb.dwait("scalar", "d_bn")
        ms = kb.ins("scalar", lambda e: e.copy(crow_sb, pY[0:16, 0:128]))
        kb.dwait("gpsimd", "d_bn")
        kb.after("gpsimd", ms, Q.get("y2"))
        db = kb.dma("gpsimd", bnc_s.rearrange("(a b) -> a b", a=16), crow_sb, "d_bn")
        kb.dwait("gpsimd", "d_bn", db)
        db = kb.dma("gpsimd", crow1_sb, bnc_s.rearrange("(a b) -> a b", a=1), "d_bn")
        kb.dwait("scalar", "d_bn", db)
        msv = kb.ins("scalar", lambda e: e.activation(xbf_sb[1][0:1, :], crow1_sb,
                                                      AF.Exp, bias=kcol_sb[0:1, 0:1],
                                                      scale=-1.0))
        for j in range(4, 8):
            emit_pv(j, pA, 2)
        kb.after("tensor", msv)
        vm = None
        for sl in range(4):
            vm = kb.ins("tensor", lambda e, sl=sl: e.matmul(
                pY[:, sl * 512:(sl + 1) * 512], onerowb_sb,
                xbf_sb[1][0:1, sl * 512:(sl + 1) * 512], start=True, stop=True))
        kb.after("scalar", vm, Q.get("y2"))
        ms_abc = kb.ins("scalar", lambda e: e.copy(abc_sb, pY))
        for j in range(8, NCH):
            emit_pv(j, pA, 2)
        pv2_last = pv1_last
        kb.dwait("vector", "d_y")
        kb.after("vector", ms_abc)
        mv = kb.ins("vector", lambda e: e.tensor_tensor(
            bbc_sb[:, 0:1024], tmp32_sb[:, 1024:2048], abc_sb[:, 0:1024], op=ALU.mult))
        kb.fence("vector", mv)
        y1 = kb.ins("vector", lambda e: e.tensor_tensor(
            ysb_sb[:, 0:1024], bbc_sb[:, 0:1024], tmp32_sb[:, 0:1024], op=ALU.add))
        kb.after("vector", pv2_last)
        mv = kb.ins("vector", lambda e: e.tensor_tensor(
            bbc_sb[:, 1024:2048], pA[:, 1024:2048], abc_sb[:, 1024:2048], op=ALU.mult))
        kb.fence("vector", mv)
        y2 = kb.ins("vector", lambda e: e.tensor_tensor(
            ysb_sb[:, 1024:2048], bbc_sb[:, 1024:2048], pA[:, 0:1024], op=ALU.add))
        kb.after("gpsimd", y2)
        kb.dma("gpsimd", y_s[s, h], ysb_sb, "d_y")
        return {"y2": y2, "pv2_last": pv2_last, "att_last": att_last,
                "mlnz_prev": mlnz, "y1_prev": y1, "ms_abc": ms_abc,
                "mcc": mcc, "zadd15": vAm[NCH - 1], "qm_last": P["qm_last"]}

    def proj_residual(s, m_attn):
        kb.dwait("sync", "d_y")
        kb.after("sync", m_attn)
        m_ev = [None, None]
        m_mm = None
        for cp in range(8):
            g = cp // 2
            if cp % 2 == 0:
                if g >= 1:
                    kb.after("sync", m_mm)
                dwb = 0
                for m in range(H):
                    dwb = kb.dma("sync", wblk[m], w_s[g, m], "d_wb")
                kb.dwait("tensor", "d_wb", dwb)
            kb.after("tensor", m_ev[0], m_ev[1])
            pe_y = [None, None]
            for m in range(H):
                yb = ybuf_sb[m % 2]
                q = "sync" if m % 2 == 0 else "scalar"
                if pe_y[m % 2] is not None:
                    kb.after(q, pe_y[m % 2])
                elif cp > 0:
                    kb.after(q, m_mm)
                dyb = kb.dma(q, yb, y_s[s, m], f"d_yb{m % 2}")
                kb.dwait("tensor", f"d_yb{m % 2}", dyb)
                for cc in range(2):
                    for sl in range(4):
                        m_mm = kb.ins("tensor", lambda e, m=m, cc=cc, sl=sl, cp=cp, yb=yb:
                                      e.matmul(
                                          (pA if cc == 0 else pY)[:, sl * 512:(sl + 1) * 512],
                                          wblk[m][:, ((cp % 2) * 2 + cc) * 128:
                                                  ((cp % 2) * 2 + cc + 1) * 128],
                                          yb[:, sl * 512:(sl + 1) * 512],
                                          start=(m == 0), stop=(m == H - 1)))
                pe_y[m % 2] = m_mm
            for cc in range(2):
                chg = cp * 2 + cc
                xdst = tmp32_sb if cc == 0 else abc_sb
                odst = bbc_sb if cc == 0 else tmp32_sb
                kb.dwait("sync", "d_x1s")
                kb.after("sync", m_ev[cc])
                dm = kb.dma("sync", xdst, x_in[s, chg * 128:(chg + 1) * 128, :], f"d_x{cc}")
                kb.dwait("vector", f"d_x{cc}", dm)
                kb.after("vector", m_mm)
                m_ev[cc] = kb.ins("vector", lambda e, cc=cc, xdst=xdst, odst=odst:
                                  e.tensor_tensor(odst, (pA if cc == 0 else pY), xdst,
                                                  op=ALU.add))
                kb.after("gpsimd", m_ev[cc])
                kb.dma("gpsimd", x1_s[s, chg], odst, "d_x1s")
        kb.dwait("sync", "d_x1s")
        kb.dwait("vector", "d_x1s")
        return m_ev[1]

    def ffn_tile(s, q, pm):
        t0 = q * TTOK
        kb.phase_wait("sync", pm)
        kb.phase_wait("vector", pm)
        kb.phase_wait("scalar", pm)
        kb.phase_wait("tensor", pm)
        mabc, _vm = ln_pass(lambda ch: x1_s[s, ch][:, t0:t0 + TTOK], TTOK, resident=x1t)
        kb.after("vector", mabc)
        mfw = None
        for ch in range(NCH):
            m1 = kb.ins("vector", lambda e, ch=ch: e.tensor_tensor(
                tmp32_sb[:, 0:TTOK], x1t[ch], abc_sb[:, 0:TTOK], op=ALU.mult))
            kb.fence("vector", m1)
            m2 = kb.ins("vector", lambda e, ch=ch: e.tensor_tensor(
                tmp32_sb[:, 512:512 + TTOK], tmp32_sb[:, 0:TTOK], bbc_sb[:, 0:TTOK],
                op=ALU.subtract))
            kb.fence("vector", m2)
            mfw = kb.ins("vector", lambda e, ch=ch: e.tensor_scalar_mul(
                fwpp[ch], tmp32_sb[:, 512:512 + TTOK], suug_sb[:, ch:ch + 1]))
        src, dst = fwpp, xf_ch
        stage_mark = mfw
        for bit in (1, 2, 4, 8):
            kb.fence("vector", stage_mark)
            for a in range(NCH):
                p = a ^ bit
                if a & bit == 0:
                    stage_mark = kb.ins("vector", lambda e, o=dst[a], u=src[a], v=src[p]:
                                        e.tensor_tensor(o, u, v, op=ALU.add))
                else:
                    stage_mark = kb.ins("vector", lambda e, o=dst[a], u=src[p], v=src[a]:
                                        e.tensor_tensor(o, u, v, op=ALU.subtract))
            src, dst = dst, src
        kb.after("tensor", stage_mark)
        hm = [None] * 4
        smk = [None] * 4
        for ch in range(NCH):
            k = ch % 4
            kb.after("tensor", smk[k])
            hm[k] = kb.ins("tensor", lambda e, ch=ch, src=src: e.matmul(
                pA[:, (ch % 4) * 512:(ch % 4 + 1) * 512], h128_sb, src[ch],
                start=True, stop=True))
            kb.after("scalar", hm[k])
            smk[k] = kb.ins("scalar", lambda e, ch=ch: e.activation(
                xf_ch[ch], pA[:, (ch % 4) * 512:(ch % 4 + 1) * 512], AF.Copy,
                bias=0.0, scale=FW1_SCALE))
        if q == 0:
            tapbf(f"xf{s}", xf_ch[0], TTOK, prod=smk[0])
        kb.after("tensor", *smk)
        m_gate = [None, None]
        m_sg = [None, None]
        m_ug = [None, None]
        for a in range(NACH):
            par = a % 2
            kb.after("sync", m_ug[par])
            kb.dma("sync", wbf[:, par * 4096:par * 4096 + 2048], up_s[a], f"d_ug{par}")
            dug = kb.dma("sync", wbf[:, par * 4096 + 2048:par * 4096 + 4096], gt_s[a],
                         f"d_ug{par}")
            kb.dwait("tensor", f"d_ug{par}", dug)
            kb.after("tensor", m_gate[par])
            pu = pA[:, par * 1024:par * 1024 + 512]
            pg = pA[:, par * 1024 + 512:par * 1024 + 1024]
            for ch in range(NCH):
                kb.ins("tensor", lambda e, a=a, ch=ch, par=par, pu=pu: e.matmul(
                    pu, ugw[par][ch], xf_ch[ch], start=(ch == 0), stop=(ch == NCH - 1)))
            for ch in range(NCH):
                m_ug[par] = kb.ins("tensor", lambda e, a=a, ch=ch, par=par, pg=pg: e.matmul(
                    pg, ugw[par][16 + ch], xf_ch[ch], start=(ch == 0), stop=(ch == NCH - 1)))
            kb.after("scalar", m_ug[par], m_gate[par])
            m_sg[par] = kb.ins("scalar", lambda e, a=a, pg=pg, par=par: e.activation(
                sgbuf_sb[par], pg, AF.Silu, bias=0.0, scale=wsg_sb[:, a:a + 1]))
            kb.after("vector", m_sg[par])
            m_gate[par] = kb.ins("vector", lambda e, a=a, pu=pu, par=par:
                                 e.scalar_tensor_tensor(
                                     g_store[a], pu, wsu_sb[:, a:a + 1], sgbuf_sb[par],
                                     op0=ALU.mult, op1=ALU.mult))
        kb.after("tensor", m_gate[0], m_gate[1])
        m_add = None
        dn_mm = None
        for cq in range(4):
            dn_prev = [None, None] if dn_mm is None else [dn_mm, dn_mm]
            for grp in range(16):
                par = grp % 2
                kb.after("sync", dn_prev[par])
                ddn = kb.dma("sync", tmp32_sb.bitcast(BF16)[:, par * 2048:(par + 1) * 2048],
                             dn_s[cq][:, grp * 2048:(grp + 1) * 2048], f"d_dn{par}")
                kb.dwait("tensor", f"d_dn{par}", ddn)
                if cq > 0 and grp == 0:
                    kb.after("tensor", m_add)
                for aa in range(4):
                    a = grp * 4 + aa
                    for cc in range(4):
                        dn_mm = kb.ins("tensor", lambda e, a=a, aa=aa, cc=cc, par=par:
                                       e.matmul(
                                           pY[:, cc * 512:(cc + 1) * 512],
                                           dnw[par][aa][:, cc * 128:(cc + 1) * 128],
                                           g_store[a], start=(a == 0), stop=(a == NACH - 1)))
                dn_prev[par] = dn_mm
            kb.after("scalar", dn_mm)
            m_add = None
            for cc in range(4):
                chg = cq * 4 + cc
                kb.dwait("scalar", "d_out")
                kb.after("scalar", m_add)
                msc = kb.ins("scalar", lambda e, cc=cc, chg=chg: e.activation(
                    abc_sb[:, 0:TTOK], pY[:, cc * 512:(cc + 1) * 512], AF.Copy,
                    bias=0.0, scale=wsd_sb[:, chg:chg + 1]))
                kb.after("vector", msc)
                m_add = kb.ins("vector", lambda e, cc=cc, chg=chg: e.tensor_tensor(
                    bbc_sb[:, 0:TTOK], abc_sb[:, 0:TTOK], x1t[chg], op=ALU.add))
                kb.after("gpsimd", m_add)
                kb.dma("gpsimd", out_t[s, chg][:, t0:t0 + TTOK], bbc_sb[:, 0:TTOK], "d_out")
                kb.dwait("vector", "d_out")
        kb.dwait("sync", "d_out")
        return kb.phase_mark()

    if "startup" in stages:
        startup_smalls()
        convert_square(wt_in, wt_s)
        convert_square(w_in, w_s)
        convert_upgt(wupt_in, up_s)
        convert_upgt(wgt_in, gt_s)
        transform_dn()
        for e in ("sync", "gpsimd", "tensor", "scalar", "vector"):
            kb.dwait(e, "cv_st")
            kb.dwait(e, "d_cs0")
            kb.dwait(e, "d_cs1")
    pm = kb.phase_mark()
    for s in range(SEQN):
        if f"seq{s}" in stages:
            ln_wproj_n(s, pm)
            P0 = {"att_last": kb.mark("tensor"), "ms_abc": kb.mark("scalar"),
                  "mcc": kb.mark("vector"), "qm_last": kb.mark("tensor"),
                  "zadd15": kb.mark("vector")}
            P = attn_prechain(s, 0, P0)
            Q = {"pv2_last": kb.mark("tensor"), "mlnz_prev": kb.mark("scalar"),
                 "y1_prev": kb.mark("vector")}
            for h in range(H):
                Q = attn_head(s, h, P, Q)
                if h + 1 < H:
                    P = attn_prechain(s, h + 1, Q)
            proj_residual(s, Q["y2"])
            pm = kb.phase_mark()
    if "ffn" in stages:
        for s in range(SEQN):
            for q in range(T // TTOK):
                pm = ffn_tile(s, q, pm)
    kb.dwait("gpsimd", "d_out")
    kb.dwait("gpsimd", "d_tap")
    kb.dwait("sync", "d_out")

    with nc.Block() as block:
        @block.sync
        def _(eng):
            for fn in kb.ops["sync"]:
                fn(eng)

        @block.vector
        def _(eng):
            for fn in kb.ops["vector"]:
                fn(eng)

        @block.scalar
        def _(eng):
            for fn in kb.ops["scalar"]:
                fn(eng)

        @block.tensor
        def _(eng):
            for fn in kb.ops["tensor"]:
                fn(eng)

        @block.gpsimd
        def _(eng):
            for fn in kb.ops["gpsimd"]:
                fn(eng)

    return kb


def host_inputs(inputs):
    x = np.asarray(inputs["L0_attn_x"], dtype=np.float32)
    W = np.asarray(inputs["L0_attn_W"], dtype=np.float32)
    Wt = np.ascontiguousarray(W.T)
    WupT = np.ascontiguousarray(np.asarray(inputs["L0_ffn_W_up"], dtype=np.float32).T)
    WgT = np.ascontiguousarray(np.asarray(inputs["L0_ffn_W_gate"], dtype=np.float32).T)
    WdT = np.asarray(inputs["L0_ffn_W_down"], dtype=np.float32).T
    WdTb = np.ascontiguousarray(
        WdT.reshape(16, 4, 128, 4, 512).transpose(0, 3, 2, 1, 4)
    ).reshape(16, 4, 128, 2048)
    g1 = np.asarray(inputs["L0_attn_g1"], dtype=np.float32).reshape(1, 128)
    g2 = np.asarray(inputs["L0_attn_g2"], dtype=np.float32).reshape(1, 128)
    g3 = np.asarray(inputs["L0_attn_g3"], dtype=np.float32).reshape(1, 128)
    suug = np.asarray(inputs["L0_ffn_SU_ug"], dtype=np.float32).reshape(NCH, 128)
    sudp = np.asarray(inputs["L0_ffn_SU_dp"], dtype=np.float32).reshape(NACH, 128)
    wsug = np.asarray(inputs["L0_ffn_Wscale_ug"], dtype=np.float32)
    wsu = np.ascontiguousarray(wsug[:INTER].reshape(NACH, 128))
    wsg = np.ascontiguousarray(wsug[INTER:].reshape(NACH, 128))
    wsd = np.asarray(inputs["L0_ffn_Wscale_dp"], dtype=np.float32).reshape(NCH, 128)

    h128 = _hadamard(128).astype(ml_dtypes.bfloat16)
    idb = np.eye(128, dtype=ml_dtypes.bfloat16)
    idf = np.eye(128, dtype=np.float32)
    onecol = np.ones((128, 1), dtype=ml_dtypes.bfloat16)
    onerow = np.ones((1, 128), dtype=np.float32)
    onerowb = np.ones((1, 128), dtype=ml_dtypes.bfloat16)
    cst = np.zeros((128, 4), dtype=np.float32)
    cst[:, 0] = EPS_LN
    cst[:, 1] = EPS_RMS
    cst[:, 2] = 1e-37

    x_fm = np.ascontiguousarray(x.transpose(0, 2, 1))

    shared = dict(g1=g1, g2=g2, g3=g3,
                  SUug=suug, SUdp=sudp, WSu=wsu, WSg=wsg, WSd=wsd,
                  H128=h128, IDB=idb, IDF=idf, ONECOL=onecol, ONEROW=onerow,
                  ONEROWB=onerowb, CST=cst, Wt=Wt, WdT=WdTb, W=W,
                  WupT=WupT, WgT=WgT)
    in_maps = []
    for k in range(NCORES):
        m = {"x": x_fm[k * SEQN:(k + 1) * SEQN]}
        m.update(shared)
        in_maps.append(m)
    return in_maps


def kernel(**inputs):
    in_maps = host_inputs(inputs)
    kbp = build_program()
    res = run_bass_kernel_spmd(kbp.nc, in_maps, core_ids=list(range(NCORES)))
    parts = [r["out"] for r in res.results]
    out_fm = np.concatenate(parts, axis=0).reshape(B, C, T)
    return np.ascontiguousarray(out_fm.transpose(0, 2, 1))


# version 12
# speedup vs baseline: 1.2149x; 1.0459x over previous
import math
import numpy as np
import ml_dtypes
import concourse.bass as bass
import concourse.mybir as mybir
from concourse.bass_utils import run_bass_kernel_spmd

F32 = mybir.dt.float32
F32R = mybir.dt.float32r
BF16 = mybir.dt.bfloat16
AF = mybir.ActivationFunctionType
ALU = mybir.AluOpType
AX = mybir.AxisListType

B, T, C, H, DH = 16, 2048, 2048, 16, 128
INTER = 8192
NCORES = 8
SEQN = 2
NCH = C // 128
NACH = INTER // 128
EPS_RMS = float(np.finfo(np.float32).eps)
EPS_LN = 1e-5
FW1_SCALE = 1.0 / (64.0 * math.sqrt(2048.0))
FW2_SCALE = 1.0 / (64.0 * math.sqrt(8192.0))
INVSQ_DH = 1.0 / math.sqrt(DH)
TTOK = 512

ENGINES = ("sync", "vector", "scalar", "tensor", "gpsimd")
COMPUTE = ("vector", "scalar", "tensor", "gpsimd")
ALL_STAGES = ("startup", "seq0", "seq1", "ffn")


def _hadamard(n):
    h = np.array([[1.0]], dtype=np.float32)
    while h.shape[0] < n:
        h = np.block([[h, h], [h, -h]])
    return h


class KB:
    def __init__(self):
        self.nc = bass.Bass("TRN2", target_bir_lowering=False)
        self.ops = {e: [] for e in ENGINES}
        self.sems = {}
        self.cnt = {}
        self.ecnt = {e: 0 for e in COMPUTE}
        self.esem = {e: self.nc.alloc_semaphore(f"e_{e}") for e in COMPUTE}
        self.lastwait = {e: {} for e in ENGINES}

    def sem(self, name):
        if name not in self.sems:
            self.sems[name] = self.nc.alloc_semaphore(name)
            self.cnt[name] = 0
        return self.sems[name]

    def dval(self, name):
        self.sem(name)
        return self.cnt[name]

    def dwait(self, engine, name, value=None):
        s = self.sem(name)
        v = self.cnt[name] if value is None else value
        if v <= 0:
            return
        key = ("d", name)
        if self.lastwait[engine].get(key, -1) >= v:
            return
        self.lastwait[engine][key] = v
        self.ops[engine].append(lambda eng, s=s, v=v: eng.wait_ge(s, v))

    def dma(self, engine, out, in_, sem_name):
        s = self.sem(sem_name)
        self.cnt[sem_name] += 16
        self.ops[engine].append(
            lambda eng, o=out, i=in_, s=s: eng.dma_start(out=o, in_=i).then_inc(s, 16))
        return self.cnt[sem_name]

    def ins(self, engine, build):
        s = self.esem[engine]
        if engine in ("vector", "scalar", "gpsimd") and self.ecnt[engine] > 0:
            v = self.ecnt[engine]
            key = ("e", engine)
            if self.lastwait[engine].get(key, -1) < v:
                self.lastwait[engine][key] = v
                self.ops[engine].append(lambda eng, s=s, v=v: eng.wait_ge(s, v))
        self.ecnt[engine] += 1
        self.ops[engine].append(lambda eng, s=s, b=build: b(eng).then_inc(s, 1))
        return (engine, self.ecnt[engine])

    def mark(self, engine):
        return (engine, self.ecnt[engine])

    def phase_mark(self):
        return {e: self.ecnt[e] for e in COMPUTE}

    def phase_wait(self, engine, pm):
        if pm is None:
            return
        for pe in COMPUTE:
            self.after(engine, (pe, pm[pe]))

    def after(self, engine, *marks):
        for m in marks:
            if m is None:
                continue
            pe, v = m
            if v <= 0:
                continue
            if pe == engine:
                continue
            key = ("e", pe)
            if self.lastwait[engine].get(key, -1) >= v:
                continue
            self.lastwait[engine][key] = v
            self.ops[engine].append(
                lambda eng, s=self.esem[pe], v=v: eng.wait_ge(s, v))

    def fence(self, engine, *marks):
        for m in marks:
            if m is None:
                continue
            pe, v = m
            if v <= 0:
                continue
            key = ("e", pe)
            if self.lastwait[engine].get(key, -1) >= v:
                continue
            self.lastwait[engine][key] = v
            self.ops[engine].append(
                lambda eng, s=self.esem[pe], v=v: eng.wait_ge(s, v))


def build_program(stages=ALL_STAGES, taps=()):
    kb = KB()
    nc = kb.nc
    tapset = set(taps)

    x_in = nc.dram_tensor("x", [SEQN, C, T], F32, kind="ExternalInput").ap()
    wt_in = nc.dram_tensor("Wt", [C, C], F32, kind="ExternalInput").ap()
    w_in = nc.dram_tensor("W", [C, C], F32, kind="ExternalInput").ap()
    wupt_in = nc.dram_tensor("WupT", [C, INTER], F32, kind="ExternalInput").ap()
    wgt_in = nc.dram_tensor("WgT", [C, INTER], F32, kind="ExternalInput").ap()
    wdt_in = nc.dram_tensor("WdT", [16, 4, 128, 2048], F32, kind="ExternalInput").ap()
    g1_in = nc.dram_tensor("g1", [1, 128], F32, kind="ExternalInput").ap()
    g2_in = nc.dram_tensor("g2", [1, 128], F32, kind="ExternalInput").ap()
    g3_in = nc.dram_tensor("g3", [1, 128], F32, kind="ExternalInput").ap()
    suug_in = nc.dram_tensor("SUug", [NCH, 128], F32, kind="ExternalInput").ap()
    sudp_in = nc.dram_tensor("SUdp", [NACH, 128], F32, kind="ExternalInput").ap()
    wsu_in = nc.dram_tensor("WSu", [NACH, 128], F32, kind="ExternalInput").ap()
    wsg_in = nc.dram_tensor("WSg", [NACH, 128], F32, kind="ExternalInput").ap()
    wsd_in = nc.dram_tensor("WSd", [NCH, 128], F32, kind="ExternalInput").ap()
    h128_in = nc.dram_tensor("H128", [128, 128], BF16, kind="ExternalInput").ap()
    idb_in = nc.dram_tensor("IDB", [128, 128], BF16, kind="ExternalInput").ap()
    idf_in = nc.dram_tensor("IDF", [128, 128], F32, kind="ExternalInput").ap()
    onecol_in = nc.dram_tensor("ONECOL", [128, 1], BF16, kind="ExternalInput").ap()
    onerow_in = nc.dram_tensor("ONEROW", [1, 128], F32, kind="ExternalInput").ap()
    onerowb_in = nc.dram_tensor("ONEROWB", [1, 128], BF16, kind="ExternalInput").ap()
    cst_in = nc.dram_tensor("CST", [128, 4], F32, kind="ExternalInput").ap()

    out_t = nc.dram_tensor("out", [SEQN, NCH, 128, T], F32, kind="ExternalOutput").ap()

    wt_s = nc.dram_tensor("wt_s", [4, NCH, 128, 512], BF16, kind="Internal").ap()
    w_s = nc.dram_tensor("w_s", [4, NCH, 128, 512], BF16, kind="Internal").ap()
    up_s = nc.dram_tensor("up_s", [NACH, 128, NCH * 128], BF16, kind="Internal").ap()
    gt_s = nc.dram_tensor("gt_s", [NACH, 128, NCH * 128], BF16, kind="Internal").ap()
    dn_s = nc.dram_tensor("dn_s", [4, 128, NACH * 512], BF16, kind="Internal").ap()
    y_s = nc.dram_tensor("y_s", [SEQN, H, 128, T], BF16, kind="Internal").ap()
    x1_s = nc.dram_tensor("x1_s", [SEQN, NCH, 128, T], F32, kind="Internal").ap()
    bnc_s = nc.dram_tensor("bnc_s", [T], F32, kind="Internal").ap()

    A = nc.alloc_sbuf_tensor
    AR_A = A("AR_A", [128, 16384], F32)[:]
    AR_B = A("AR_B", [128, 16384], F32)[:]
    AR_W = A("AR_W", [128, 6144], F32)[:]

    xbf_sb = [A(f"xbf{i}", [128, T], BF16)[:] for i in range(2)]
    sq_sb = [A(f"sq{i}", [128, T], BF16)[:] for i in range(2)]
    tmp32_sb = A("tmp32", [128, T], F32)[:]
    abc_sb = A("abc", [128, T], F32)[:]
    bbc_sb = A("bbc", [128, T], F32)[:]
    xch_sb = [AR_W[:, 0:2048], AR_W[:, 2048:4096]]
    rbc_sb = bbc_sb
    sgbuf_sb = [sq_sb[0].bitcast(F32)[:, 0:512], sq_sb[1].bitcast(F32)[:, 0:512]]

    z2_sb = A("z2", [128, 32], F32)[:]
    mh_sb = A("mh", [128, 16], F32)[:]
    nmh_sb = A("nmh", [128, 16], F32)[:]
    z_sb = A("z", [128, 16], F32)[:]
    lnz_sb = A("lnz", [128, 16], F32)[:]
    cc_sb = A("cc", [128, 16], F32)[:]
    u_sb = A("u", [128, 16], F32)[:]
    iz_sb = A("iz", [128, 16], F32)[:]
    tq_sb = A("tq", [128, 16], F32)[:]
    kc1_sb = A("kc1", [128, 1], F32)[:]
    kb16_sb = A("kb16", [128, 1], BF16)[:]
    kcol_sb = A("kcol", [128, 1], F32)[:]
    nkcol_sb = A("nkcol", [128, 1], F32)[:]
    ktp_sb = A("ktp", [1, 128], F32)[:]
    kscal_sb = A("kscal", [1, 1], F32)[:]
    crow_sb = A("crow", [16, 128], F32)[:]
    cst_sb = A("cst", [128, 4], F32)[:]
    idb_sb = A("idb", [128, 128], BF16)[:]
    idf_sb = A("idf", [128, 128], F32)[:]
    h128_sb = A("h128", [128, 128], BF16)[:]
    onecol_sb = A("onecol", [128, 1], BF16)[:]
    onerow_sb = A("onerow", [1, 128], F32)[:]
    onerowb_sb = A("onerowb", [1, 128], BF16)[:]
    g1_sb = A("g1s", [1, 128], F32)[:]
    g2_sb = A("g2s", [1, 128], F32)[:]
    g3_sb = A("g3s", [1, 128], F32)[:]
    ggcol_sb = A("ggcol", [128, 1], F32)[:]
    g3col_sb = A("g3col", [128, 1], F32)[:]
    suug_sb = A("suug", [128, NCH], F32)[:]
    sudp_sb = A("sudp", [128, NACH], F32)[:]
    wsu_sb = A("wsu", [128, NACH], F32)[:]
    wsg_sb = A("wsg", [128, NACH], F32)[:]
    wsd_sb = A("wsd", [128, NCH], F32)[:]
    sm16_sb = A("sm16", [NCH, 128], F32)[:]
    sm64_sb = A("sm64", [NACH, 128], F32)[:]

    rowm_sb = A("rowm", [1, T], F32)[:]
    trow_sb = abc_sb[0:1, :]
    t2row_sb = bbc_sb[0:1, :]
    m2row_sb = rowm_sb
    arow_sb = tmp32_sb[0:1, :]
    brow_sb = AR_W[0:1, 4096:6144]
    crow1_sb = abc_sb[0:1, :]
    vrow_sb = tmp32_sb[0:1, :]

    ebf = AR_A.bitcast(BF16)
    E_strip = [ebf[:, j * T:(j + 1) * T] for j in range(NCH)]
    xn_ch = E_strip
    g_store = [ebf[:, a * 512:(a + 1) * 512] for a in range(NACH)]
    cvt32 = AR_A[:, 0:8192]
    cvt16 = ebf[:, 16384:16384 + 8192]

    nbf = AR_B.bitcast(BF16)
    n_ch = [nbf[:, h * T:(h + 1) * T] for h in range(H)]
    dnslotB = [nbf[:, i * 512:(i + 1) * 512] for i in range(NACH)]
    x1t = [AR_B[:, ch * 512:(ch + 1) * 512] for ch in range(NCH)]
    xf_ch = [nbf[:, 16384 + ch * 512:16384 + (ch + 1) * 512] for ch in range(NCH)]
    fwpp = [nbf[:, 24576 + ch * 512:24576 + (ch + 1) * 512] for ch in range(NCH)]

    wbf = AR_W.bitcast(BF16)
    wblk = [wbf[:, c * 512:(c + 1) * 512] for c in range(16)]
    ybuf_sb = [wbf[:, 8192:10240], wbf[:, 10240:12288]]
    vtok_sb = wbf[:, 0:2048]
    vtokA_sb = wbf[:, 2048:4096]
    vtokB_sb = wbf[:, 4096:6144]
    ysb_sb = wbf[:, 6144:8192]
    ng_sb = wbf[:, 8192:10240]
    ng3_sb = wbf[:, 10240:12288]
    dncv = [wbf[:, 4096 + k * 512:4096 + (k + 1) * 512] for k in range(4)]
    dncv32 = [AR_W[:, k * 512:(k + 1) * 512] for k in range(4)]
    ugw = [[wbf[:, i * 4096 + c * 128:i * 4096 + (c + 1) * 128] for c in range(32)]
           for i in range(2)]
    dnw = [[tmp32_sb.bitcast(BF16)[:, i * 2048 + a * 512:i * 2048 + (a + 1) * 512]
            for a in range(4)] for i in range(2)]

    psATT = nc.alloc_psum_tensor("psATT", [128, 2048], F32)
    psY = nc.alloc_psum_tensor("psY", [128, 2048], F32)
    pA = psATT[:]
    pY = psY[:]

    assert nc.sbuf_bytes_remaining >= 0

    tap_t = {}
    for name in tapset:
        tap_t[name] = nc.dram_tensor(f"tap_{name}", [128, T], F32, kind="ExternalOutput").ap()

    def tapbf(name, src_bf, width=T, prod=None):
        if name in tapset:
            kb.dwait("vector", "d_tap")
            kb.after("vector", prod)
            m = kb.ins("vector", lambda e, s=src_bf, w=width: e.tensor_copy(bbc_sb[:, 0:w], s))
            kb.after("gpsimd", m)
            kb.after("scalar", m)
            kb.after("tensor", m)
            kb.dma("gpsimd", tap_t[name], bbc_sb, "d_tap")
            for e in ("vector", "scalar", "sync"):
                kb.dwait(e, "d_tap")

    def startup_smalls():
        for ap_in, ap_sb in ((g1_in, g1_sb), (g2_in, g2_sb), (g3_in, g3_sb),
                             (idb_in, idb_sb), (idf_in, idf_sb), (h128_in, h128_sb),
                             (onecol_in, onecol_sb), (onerow_in, onerow_sb),
                             (cst_in, cst_sb), (onerowb_in, onerowb_sb),
                             (suug_in, sm16_sb)):
            kb.dma("sync", ap_sb, ap_in, "d_ld")
        kb.dwait("tensor", "d_ld")
        kb.ins("tensor", lambda e: e.transpose(pA[:, 0:1], g1_sb, idf_sb[0:1, 0:1]))
        kb.ins("tensor", lambda e: e.transpose(pA[:, 1:2], g2_sb, idf_sb[0:1, 0:1]))
        kb.ins("tensor", lambda e: e.transpose(pA[:, 2:3], g3_sb, idf_sb[0:1, 0:1]))
        mt = kb.ins("tensor", lambda e: e.transpose(pA[:, 512:528], sm16_sb,
                                                    idf_sb[0:16, 0:16]))
        kb.after("vector", mt)
        kb.ins("vector", lambda e: e.tensor_copy(suug_sb, pA[:, 512:528]))
        kb.ins("vector", lambda e: e.tensor_copy(ggcol_sb, pA[:, 0:1]))
        kb.ins("vector", lambda e: e.scalar_tensor_tensor(
            ggcol_sb, ggcol_sb, INVSQ_DH, pA[:, 1:2], op0=ALU.mult, op1=ALU.mult))
        mv = kb.ins("vector", lambda e: e.tensor_copy(g3col_sb, pA[:, 2:3]))

        state = {"mv": mv}

        def tp_then(dst_builder, src_dram, width):
            sm = sm64_sb if width == 64 else sm16_sb
            psv = pA[:, 1024:1024 + width]
            kb.after("sync", state["mv"])
            dm = kb.dma("sync", sm, src_dram, "d_ld")
            kb.dwait("tensor", "d_ld", dm)
            kb.after("tensor", state["mv"])
            mt = kb.ins("tensor", lambda e, sm=sm, p=psv: e.transpose(
                p, sm, idf_sb[0:sm.shape[0], 0:sm.shape[0]]))
            kb.after("vector", mt)
            state["mv"] = kb.ins("vector", dst_builder)

        tp_then(lambda e: e.tensor_copy(sudp_sb, pA[:, 1024:1088]), sudp_in, 64)
        tp_then(lambda e: e.scalar_tensor_tensor(
            wsu_sb, pA[:, 1024:1088], 64.0, sudp_sb, op0=ALU.mult, op1=ALU.mult), wsu_in, 64)
        tp_then(lambda e: e.tensor_scalar_mul(wsg_sb, pA[:, 1024:1088], 64.0), wsg_in, 64)
        tp_then(lambda e: e.tensor_scalar_mul(wsd_sb, pA[:, 1024:1040], 64.0 * FW2_SCALE), wsd_in, 16)
        kb.after("tensor", state["mv"])
        kb.after("scalar", state["mv"])

    cvt32p = [AR_A[:, 0:4096], AR_A[:, 4096:8192]]
    cvt16p = [ebf[:, 16384:20480], ebf[:, 20480:24576]]
    cv_state = {"mv": [None, None], "st": [0, 0]}

    def _cv_piece(par, load_src, width, store_fn):
        b32 = cvt32p[par][:, 0:width]
        b16 = cvt16p[par][:, 0:width]
        lq = "sync" if par == 0 else "gpsimd"
        kb.dwait(lq, f"d_cs{par}", cv_state["st"][par])
        dm = kb.dma(lq, b32, load_src, f"d_cv{par}")
        kb.dwait("vector", f"d_cv{par}", dm)
        mv = kb.ins("vector", lambda e, o=b16, i=b32: e.tensor_copy(o, i))
        kb.after("scalar", mv)
        store_fn(b16, par)
        cv_state["st"][par] = kb.dval(f"d_cs{par}")

    def convert_square(src, dst):
        for ch in range(NCH):
            def store(b16, par, ch=ch, dst=dst):
                kb.dma("scalar", dst[:, ch].rearrange("g p f -> p g f"), b16, f"d_cs{par}")
            _cv_piece(ch % 2, src[ch * 128:(ch + 1) * 128, :], 2048, store)

    def convert_upgt(src, dst):
        for ch in range(NCH):
            for h in range(2):
                def store(b16, par, ch=ch, h=h, dst=dst):
                    kb.dma("scalar",
                           dst[h * 32:(h + 1) * 32, :, ch * 128:(ch + 1) * 128]
                           .rearrange("a p j -> p a j"),
                           b16.rearrange("p (a j) -> p a j", a=32), f"d_cs{par}")
                _cv_piece((ch * 2 + h) % 2,
                          src[ch * 128:(ch + 1) * 128, h * 4096:(h + 1) * 4096], 4096, store)

    def bf_stage(srcR, dstR, bit, width):
        blk = bit * width
        s4 = srcR.rearrange("p (g two f) -> p g two f", two=2, f=blk)
        d4 = dstR.rearrange("p (g two f) -> p g two f", two=2, f=blk)
        kb.ins("vector", lambda e: e.tensor_tensor(
            d4[:, :, 0:1, :], s4[:, :, 0:1, :], s4[:, :, 1:2, :], op=ALU.add))
        return kb.ins("vector", lambda e: e.tensor_tensor(
            d4[:, :, 1:2, :], s4[:, :, 0:1, :], s4[:, :, 1:2, :], op=ALU.subtract))

    def transform_dn():
        slotA = [ebf[:, i * 512:(i + 1) * 512] for i in range(NACH)]
        slotB = dnslotB
        for e in ("scalar", "vector", "tensor", "sync"):
            kb.dwait(e, "cv_st")
            kb.dwait(e, "d_cs0")
            kb.dwait(e, "d_cs1")
        b32g = [AR_W[:, 0:2048], AR_W[:, 2048:4096]]
        b16g = [wbf[:, 8192:10240], wbf[:, 10240:12288]]
        mm_marks = [None, None]
        cp_marks = [None, None]
        v_marks = [None, None]
        for cq in range(4):
            for g in range(NACH // 4):
                par = g % 2
                b32, b16 = b32g[par], b16g[par]
                ps = pA if par == 0 else pY
                kb.after("sync", v_marks[par])
                dm = kb.dma("sync", b32, wdt_in[g, cq], f"d_cv{par}")
                kb.dwait("vector", f"d_cv{par}", dm)
                kb.after("vector", mm_marks[par])
                v_marks[par] = kb.ins("vector", lambda e, o=b16, i=b32: e.tensor_copy(o, i))
                kb.after("tensor", v_marks[par], cp_marks[par])
                mm = None
                for k in range(4):
                    mm = kb.ins("tensor", lambda e, k=k, ps=ps, b16=b16: e.matmul(
                        ps[:, k * 512:(k + 1) * 512], h128_sb,
                        b16[:, k * 512:(k + 1) * 512], start=True, stop=True))
                mm_marks[par] = mm
                kb.after("scalar", mm)
                cp = None
                for k in range(4):
                    cp = kb.ins("scalar", lambda e, k=k, g=g, ps=ps: e.copy(
                        slotA[g * 4 + k], ps[:, k * 512:(k + 1) * 512]))
                cp_marks[par] = cp
            kb.after("vector", *cp_marks)
            srcR, dstR = ebf[:, 0:NACH * 512], nbf[:, 0:NACH * 512]
            stage_mark = None
            for bit in (1, 2, 4, 8, 16, 32):
                stage_mark = bf_stage(srcR, dstR, bit, 512)
                srcR, dstR = dstR, srcR
            kb.after("scalar", stage_mark)
            kb.dma("scalar", dn_s[cq], srcR, "cv_st")
            kb.dwait("sync", "cv_st")
            kb.dwait("vector", "cv_st")
            v_marks = [None] * 4

    def ln_pass(x_src, W, resident=None):
        nsl = W // 512
        if resident is not None:
            for ch in range(NCH):
                kb.dma("sync", resident[ch], x_src(ch), "d_xr")
            kb.dwait("vector", "d_xr")
        pe_marks = [None, None]
        v_marks = [None, None]
        mm = None
        for ch in range(NCH):
            if resident is None:
                xb = xch_sb[ch % 2][:, 0:W]
                kb.after("sync", v_marks[ch % 2])
                dm = kb.dma("sync", xb, x_src(ch), f"d_x{ch % 2}")
                kb.dwait("vector", f"d_x{ch % 2}", dm)
            else:
                xb = resident[ch]
            kb.after("vector", pe_marks[ch % 2])
            v_marks[ch % 2] = kb.ins("vector", lambda e, x=xb, ch=ch, W=W: e.tensor_copy(
                xbf_sb[ch % 2][:, 0:W], x))
            kb.fence("vector", v_marks[ch % 2])
            mv2 = kb.ins("vector", lambda e, ch=ch, W=W: e.tensor_tensor(
                sq_sb[ch % 2][:, 0:W], xbf_sb[ch % 2][:, 0:W], xbf_sb[ch % 2][:, 0:W],
                op=ALU.mult))
            kb.after("tensor", mv2)
            st, sp = (ch == 0), (ch == NCH - 1)
            for sl in range(nsl):
                kb.ins("tensor", lambda e, sl=sl, st=st, sp=sp, ch=ch: e.matmul(
                    pY[0:1, sl * 512:(sl + 1) * 512], onecol_sb,
                    xbf_sb[ch % 2][:, sl * 512:(sl + 1) * 512], start=st, stop=sp))
            for sl in range(nsl):
                mm = kb.ins("tensor", lambda e, sl=sl, st=st, sp=sp, ch=ch: e.matmul(
                    pY[32:33, sl * 512:(sl + 1) * 512], onecol_sb,
                    sq_sb[ch % 2][:, sl * 512:(sl + 1) * 512], start=st, stop=sp))
            pe_marks[ch % 2] = mm
        kb.after("vector", mm)
        kb.ins("vector", lambda e, W=W: e.tensor_scalar_mul(trow_sb[:, 0:W], pY[0:1, 0:W],
                                                            1.0 / 2048.0))
        kb.ins("vector", lambda e, W=W: e.tensor_scalar_mul(t2row_sb[:, 0:W],
                                                            pY[32:33, 0:W], 1.0 / 2048.0))
        mva = kb.ins("vector", lambda e, W=W: e.tensor_tensor(
            m2row_sb[:, 0:W], trow_sb[:, 0:W], trow_sb[:, 0:W], op=ALU.mult))
        kb.fence("vector", mva)
        mv = kb.ins("vector", lambda e, W=W: e.tensor_tensor(
            t2row_sb[:, 0:W], t2row_sb[:, 0:W], m2row_sb[:, 0:W], op=ALU.subtract))
        kb.after("scalar", mv)
        ms = kb.ins("scalar", lambda e, W=W: e.activation(
            m2row_sb[:, 0:W], t2row_sb[:, 0:W], AF.Ln, bias=cst_sb[0:1, 0:1], scale=1.0))
        kb.fence("scalar", ms)
        ms = kb.ins("scalar", lambda e, W=W: e.activation(
            arow_sb[:, 0:W], m2row_sb[:, 0:W], AF.Exp, bias=0.0, scale=-0.5))
        kb.after("vector", ms)
        mv = kb.ins("vector", lambda e, W=W: e.tensor_tensor(
            brow_sb[:, 0:W], trow_sb[:, 0:W], arow_sb[:, 0:W],
            op=ALU.mult))
        kb.after("tensor", mv)
        for sl in range(nsl):
            kb.ins("tensor", lambda e, sl=sl: e.matmul(
                pY[:, sl * 512:(sl + 1) * 512], onerow_sb, arow_sb[:, sl * 512:(sl + 1) * 512],
                start=True, stop=True))
        for sl in range(nsl):
            mm = kb.ins("tensor", lambda e, sl=sl: e.matmul(
                pA[:, sl * 512:(sl + 1) * 512], onerow_sb, brow_sb[:, sl * 512:(sl + 1) * 512],
                start=True, stop=True))
        kb.after("scalar", mm)
        kb.ins("scalar", lambda e, W=W: e.copy(abc_sb[:, 0:W], pY[:, 0:W]))
        ms = kb.ins("scalar", lambda e, W=W: e.copy(bbc_sb[:, 0:W], pA[:, 0:W]))
        return ms, v_marks

    def ln_wproj_n(s, pm):
        kb.phase_wait("sync", pm)
        kb.phase_wait("vector", pm)
        kb.phase_wait("scalar", pm)
        kb.phase_wait("tensor", pm)
        mabc, v_marks = ln_pass(lambda ch: x_in[s, ch * 128:(ch + 1) * 128, :], T)
        kb.after("vector", mabc)
        mlast = None
        for ch in range(NCH):
            xb = xch_sb[ch % 2]
            kb.after("sync", v_marks[ch % 2])
            dm = kb.dma("sync", xb, x_in[s, ch * 128:(ch + 1) * 128, :], f"d_x{ch % 2}")
            kb.dwait("vector", f"d_x{ch % 2}", dm)
            v_marks[ch % 2] = kb.ins("vector", lambda e, x=xb: e.tensor_tensor(
                tmp32_sb, x, abc_sb, op=ALU.mult))
            kb.fence("vector", v_marks[ch % 2])
            mlast = kb.ins("vector", lambda e, ch=ch: e.tensor_tensor(
                xn_ch[ch], tmp32_sb, bbc_sb, op=ALU.subtract))
        if s == 0:
            tapbf("xn0", xn_ch[0], prod=mlast)

        kb.after("tensor", mlast)
        m_nmul = None
        m_ncopy = None
        m_rbc_prev = None
        for hch in range(H):
            g = hch // 4
            if hch % 4 == 0:
                kb.after("sync", kb.mark("tensor"), mlast)
                dwb = 0
                for ch in range(NCH):
                    dwb = kb.dma("sync", wblk[ch], wt_s[g, ch], "d_wb")
                kb.dwait("tensor", "d_wb", dwb)
            kb.after("tensor", m_ncopy)
            mm = None
            for sl in range(4):
                for ch in range(NCH):
                    mm = kb.ins("tensor", lambda e, sl=sl, ch=ch, hch=hch: e.matmul(
                        pA[:, sl * 512:(sl + 1) * 512],
                        wblk[ch][:, (hch % 4) * 128:(hch % 4 + 1) * 128],
                        xn_ch[ch][:, sl * 512:(sl + 1) * 512],
                        start=(ch == 0), stop=(ch == NCH - 1)))
            kb.after("scalar", mm)
            m_ncopy = kb.ins("scalar", lambda e, hch=hch: e.copy(n_ch[hch], pA))
            kb.after("vector", m_ncopy)
            mv = kb.ins("vector", lambda e, hch=hch: e.tensor_tensor(
                sq_sb[hch % 2], n_ch[hch], n_ch[hch], op=ALU.mult))
            kb.after("tensor", mv, m_rbc_prev)
            qm = None
            for sl in range(4):
                qm = kb.ins("tensor", lambda e, sl=sl, hch=hch: e.matmul(
                    pY[0:1, sl * 512:(sl + 1) * 512], onecol_sb,
                    sq_sb[hch % 2][:, sl * 512:(sl + 1) * 512], start=True, stop=True))
            kb.after("scalar", qm)
            ms = kb.ins("scalar", lambda e: e.activation(
                m2row_sb, pY[0:1, :], AF.Ln, bias=cst_sb[0:1, 1:2], scale=1.0 / 128.0))
            kb.fence("scalar", ms)
            ms = kb.ins("scalar", lambda e: e.activation(
                arow_sb, m2row_sb, AF.Exp, bias=0.0, scale=-0.5))
            kb.after("tensor", ms)
            rm = None
            for sl in range(4):
                rm = kb.ins("tensor", lambda e, sl=sl: e.matmul(
                    pY[:, sl * 512:(sl + 1) * 512], onerow_sb,
                    arow_sb[:, sl * 512:(sl + 1) * 512], start=True, stop=True))
            kb.after("scalar", rm)
            m_rbc = kb.ins("scalar", lambda e: e.copy(rbc_sb, pY))
            m_rbc_prev = m_rbc
            kb.after("vector", m_rbc)
            m_nmul = kb.ins("vector", lambda e, hch=hch: e.tensor_tensor(
                n_ch[hch], n_ch[hch], rbc_sb, op=ALU.mult))
        if s == 0:
            tapbf("n0", n_ch[0], prod=m_nmul)
        return m_nmul

    def attn_prechain(s, h, P):
        nh = n_ch[h]
        kb.after("scalar", P.get("att_last"))
        ms_ng = kb.ins("scalar", lambda e: e.activation(
            ng_sb, nh, AF.Copy, bias=0.0, scale=ggcol_sb))
        ms_ng3 = kb.ins("scalar", lambda e: e.activation(
            ng3_sb, nh, AF.Copy, bias=0.0, scale=g3col_sb))
        kb.after("vector", ms_ng, P.get("qm_last"))
        msqg = kb.ins("vector", lambda e: e.tensor_tensor(
            sq_sb[0], ng_sb, ng_sb, op=ALU.mult))
        kb.after("scalar", P.get("zadd15"))
        cp_m = [None, None]
        for j in range(NCH):
            pslot = pY[:, 1024 + (j % 2) * 512:1024 + (j % 2) * 512 + 64].bitcast(BF16)
            kb.after("tensor", cp_m[j % 2],
                     ms_ng3 if j < 2 else None,
                     P.get("ms_abc") if j < 2 else None)
            tp = kb.ins("tensor", lambda e, j=j, p=pslot: e.transpose(
                p, ng3_sb[:, j * 128:(j + 1) * 128], idb_sb))
            kb.after("scalar", tp)
            cp_m[j % 2] = kb.ins("scalar", lambda e, j=j, p=pslot: e.copy(
                vtok_sb[:, j * 128:(j + 1) * 128], p))
        kb.after("tensor", msqg, P.get("ms_abc"))
        qm = None
        for j in range(NCH):
            qm = kb.ins("tensor", lambda e, j=j: e.matmul(
                pY[:, j:j + 1], sq_sb[0][:, j * 128:(j + 1) * 128], onecol_sb,
                start=True, stop=True))
        kb.after("scalar", qm, P.get("mcc"))
        ms = kb.ins("scalar", lambda e: e.activation(tq_sb, pY[:, 0:16], AF.Ln,
                                                     bias=cst_sb[:, 2:3], scale=128.0))
        kb.fence("scalar", ms)
        ms_mh = kb.ins("scalar", lambda e: e.activation(mh_sb, tq_sb, AF.Exp, bias=0.0,
                                                        scale=0.5))
        kb.after("vector", ms_mh)
        mv_nmh = kb.ins("vector", lambda e: e.tensor_scalar_mul(nmh_sb, mh_sb, -1.0))
        mred = kb.ins("vector", lambda e: e.tensor_reduce(kc1_sb, mh_sb, axis=AX.X,
                                                          op=ALU.max))
        kb.fence("vector", mred)
        mv = kb.ins("vector", lambda e: e.tensor_copy(kb16_sb, kc1_sb))
        kb.after("tensor", mv)
        mt = kb.ins("tensor", lambda e: e.transpose(pY[0:1, 16:80].bitcast(BF16), kb16_sb,
                                                    idb_sb))
        kb.after("scalar", mt)
        ms = kb.ins("scalar", lambda e: e.copy(ktp_sb, pY[0:1, 16:80].bitcast(BF16)))
        kb.after("vector", ms)
        mv = kb.ins("vector", lambda e: e.tensor_reduce(kscal_sb, ktp_sb,
                                                        axis=AX.X, op=ALU.max))
        kb.after("tensor", mv)
        mt = kb.ins("tensor", lambda e: e.matmul(pY[:, 96:97], onerow_sb, kscal_sb,
                                                 start=True, stop=True))
        kb.after("scalar", mt)
        ms_kcol = kb.ins("scalar", lambda e: e.copy(kcol_sb, pY[:, 96:97]))
        kb.after("vector", ms_kcol)
        mv_nk = kb.ins("vector", lambda e: e.tensor_scalar_mul(nkcol_sb, kcol_sb, -1.0))
        kb.after("scalar", mv_nk)
        mu = kb.ins("scalar", lambda e: e.activation(u_sb, mh_sb, AF.Exp, bias=nkcol_sb,
                                                     scale=1.0))
        return {"mu": mu, "nmh": mv_nmh, "ms_kcol": ms_kcol, "qm_last": qm,
                "cp_last": cp_m}

    def attn_head(s, h, P, Q):
        nh = n_ch[h]
        kb.after("vector", P["mu"], Q.get("pv2_last"), *[m for m in P["cp_last"] if m])
        mvB = None
        for j in range(NCH):
            mvB = kb.ins("vector", lambda e, j=j: e.tensor_scalar_mul(
                vtokB_sb[:, j * 128:(j + 1) * 128], vtok_sb[:, j * 128:(j + 1) * 128],
                u_sb[:, j:j + 1]))
        kb.after("tensor", Q.get("y2"), P["ms_kcol"], P["mu"])
        kb.after("scalar", Q.get("pv2_last"), P["nmh"], Q.get("zadd15"))
        expm = {}
        vAm = {}
        att_last = None
        pv1_last = None

        def emit_att(j):
            nonlocal att_last
            for half in range(2):
                kb.after("tensor", expm.get((j - 1, half)))
                for sl in (0, 1):
                    c = half * 2 + sl
                    att_last = kb.ins("tensor", lambda e, j=j, c=c: e.matmul(
                        pA[:, c * 512:(c + 1) * 512], ng_sb[:, j * 128:(j + 1) * 128],
                        nh[:, c * 512:(c + 1) * 512], start=True, stop=True))
                kb.after("scalar", att_last)
                expm[(j, half)] = kb.ins("scalar", lambda e, j=j, half=half: e.activation(
                    E_strip[j][:, half * 1024:(half + 1) * 1024],
                    pA[:, half * 1024:(half + 1) * 1024], AF.Exp,
                    bias=nmh_sb[:, j:j + 1], scale=1.0,
                    accum_out=z2_sb[:, 2 * j + half:2 * j + half + 1]))

        def emit_zA(j):
            kb.after("vector", expm[(j, 1)],
                     Q.get("mlnz_prev") if j == 0 else None)
            kb.ins("vector", lambda e, j=j: e.tensor_tensor(
                z_sb[:, j:j + 1], z2_sb[:, 2 * j:2 * j + 1],
                z2_sb[:, 2 * j + 1:2 * j + 2], op=ALU.add))
            kb.ins("vector", lambda e, j=j: e.reciprocal(iz_sb[:, j:j + 1],
                                                         z_sb[:, j:j + 1]))
            vAm[j] = kb.ins("vector", lambda e, j=j: e.tensor_scalar_mul(
                vtokA_sb[:, j * 128:(j + 1) * 128], vtok_sb[:, j * 128:(j + 1) * 128],
                iz_sb[:, j:j + 1]))

        def emit_pv(j, ps, off):
            nonlocal pv1_last
            st, sp = (j == 0), (j == NCH - 1)
            for c, (vt, half) in enumerate(((vtokA_sb, 0 + off), (vtokA_sb, 1 + off),
                                            (vtokB_sb, 0 + off), (vtokB_sb, 1 + off))):
                pv1_last = kb.ins("tensor", lambda e, j=j, c=c, vt=vt, half=half,
                                  st=st, sp=sp, ps=ps: e.matmul(
                    ps[:, c * 512:(c + 1) * 512], vt[:, j * 128:(j + 1) * 128],
                    E_strip[j][:, half * 512:(half + 1) * 512], start=st, stop=sp))

        emit_att(0)
        for j in range(NCH):
            if j + 1 < NCH:
                emit_att(j + 1)
            emit_zA(j)
            kb.after("tensor", vAm[j], mvB if j == 0 else None)
            emit_pv(j, pY, 0)
        pv1_last_m = pv1_last

        kb.after("scalar", vAm[NCH - 1])
        mlnz = kb.ins("scalar", lambda e: e.activation(lnz_sb, z_sb, AF.Ln, bias=0.0,
                                                       scale=1.0))
        kb.after("vector", mlnz)
        mcc = kb.ins("vector", lambda e: e.tensor_tensor(cc_sb, mh_sb, lnz_sb,
                                                         op=ALU.add))
        kb.after("scalar", pv1_last_m, Q.get("y1_prev"))
        cpA = kb.ins("scalar", lambda e: e.copy(tmp32_sb[:, 0:1024], pY[:, 0:1024]))
        cpB = kb.ins("scalar", lambda e: e.copy(tmp32_sb[:, 1024:2048], pY[:, 1024:2048]))

        kb.after("tensor", expm[(NCH - 1, 1)])
        for j in range(4):
            emit_pv(j, pA, 2)
        pv2_j3 = pv1_last
        kb.after("tensor", mcc, cpB)
        mt = kb.ins("tensor", lambda e: e.transpose(pY[0:16, 0:128], cc_sb, idf_sb))
        kb.after("scalar", mt)
        kb.dwait("scalar", "d_bn")
        ms = kb.ins("scalar", lambda e: e.copy(crow_sb, pY[0:16, 0:128]))
        kb.after("gpsimd", ms, Q.get("y2"))
        db = kb.dma("gpsimd", crow1_sb.rearrange("a (b c) -> a b c", b=16), crow_sb,
                    "d_bn")
        kb.dwait("scalar", "d_bn", db)
        msv = kb.ins("scalar", lambda e: e.activation(xbf_sb[1][0:1, :], crow1_sb,
                                                      AF.Exp, bias=kcol_sb[0:1, 0:1],
                                                      scale=-1.0))
        for j in range(4, 8):
            emit_pv(j, pA, 2)
        kb.after("tensor", msv)
        vm = None
        for sl in range(4):
            vm = kb.ins("tensor", lambda e, sl=sl: e.matmul(
                pY[:, sl * 512:(sl + 1) * 512], onerowb_sb,
                xbf_sb[1][0:1, sl * 512:(sl + 1) * 512], start=True, stop=True))
        kb.after("scalar", vm, Q.get("y2"))
        ms_abc = kb.ins("scalar", lambda e: e.copy(abc_sb, pY))
        for j in range(8, NCH):
            emit_pv(j, pA, 2)
        pv2_last = pv1_last
        kb.dwait("vector", "d_y")
        kb.after("vector", ms_abc)
        mv = kb.ins("vector", lambda e: e.tensor_tensor(
            bbc_sb[:, 0:1024], tmp32_sb[:, 1024:2048], abc_sb[:, 0:1024], op=ALU.mult))
        kb.fence("vector", mv)
        y1 = kb.ins("vector", lambda e: e.tensor_tensor(
            ysb_sb[:, 0:1024], bbc_sb[:, 0:1024], tmp32_sb[:, 0:1024], op=ALU.add))
        kb.after("vector", pv2_last)
        mv = kb.ins("vector", lambda e: e.tensor_tensor(
            bbc_sb[:, 1024:2048], pA[:, 1024:2048], abc_sb[:, 1024:2048], op=ALU.mult))
        kb.fence("vector", mv)
        y2 = kb.ins("vector", lambda e: e.tensor_tensor(
            ysb_sb[:, 1024:2048], bbc_sb[:, 1024:2048], pA[:, 0:1024], op=ALU.add))
        kb.after("gpsimd", y2)
        kb.dma("gpsimd", y_s[s, h], ysb_sb, "d_y")
        return {"y2": y2, "pv2_last": pv2_last, "att_last": att_last,
                "mlnz_prev": mlnz, "y1_prev": y1, "ms_abc": ms_abc,
                "mcc": mcc, "zadd15": vAm[NCH - 1], "qm_last": P["qm_last"]}

    def proj_residual(s, m_attn):
        kb.dwait("sync", "d_y")
        kb.after("sync", m_attn)
        m_ev = [None, None]
        m_mm = None
        for cp in range(8):
            g = cp // 2
            if cp % 2 == 0:
                if g >= 1:
                    kb.after("sync", m_mm)
                dwb = 0
                for m in range(H):
                    dwb = kb.dma("sync", wblk[m], w_s[g, m], "d_wb")
                kb.dwait("tensor", "d_wb", dwb)
            kb.after("tensor", m_ev[0], m_ev[1])
            pe_y = [None, None]
            for m in range(H):
                yb = ybuf_sb[m % 2]
                q = "sync" if m % 2 == 0 else "scalar"
                if pe_y[m % 2] is not None:
                    kb.after(q, pe_y[m % 2])
                elif cp > 0:
                    kb.after(q, m_mm)
                dyb = kb.dma(q, yb, y_s[s, m], f"d_yb{m % 2}")
                kb.dwait("tensor", f"d_yb{m % 2}", dyb)
                for cc in range(2):
                    for sl in range(4):
                        m_mm = kb.ins("tensor", lambda e, m=m, cc=cc, sl=sl, cp=cp, yb=yb:
                                      e.matmul(
                                          (pA if cc == 0 else pY)[:, sl * 512:(sl + 1) * 512],
                                          wblk[m][:, ((cp % 2) * 2 + cc) * 128:
                                                  ((cp % 2) * 2 + cc + 1) * 128],
                                          yb[:, sl * 512:(sl + 1) * 512],
                                          start=(m == 0), stop=(m == H - 1)))
                pe_y[m % 2] = m_mm
            for cc in range(2):
                chg = cp * 2 + cc
                xdst = tmp32_sb if cc == 0 else abc_sb
                odst = bbc_sb if cc == 0 else tmp32_sb
                kb.dwait("sync", "d_x1s")
                kb.after("sync", m_ev[cc])
                dm = kb.dma("sync", xdst, x_in[s, chg * 128:(chg + 1) * 128, :], f"d_x{cc}")
                kb.dwait("vector", f"d_x{cc}", dm)
                kb.after("vector", m_mm)
                m_ev[cc] = kb.ins("vector", lambda e, cc=cc, xdst=xdst, odst=odst:
                                  e.tensor_tensor(odst, (pA if cc == 0 else pY), xdst,
                                                  op=ALU.add))
                kb.after("gpsimd", m_ev[cc])
                kb.dma("gpsimd", x1_s[s, chg], odst, "d_x1s")
        kb.dwait("sync", "d_x1s")
        kb.dwait("vector", "d_x1s")
        return m_ev[1]

    ffn_ost = {"d": [0, 0], "madd": [None, None], "msc_last": None}

    def ffn_tile(s, q, pm):
        t0 = q * TTOK
        kb.phase_wait("sync", pm)
        kb.phase_wait("vector", pm)
        kb.phase_wait("scalar", pm)
        kb.phase_wait("tensor", pm)
        for par in range(2):
            kb.dwait("scalar", f"d_o{par}")
            kb.dwait("vector", f"d_o{par}")
        mabc, _vm = ln_pass(lambda ch: x1_s[s, ch][:, t0:t0 + TTOK], TTOK, resident=x1t)
        kb.after("vector", mabc)
        mfw = None
        for ch in range(NCH):
            m1 = kb.ins("vector", lambda e, ch=ch: e.tensor_tensor(
                tmp32_sb[:, 0:TTOK], x1t[ch], abc_sb[:, 0:TTOK], op=ALU.mult))
            kb.fence("vector", m1)
            m2 = kb.ins("vector", lambda e, ch=ch: e.tensor_tensor(
                tmp32_sb[:, 512:512 + TTOK], tmp32_sb[:, 0:TTOK], bbc_sb[:, 0:TTOK],
                op=ALU.subtract))
            kb.fence("vector", m2)
            mfw = kb.ins("vector", lambda e, ch=ch: e.tensor_scalar_mul(
                fwpp[ch], tmp32_sb[:, 512:512 + TTOK], suug_sb[:, ch:ch + 1]))
        src = fwpp
        srcR, dstR = nbf[:, 24576:32768], nbf[:, 16384:24576]
        stage_mark = mfw
        for bit in (1, 2, 4, 8):
            stage_mark = bf_stage(srcR, dstR, bit, 512)
            srcR, dstR = dstR, srcR
        kb.after("tensor", stage_mark)
        hm = [None] * 4
        smk = [None] * 4
        for ch in range(NCH):
            k = ch % 4
            kb.after("tensor", smk[k])
            hm[k] = kb.ins("tensor", lambda e, ch=ch, src=src: e.matmul(
                pA[:, (ch % 4) * 512:(ch % 4 + 1) * 512], h128_sb, src[ch],
                start=True, stop=True))
            kb.after("scalar", hm[k])
            smk[k] = kb.ins("scalar", lambda e, ch=ch: e.activation(
                xf_ch[ch], pA[:, (ch % 4) * 512:(ch % 4 + 1) * 512], AF.Copy,
                bias=0.0, scale=FW1_SCALE))
        if q == 0:
            tapbf(f"xf{s}", xf_ch[0], TTOK, prod=smk[0])
        kb.after("tensor", *smk)
        m_gate = [None, None]
        m_sg = [None, None]
        m_ug = [None, None]
        for a in range(NACH):
            par = a % 2
            kb.after("sync", m_ug[par])
            kb.dma("sync", wbf[:, par * 4096:par * 4096 + 2048], up_s[a], f"d_ug{par}")
            dug = kb.dma("sync", wbf[:, par * 4096 + 2048:par * 4096 + 4096], gt_s[a],
                         f"d_ug{par}")
            kb.dwait("tensor", f"d_ug{par}", dug)
            kb.after("tensor", m_gate[par])
            pu = pA[:, par * 1024:par * 1024 + 512]
            pg = pA[:, par * 1024 + 512:par * 1024 + 1024]
            for ch in range(NCH):
                kb.ins("tensor", lambda e, a=a, ch=ch, par=par, pu=pu: e.matmul(
                    pu, ugw[par][ch], xf_ch[ch], start=(ch == 0), stop=(ch == NCH - 1)))
            for ch in range(NCH):
                m_ug[par] = kb.ins("tensor", lambda e, a=a, ch=ch, par=par, pg=pg: e.matmul(
                    pg, ugw[par][16 + ch], xf_ch[ch], start=(ch == 0), stop=(ch == NCH - 1)))
            kb.after("scalar", m_ug[par], m_gate[par])
            m_sg[par] = kb.ins("scalar", lambda e, a=a, pg=pg, par=par: e.activation(
                sgbuf_sb[par], pg, AF.Silu, bias=0.0, scale=wsg_sb[:, a:a + 1]))
            kb.after("vector", m_sg[par])
            m_gate[par] = kb.ins("vector", lambda e, a=a, pu=pu, par=par:
                                 e.scalar_tensor_tensor(
                                     g_store[a], pu, wsu_sb[:, a:a + 1], sgbuf_sb[par],
                                     op0=ALU.mult, op1=ALU.mult))
        kb.after("tensor", m_gate[0], m_gate[1])
        dn_mm = None
        for cq in range(4):
            dn_prev = [None, None] if dn_mm is None else [dn_mm, dn_mm]
            for grp in range(16):
                par = grp % 2
                kb.after("sync", dn_prev[par])
                ddn = kb.dma("sync", tmp32_sb.bitcast(BF16)[:, par * 2048:(par + 1) * 2048],
                             dn_s[cq][:, grp * 2048:(grp + 1) * 2048], f"d_dn{par}")
                kb.dwait("tensor", f"d_dn{par}", ddn)
                if cq > 0 and grp == 0:
                    kb.after("tensor", ffn_ost["msc_last"])
                for aa in range(4):
                    a = grp * 4 + aa
                    for cc in range(4):
                        dn_mm = kb.ins("tensor", lambda e, a=a, aa=aa, cc=cc, par=par:
                                       e.matmul(
                                           pY[:, cc * 512:(cc + 1) * 512],
                                           dnw[par][aa][:, cc * 128:(cc + 1) * 128],
                                           g_store[a], start=(a == 0), stop=(a == NACH - 1)))
                dn_prev[par] = dn_mm
            kb.after("scalar", dn_mm)
            for cc in range(4):
                chg = cq * 4 + cc
                par = chg % 2
                ab = abc_sb[:, par * 512:par * 512 + TTOK]
                bb = bbc_sb[:, par * 512:par * 512 + TTOK]
                kb.after("scalar", ffn_ost["madd"][par])
                msc = kb.ins("scalar", lambda e, cc=cc, chg=chg, ab=ab: e.activation(
                    ab, pY[:, cc * 512:(cc + 1) * 512], AF.Copy,
                    bias=0.0, scale=wsd_sb[:, chg:chg + 1]))
                ffn_ost["msc_last"] = msc
                kb.dwait("vector", f"d_o{par}", ffn_ost["d"][par])
                kb.after("vector", msc)
                m_add = kb.ins("vector", lambda e, chg=chg, ab=ab, bb=bb: e.tensor_tensor(
                    bb, ab, x1t[chg], op=ALU.add))
                ffn_ost["madd"][par] = m_add
                kb.after("gpsimd", m_add)
                ffn_ost["d"][par] = kb.dma("gpsimd", out_t[s, chg][:, t0:t0 + TTOK], bb,
                                           f"d_o{par}")
        return kb.phase_mark()

    if "startup" in stages:
        startup_smalls()
        convert_square(wt_in, wt_s)
        convert_square(w_in, w_s)
        convert_upgt(wupt_in, up_s)
        convert_upgt(wgt_in, gt_s)
        transform_dn()
        for e in ("sync", "gpsimd", "tensor", "scalar", "vector"):
            kb.dwait(e, "cv_st")
            kb.dwait(e, "d_cs0")
            kb.dwait(e, "d_cs1")
    pm = kb.phase_mark()
    for s in range(SEQN):
        if f"seq{s}" in stages:
            ln_wproj_n(s, pm)
            P0 = {"att_last": kb.mark("tensor"), "ms_abc": kb.mark("scalar"),
                  "mcc": kb.mark("vector"), "qm_last": kb.mark("tensor"),
                  "zadd15": kb.mark("vector")}
            P = attn_prechain(s, 0, P0)
            Q = {"pv2_last": kb.mark("tensor"), "mlnz_prev": kb.mark("scalar"),
                 "y1_prev": kb.mark("vector")}
            for h in range(H):
                Q = attn_head(s, h, P, Q)
                if h + 1 < H:
                    P = attn_prechain(s, h + 1, Q)
            proj_residual(s, Q["y2"])
            pm = kb.phase_mark()
    if "ffn" in stages:
        for s in range(SEQN):
            for q in range(T // TTOK):
                pm = ffn_tile(s, q, pm)
    kb.dwait("gpsimd", "d_o0")
    kb.dwait("gpsimd", "d_o1")
    kb.dwait("gpsimd", "d_tap")
    kb.dwait("sync", "d_o0")
    kb.dwait("sync", "d_o1")

    with nc.Block() as block:
        @block.sync
        def _(eng):
            for fn in kb.ops["sync"]:
                fn(eng)

        @block.vector
        def _(eng):
            for fn in kb.ops["vector"]:
                fn(eng)

        @block.scalar
        def _(eng):
            for fn in kb.ops["scalar"]:
                fn(eng)

        @block.tensor
        def _(eng):
            for fn in kb.ops["tensor"]:
                fn(eng)

        @block.gpsimd
        def _(eng):
            for fn in kb.ops["gpsimd"]:
                fn(eng)

    return kb


def host_inputs(inputs):
    x = np.asarray(inputs["L0_attn_x"], dtype=np.float32)
    W = np.asarray(inputs["L0_attn_W"], dtype=np.float32)
    Wt = np.ascontiguousarray(W.T)
    WupT = np.ascontiguousarray(np.asarray(inputs["L0_ffn_W_up"], dtype=np.float32).T)
    WgT = np.ascontiguousarray(np.asarray(inputs["L0_ffn_W_gate"], dtype=np.float32).T)
    WdT = np.asarray(inputs["L0_ffn_W_down"], dtype=np.float32).T
    WdTb = np.ascontiguousarray(
        WdT.reshape(16, 4, 128, 4, 512).transpose(0, 3, 2, 1, 4)
    ).reshape(16, 4, 128, 2048)
    g1 = np.asarray(inputs["L0_attn_g1"], dtype=np.float32).reshape(1, 128)
    g2 = np.asarray(inputs["L0_attn_g2"], dtype=np.float32).reshape(1, 128)
    g3 = np.asarray(inputs["L0_attn_g3"], dtype=np.float32).reshape(1, 128)
    suug = np.asarray(inputs["L0_ffn_SU_ug"], dtype=np.float32).reshape(NCH, 128)
    sudp = np.asarray(inputs["L0_ffn_SU_dp"], dtype=np.float32).reshape(NACH, 128)
    wsug = np.asarray(inputs["L0_ffn_Wscale_ug"], dtype=np.float32)
    wsu = np.ascontiguousarray(wsug[:INTER].reshape(NACH, 128))
    wsg = np.ascontiguousarray(wsug[INTER:].reshape(NACH, 128))
    wsd = np.asarray(inputs["L0_ffn_Wscale_dp"], dtype=np.float32).reshape(NCH, 128)

    h128 = _hadamard(128).astype(ml_dtypes.bfloat16)
    idb = np.eye(128, dtype=ml_dtypes.bfloat16)
    idf = np.eye(128, dtype=np.float32)
    onecol = np.ones((128, 1), dtype=ml_dtypes.bfloat16)
    onerow = np.ones((1, 128), dtype=np.float32)
    onerowb = np.ones((1, 128), dtype=ml_dtypes.bfloat16)
    cst = np.zeros((128, 4), dtype=np.float32)
    cst[:, 0] = EPS_LN
    cst[:, 1] = EPS_RMS
    cst[:, 2] = 1e-37

    x_fm = np.ascontiguousarray(x.transpose(0, 2, 1))

    shared = dict(g1=g1, g2=g2, g3=g3,
                  SUug=suug, SUdp=sudp, WSu=wsu, WSg=wsg, WSd=wsd,
                  H128=h128, IDB=idb, IDF=idf, ONECOL=onecol, ONEROW=onerow,
                  ONEROWB=onerowb, CST=cst, Wt=Wt, WdT=WdTb, W=W,
                  WupT=WupT, WgT=WgT)
    in_maps = []
    for k in range(NCORES):
        m = {"x": x_fm[k * SEQN:(k + 1) * SEQN]}
        m.update(shared)
        in_maps.append(m)
    return in_maps


def kernel(**inputs):
    in_maps = host_inputs(inputs)
    kbp = build_program()
    res = run_bass_kernel_spmd(kbp.nc, in_maps, core_ids=list(range(NCORES)))
    parts = [r["out"] for r in res.results]
    out_fm = np.concatenate(parts, axis=0).reshape(B, C, T)
    return np.ascontiguousarray(out_fm.transpose(0, 2, 1))
